# version 03
# baseline (speedup 1.0000x reference)
import numpy as np
import ml_dtypes
import concourse.bass as bass
import concourse.mybir as mybir
from concourse.bass_core_rust import BassInstruction
from concourse.bass_utils import run_bass_kernel_spmd

F32 = mybir.dt.float32
F32R = mybir.dt.float32r
BF16 = mybir.dt.bfloat16
AF = mybir.ActivationFunctionType
ALU = mybir.AluOpType
AX = mybir.AxisListType

T = 2048
D = 1024
NT = T // 128
ND = D // 128
H = 16
NE = 8
LN_EPS = 1e-5
SEPS = 1e-4

_TAPS = []
_NUM_CORES = 8


def _img_tok(x2d):
    return np.ascontiguousarray(
        x2d.reshape(NT, 128, D).transpose(1, 0, 2).reshape(128, NT * D))

def _unimg_tok(img):
    return np.ascontiguousarray(
        img.reshape(128, NT, D).transpose(1, 0, 2).reshape(T, D))

def _lhsT_slices(w, n_m, n_k):
    rows, d = w.shape
    assert rows == n_m * 128 and d == n_k * 128
    a = w.reshape(n_m, 128, n_k, 128)
    return np.ascontiguousarray(a.transpose(0, 3, 2, 1).reshape(n_m, 128, n_k * 128))

def _rhs_img(wT):
    d, cols = wT.shape
    nk = d // 128
    a = wT.reshape(nk, 128, cols)
    return np.ascontiguousarray(a.transpose(1, 0, 2).reshape(128, nk * cols))


class Ctr:
    def __init__(self, handle):
        self.h = handle
        self.n = 0

class B:
    def __init__(self, nc):
        self.nc = nc
        self.streams = {e: [] for e in ("tensor", "vector", "scalar", "gpsimd", "sync")}

    def emit(self, eng, fn, waits=(), incs=()):
        for (c, k) in incs:
            c.n += k
        snap = tuple((c.h, v) for (c, v) in waits if v > 0)
        tgt = tuple((c.h, k) for (c, k) in incs)
        def f(e):
            for (h, v) in snap:
                e.wait_ge(h, v)
            inst = fn(e)
            if tgt:
                assert inst is not None
                for (h, k) in tgt:
                    inst = inst.then_inc(h, k)
        self.streams[eng].append(f)

    def finalize(self, block):
        st = self.streams
        @block.sync
        def _(e):
            for f in st["sync"]: f(e)
        @block.gpsimd
        def _(e):
            for f in st["gpsimd"]: f(e)
        @block.vector
        def _(e):
            for f in st["vector"]: f(e)
        @block.scalar
        def _(e):
            for f in st["scalar"]: f(e)
        @block.tensor
        def _(e):
            for f in st["tensor"]: f(e)


def act_swap(nc, s, out, in_, func, bias=0.0, scale=1.0, accum_out=None):
    inst = s.activation(out, in_, AF.Sqrt, bias=bias, scale=scale, accum_out=accum_out)
    i = inst.ins
    clone = mybir.InstActivation(name=i.name, engine=i.engine, ins=list(i.ins),
                                 outs=list(i.outs), func=func)
    clone.sync_info = i.sync_info
    bb = nc.cur_bb.bb
    assert bb.instructions[-1].name == i.name
    bb.instructions[-1] = clone
    return BassInstruction(clone)


def build_program():
    nc = bass.Bass("TRN2", target_bir_lowering=False, detect_race_conditions=False)

    hx = nc.dram_tensor("x_img", [128, NT * D], F32, kind="ExternalInput").ap()
    hwin = nc.dram_tensor("w_in_img", [32, 128, 1024], F32, kind="ExternalInput").ap()
    hbin = nc.dram_tensor("b_in_img", [128, 32], F32, kind="ExternalInput").ap()
    hwout = nc.dram_tensor("w_out_img", [128, 8 * 1024], F32, kind="ExternalInput").ap()
    hbout = nc.dram_tensor("b_out_row", [1, D], F32, kind="ExternalInput").ap()
    hgate = nc.dram_tensor("gate_img", [128, 64], F32, kind="ExternalInput").ap()
    hwup = nc.dram_tensor("wup_img", [NE, 24, 128, 1024], F32, kind="ExternalInput").ap()
    hwdn = nc.dram_tensor("wdn_img", [NE, 8, 128, 1536], F32, kind="ExternalInput").ap()
    hsgg = nc.dram_tensor("sgg_img", [12, 128, 1024], F32, kind="ExternalInput").ap()
    hsgu = nc.dram_tensor("sgu_img", [12, 128, 1024], F32, kind="ExternalInput").ap()
    hsgd = nc.dram_tensor("sgd_img", [8, 128, 1536], F32, kind="ExternalInput").ap()
    hout = nc.dram_tensor("out_img", [128, NT * D], F32, kind="ExternalOutput").ap()
    htap = {}
    for (name, shape) in _TAPS:
        htap[name] = nc.dram_tensor("tap_" + name, list(shape), F32,
                                    kind="ExternalOutput").ap()

    KB = 1024
    ARENA = 200 * KB
    base0, _end = nc.bump_sbuf(ARENA, align_bytes=256)
    A0, B0, C0, D0, S0 = (base0, base0 + 64 * KB, base0 + 128 * KB,
                          base0 + 168 * KB, base0 + 184 * KB)

    def at(name, off, shape, dtype):
        return nc.alloc_sbuf_tensor_at(name, shape, dtype, offset=off).ap()

    X = at("X", A0, [128, NT * D], F32)
    qaug = [at(f"qaug{i}", A0 + i * 4 * KB, [66, T], BF16) for i in range(2)]
    kaug = [at(f"kaug{i}", A0 + 8 * KB + i * 4 * KB, [66, T], BF16) for i in range(2)]
    vT = at("vT", A0 + 16 * KB, [64, T], F32)
    gTb = [at(f"gTb{i}", A0 + 24 * KB + i * 4 * KB, [64, T], BF16) for i in range(2)]
    vprime = [at(f"vp{i}", A0 + 32 * KB + i * 5 * KB, [128, 16 * 65], BF16) for i in range(2)]
    qsq = at("qsq", A0 + 42 * KB, [64, 512], F32)
    rec = [at(f"rec{i}", A0 + 44 * KB + i * 2 * KB, [64, 512], F32) for i in range(2)]
    normt = [at(f"normt{i}", A0 + 48 * KB + i * 2 * KB, [64, 512], F32) for i in range(2)]
    rowt = at("rowt", A0 + 52 * KB, [1, 512], F32)
    rowt2 = at("rowt2", A0 + 54 * KB, [1, 512], F32)
    r_tok = at("r_tok", A0, [128, NT * D], F32)
    x1T = at("x1T", A0, [128, ND * T], F32R)
    z_tok = at("z_tok", A0, [128, NT * D], F32)

    xn_tok = at("xn_tok", B0, [128, NT * D], BF16)
    xnT = at("xnT", B0 + 32 * KB, [128, ND * T], BF16)
    u_s = [at(f"u{i}", B0 + i * 2 * KB, [128, 1024], BF16) for i in range(4)]
    uu_s = [at(f"uu{i}", B0 + 8 * KB + i * 2 * KB, [128, 1024], BF16) for i in range(4)]
    wring = [at(f"wr{i}", B0 + 16 * KB + i * 2 * KB, [128, 1024], BF16) for i in range(4)]
    X_re = at("X_re", B0, [128, NT * D], F32)
    x1_tok = at("x1_tok", B0, [128, NT * D], F32)
    acc = at("acc", B0, [128, ND * T], F32)

    OT = at("OT", C0, [128, 9 * T], BF16)
    hbuf = at("hbuf", C0, [128, 6 * T], BF16)
    cbcast = at("cbcast", C0 + 24 * KB, [128, T], F32)
    crow = at("crow", C0 + 32 * KB, [1, T], F32)

    winw = [at(f"winw{i}", D0 + i * 2 * KB, [128, 1024], BF16) for i in range(8)]
    woutb = at("woutb", D0, [128, 8 * 1024], BF16)
    upw = [at(f"upw{i}", D0 + i * 4 * KB, [128, 1024], F32R) for i in range(2)]
    dnw0 = at("dnw0", D0 + 8 * KB, [128, 1536], BF16)
    dn2 = at("dn2", D0 + 11 * KB, [128, 1536], BF16)

    o = S0
    ident32 = at("ident32", o, [128, 128], F32); o += 512
    identbf = at("identbf", o, [128, 128], BF16); o += 256
    ones_col = at("ones_col", o, [128, 1], F32); o += 32
    q2lhs = at("q2lhs", o, [64, 2], F32); o += 32
    k2lhs = at("k2lhs", o, [64, 2], F32); o += 32
    bias_q = at("bias_q", o, [2, 1], F32); o += 32
    bias_k = at("bias_k", o, [2, 1], F32); o += 32
    eps_col = at("eps_col", o, [128, 1], F32); o += 32
    seps1 = at("seps1", o, [1, 1], F32); o += 32
    binsb = at("binsb", o, [128, 32], F32); o += 128
    binsb_n2 = at("binsb_n2", o, [128, 32], F32); o += 128
    ones_rowf = at("ones_rowf", o, [1, 128], F32); o += 512
    boutr = at("boutr", o, [1, D], BF16); o += 2 * KB
    gatesb = at("gatesb", o, [128, 64], F32R); o += 256
    iotar = at("iotar", o, [128, 8], F32); o += 32
    sums1 = at("sums1", o, [128, NT], F32); o += 64
    sums1b = at("sums1b", o, [128, NT], F32); o += 64
    sums2 = at("sums2", o, [128, NT], F32); o += 64
    mu = at("mu", o, [128, NT], F32); o += 64
    var_ = at("var", o, [128, NT], F32); o += 64
    rstd = at("rstd", o, [128, NT], F32); o += 64
    nmr = at("nmr", o, [128, NT], F32); o += 64
    scr_sq = at("scr_sq", o, [128, D], F32); o += 4 * KB
    logits = at("logits", o, [128, NT * 8], F32); o += 512
    comb = at("comb", o, [128, NT * 8], F32); o += 512
    gm1 = at("gm1", o, [128, NT], F32); o += 64
    gm2 = at("gm2", o, [128, NT], F32); o += 64
    gmi = at("gmi", o, [128, NT], F32); o += 64
    gmi2 = at("gmi2", o, [128, NT], F32); o += 64
    gw1 = at("gw1", o, [128, NT], F32); o += 64
    gw2 = at("gw2", o, [128, NT], F32); o += 64
    gdm = at("gdm", o, [128, NT], F32); o += 64
    geqA = at("geqA", o, [128, 128], F32); o += 512
    geq1 = at("geq1", o, [128, 128], F32); o += 512
    geq2 = at("geq2", o, [128, 128], F32); o += 512
    grow2 = at("grow2", o, [128, 128], F32); o += 512
    gtmp = at("gtmp", o, [128, 128], F32); o += 512
    utmp = at("utmp", o, [128, 512], F32); o += 2 * KB
    lnz = at("lnz", o, [128, NT * ND], F32); o += 512
    assert o - S0 <= 16 * KB, f"small region overflow: {o - S0}"

    psS = [nc.alloc_psum_tensor(f"psS{i}", [128, 1024], F32).ap() for i in range(2)]
    psA = [nc.alloc_psum_tensor(f"psA{i}", [128, 512], F32).ap() for i in range(2)]
    psP = [nc.alloc_psum_tensor(f"psP{i}", [128, 512], F32).ap() for i in range(2)]
    psPbf = [p.bitcast(BF16) for p in psP]
    psU = [psS[0][:, 0:512], psS[0][:, 512:1024], psS[1][:, 0:512], psS[1][:, 512:1024]]
    psDn = [psA[0], psA[1], psP[0], psP[1]]

    import contextlib
    est = contextlib.ExitStack()
    sems = {}
    for name in ("sdin", "sdw", "sp", "sa", "sv", "sg", "so"):
        sems[name] = Ctr(est.enter_context(nc.semaphore(name)))
    sdin, sdw, sp, sa, sv, sg, so = (sems[k] for k in
                                     ("sdin", "sdw", "sp", "sa", "sv", "sg", "so"))
    blk = est.enter_context(nc.Block())
    b = B(nc)
    tap_names = {name: shape for (name, shape) in _TAPS}

    def tap(name, src, waits):
        if name not in tap_names:
            return
        shape = tap_names[name]
        b.emit("gpsimd", lambda e, name=name, src=src, shape=shape: e.dma_start(
            htap[name], src[0:shape[0], 0:shape[1]]),
            waits=waits, incs=[(so, 16)])

    b.emit("sync", lambda e: e.dma_start(X[:], hx), incs=[(sdin, 16)])
    c_X = sdin.n
    b.emit("sync", lambda e: e.dma_start(binsb[:], hbin), incs=[(sdin, 16)])
    c_bin = sdin.n
    tap("X", X, [(sdin, c_X)])
    b.emit("gpsimd", lambda e: e.dma_start(gatesb[:], hgate), incs=[(sdw, 16)])
    c_gate = sdw.n
    b.emit("gpsimd", lambda e: e.dma_start(boutr[:], hbout), incs=[(sdw, 16)])
    c_bout = sdw.n

    def _gsetup(e):
        e.memset(ident32[:], 0.0)
        nc.gpsimd.affine_select(out=ident32[:], in_=ident32[:], compare_op=ALU.not_equal,
                                fill=1.0, base=0, pattern=[[-1, 128]], channel_multiplier=1)
        e.memset(identbf[:], 0.0)
        nc.gpsimd.affine_select(out=identbf[:], in_=identbf[:], compare_op=ALU.not_equal,
                                fill=1.0, base=0, pattern=[[-1, 128]], channel_multiplier=1)
        e.memset(ones_col[:], 1.0)
        e.memset(ones_rowf[:], 1.0)
        e.memset(q2lhs[:], 0.0)
        e.memset(q2lhs[:, 0:1], 0.25)
        e.memset(k2lhs[:], 0.0)
        e.memset(k2lhs[:, 1:2], 1.0)
        e.iota(bias_q[:], pattern=[[0, 1]], base=0, channel_multiplier=1,
               allow_small_or_imprecise_dtypes=True)
        e.iota(bias_k[:], pattern=[[0, 1]], base=1, channel_multiplier=-1,
               allow_small_or_imprecise_dtypes=True)
        e.memset(eps_col[:], LN_EPS)
        e.memset(seps1[:], SEPS)
        e.memset(OT[0:1, 8 * T:9 * T], 1.0)
        return nc.gpsimd.iota(iotar[:], pattern=[[-1, 8]], base=8, channel_multiplier=0,
                              allow_small_or_imprecise_dtypes=True)
    b.emit("gpsimd", _gsetup, incs=[(sg, 1)])
    c_consts = sg.n

    win_loaded = [0] * 32
    for m in range(8):
        b.emit("gpsimd", lambda e, m=m: e.dma_start(winw[m % 8][:], hwin[m]),
               incs=[(sdw, 16)])
        win_loaded[m] = sdw.n

    b.emit("vector", lambda e: e.tensor_scalar_mul(binsb_n2[:], binsb[:], -2.0),
           waits=[(sdin, c_bin)], incs=[(sv, 1)])

    def vspace():
        b.emit("vector", lambda e: e.tensor_scalar_mul(scr_sq[:, 0:512],
                                                       scr_sq[:, 0:512], 1.0))

    def emit_ln_finish(waits):
        b.emit("vector", lambda e: e.tensor_scalar_mul(mu[:], sums1[:], 1.0 / D),
               waits=waits)
        b.emit("vector", lambda e: e.tensor_scalar_mul(var_[:], sums2[:], 1.0 / D))
        vspace()
        b.emit("vector", lambda e: e.tensor_mul(nmr[:], mu[:], mu[:]))
        vspace()
        b.emit("vector", lambda e: e.tensor_sub(var_[:], var_[:], nmr[:]), incs=[(sv, 1)])
        cv = sv.n
        b.emit("scalar", lambda e: act_swap(nc, e, rstd[:], var_[:], AF.Rsqrt,
                                            bias=eps_col[:, 0:1]),
               waits=[(sv, cv), (sg, c_consts)], incs=[(sa, 1)])
        ca = sa.n
        b.emit("vector", lambda e: e.scalar_tensor_tensor(
            nmr[:], mu[:], -1.0, rstd[:], op0=ALU.mult, op1=ALU.mult),
            waits=[(sa, ca)], incs=[(sv, 1)])
        return sv.n

    for t in range(NT):
        b.emit("vector", lambda e, t=t: e.reduce_sum(
            sums1[:, t:t + 1], X[:, t * D:(t + 1) * D], axis=AX.X),
            waits=[(sdin, c_X)] if t == 0 else [])
    b.emit("vector", lambda e: e.engine_nop(), incs=[(sv, 1)])
    c_ln1_sums = sv.n
    for t in range(NT):
        b.emit("scalar", lambda e, t=t: e.activation(
            scr_sq[:], X[:, t * D:(t + 1) * D], AF.Square,
            accum_out=sums2[:, t:t + 1]),
            waits=[(sdin, c_X)] if t == 0 else [],
            incs=[(sa, 1)] if t == NT - 1 else [])
    c_ln1_sq = sa.n
    c_ln1_fin = emit_ln_finish([(sv, c_ln1_sums), (sa, c_ln1_sq)])
    tap("mu1", mu, [(sv, c_ln1_fin)])
    tap("rstd1", rstd, [(sv, c_ln1_fin)])
    tap("s1", sums1, [(sv, c_ln1_sums)])
    tap("s2", sums2, [(sa, c_ln1_sq)])
    for t in range(NT):
        b.emit("scalar", lambda e, t=t: e.activation(
            xn_tok[:, t * D:(t + 1) * D], X[:, t * D:(t + 1) * D], AF.Identity,
            scale=rstd[:, t:t + 1], bias=nmr[:, t:t + 1]),
            waits=[(sv, c_ln1_fin)] if t == 0 else [],
            incs=[(sa, 1)] if t == NT - 1 else [])
    c_ln1_apply = sa.n
    tap("xn_tok", xn_tok, [(sa, c_ln1_apply)])

    def _vp_ones(e):
        last = None
        for i in range(2):
            for kt in range(NT):
                last = e.activation(vprime[i][:, kt * 65 + 64: kt * 65 + 65],
                                    ones_col[:], AF.Identity)
        return last
    b.emit("scalar", _vp_ones, waits=[(sg, c_consts)], incs=[(sa, 1)])
    c_vpones = sa.n

    tr_evac = [0, 0]
    for t in range(NT):
        bank = t % 2
        wl = [(sa, c_ln1_apply), (sg, c_consts)] if t < 2 else [(sv, tr_evac[bank])]
        for c in range(ND):
            b.emit("tensor", lambda e, t=t, c=c, bank=bank: e.transpose(
                psPbf[bank][:, c * 128:(c + 1) * 128],
                xn_tok[:, t * D + c * 128: t * D + (c + 1) * 128],
                identbf[:]), waits=wl if c == 0 else [],
                incs=[(sp, 1)] if c == ND - 1 else [])
        cf = sp.n
        dst = xnT.rearrange("p (c y) -> p c y", c=ND)[:, :, t * 128:(t + 1) * 128]
        b.emit("vector", lambda e, dst=dst, bank=bank: e.tensor_copy(
            dst, psPbf[bank][:].rearrange("p (c x) -> p c x", c=ND)),
            waits=[(sp, cf)], incs=[(sv, 1)])
        tr_evac[bank] = sv.n
    c_xnT = sv.n
    tap("xnT", xnT, [(sv, c_xnT)])

    pP_state = [(sv, tr_evac[0]), (sv, tr_evac[1])]
    prep_done = [None] * H
    main_sv = [0] * H
    psA_free = [0, 0]
    psS_free = [0, 0]
    psS0_borrow = [0]
    wring_free = [0, 0, 0, 0]
    uslot_free = [0, 0, 0, 0]
    last_phead_mm = [0]
    last_q2mm = [0]
    wready = {}

    def emit_prep(h):
        par = h % 2
        ready = {}
        c_prev_main = main_sv[h - 2] if h >= 2 else 0
        for mi in range(2):
            m = 2 * h + mi
            slot = m % 8
            for qc in range(4):
                bank = mi
                wl = [(sdw, win_loaded[m]), (sv, c_xnT), pP_state[bank]]
                for kt in range(8):
                    b.emit("tensor", lambda e, qc=qc, kt=kt, slot=slot, bank=bank: e.matmul(
                        psP[bank][:],
                        winw[slot][:, kt * 128:(kt + 1) * 128],
                        xnT[:, kt * T + qc * 512: kt * T + qc * 512 + 512],
                        start=(kt == 0), stop=(kt == 7)),
                        waits=wl if kt == 0 else [],
                        incs=[(sp, 1)] if kt == 7 else [])
                cfill = sp.n
                last_phead_mm[0] = cfill
                if mi == 0:
                    b.emit("scalar", lambda e, par=par, qc=qc, m=m: e.activation(
                        qaug[par][0:64, qc * 512: qc * 512 + 512], psP[0][0:64, :],
                        AF.Identity, scale=-2.0, bias=binsb_n2[0:64, m:m + 1]),
                        waits=[(sp, cfill), (sv, c_prev_main)])
                    b.emit("scalar", lambda e, par=par, qc=qc, m=m: e.activation(
                        kaug[par][0:64, qc * 512: qc * 512 + 512], psP[0][64:128, :],
                        AF.Identity, bias=binsb[64:128, m:m + 1]), incs=[(sa, 1)])
                else:
                    b.emit("scalar", lambda e, qc=qc, m=m: e.activation(
                        vT[0:64, qc * 512: qc * 512 + 512], psP[1][0:64, :],
                        AF.Identity, bias=binsb[0:64, m:m + 1]),
                        waits=[(sp, cfill), (sv, c_prev_main)])
                    b.emit("scalar", lambda e, par=par, qc=qc, m=m: e.activation(
                        gTb[par][0:64, qc * 512: qc * 512 + 512], psP[1][64:128, :],
                        AF.Identity, bias=binsb[64:128, m:m + 1]), incs=[(sa, 1)])
                pP_state[bank] = (sa, sa.n)
            if m + 8 < 32:
                cm = sp.n
                b.emit("gpsimd", lambda e, m=m, slot=slot: e.dma_start(
                    winw[slot][:], hwin[m + 8]), waits=[(sp, cm)], incs=[(sdw, 16)])
                win_loaded[m + 8] = sdw.n
        ready["qk_evac"] = sa.n

        for which in range(2):
            src = qaug[par] if which == 0 else kaug[par]
            lhs = q2lhs if which == 0 else k2lhs
            bias = bias_q if which == 0 else bias_k
            for qc in range(4):
                b.emit("vector", lambda e, src=src, qc=qc: e.tensor_mul(
                    qsq[:], src[0:64, qc * 512: qc * 512 + 512],
                    src[0:64, qc * 512: qc * 512 + 512]),
                    waits=[(sa, ready["qk_evac"]), (sp, last_q2mm[0])],
                    incs=[(sv, 1)])
                cs = sv.n
                b.emit("tensor", lambda e, lhs=lhs: e.matmul(
                    psP[0][0:2, :], lhs[:], qsq[:], start=True, stop=True),
                    waits=[(sv, cs), pP_state[0], (sg, c_consts)], incs=[(sp, 1)])
                last_q2mm[0] = sp.n
                b.emit("scalar", lambda e, src=src, qc=qc, bias=bias: e.activation(
                    src[64:66, qc * 512: qc * 512 + 512], psP[0][0:2, :], AF.Identity,
                    bias=bias[:, 0:1]), waits=[(sp, last_q2mm[0])], incs=[(sa, 1)])
                pP_state[0] = (sa, sa.n)
        ready["aug"] = sa.n

        for grp in range(2):
            wl0 = [pP_state[1], (sa, c_vpones)]
            for k8 in range(8):
                kt = grp * 8 + k8
                b.emit("tensor", lambda e, kt=kt, k8=k8: e.transpose(
                    psP[1][:, k8 * 64:(k8 + 1) * 64],
                    vT[0:64, kt * 128:(kt + 1) * 128], ident32[0:64, 0:64]),
                    waits=wl0 if k8 == 0 else [],
                    incs=[(sp, 1)] if k8 == 7 else [])
            cf = sp.n
            dst = vprime[par].rearrange("p (k y) -> p k y", k=16)[
                :, grp * 8:(grp + 1) * 8, 0:64]
            b.emit("vector", lambda e, dst=dst: e.tensor_copy(
                dst, psP[1][:].rearrange("p (k y) -> p k y", k=8)),
                waits=[(sp, cf)], incs=[(sv, 1)])
            pP_state[1] = (sv, sv.n)
        ready["vp"] = sv.n
        ready["g"] = ready["qk_evac"]
        prep_done[h] = ready
        if h == 0:
            tap("qaug0", qaug[0], [(sa, ready["aug"])])
            tap("kaug0", kaug[0], [(sa, ready["aug"])])
            tap("vT", vT, [(sa, ready["qk_evac"])])
            tap("gTb0", gTb[0], [(sa, ready["qk_evac"])])
            tap("vp0", vprime[0], [(sv, ready["vp"])])

    LAG = 3

    def emit_S_EW(h, g):
        par = h % 2
        rd = prep_done[h]
        qc2, kt = divmod(g, NT)
        i = g % 2
        slot2 = g % 4
        for half in range(2):
            wl = []
            if half == 0:
                wl = [(sa, rd["aug"]), (sa, psS_free[i])]
                if i == 0:
                    wl.append((sa, psS0_borrow[0]))
            b.emit("tensor", lambda e, par=par, kt=kt, qc2=qc2, half=half, i=i: e.matmul(
                psS[i][:, half * 512:(half + 1) * 512],
                kaug[par][:, kt * 128:(kt + 1) * 128],
                qaug[par][:, qc2 * 1024 + half * 512: qc2 * 1024 + (half + 1) * 512],
                start=True, stop=True),
                waits=wl, incs=[(sp, 1)] if half == 1 else [])
        cS = sp.n
        b.emit("scalar", lambda e, i=i, slot2=slot2: act_swap(
            nc, e, u_s[slot2][:], psS[i][:], AF.Rsqrt),
            waits=[(sp, cS), (sv, uslot_free[slot2])], incs=[(sa, 1)])
        psS_free[i] = sa.n
        cu = sa.n
        if g % 2 == 0:
            b.emit("scalar", lambda e, slot2=slot2: e.activation(
                uu_s[slot2][:], u_s[slot2][:], AF.Square), incs=[(sa, 1)])
            cuu = (sa, sa.n)
        else:
            b.emit("vector", lambda e, slot2=slot2: e.tensor_mul(
                uu_s[slot2][:], u_s[slot2][:], u_s[slot2][:]),
                waits=[(sa, cu)], incs=[(sv, 1)])
            cuu = (sv, sv.n)
        b.emit("vector", lambda e, slot2=slot2: e.scalar_tensor_tensor(
            uu_s[slot2][:], uu_s[slot2][:], -SEPS, u_s[slot2][:],
            op0=ALU.mult, op1=ALU.add), waits=[cuu, (sa, cu)])
        b.emit("vector", lambda e, slot2=slot2, wslot=g % 4: e.tensor_mul(
            wring[wslot][:], uu_s[slot2][:], uu_s[slot2][:]),
            waits=[(sp, wring_free[g % 4])], incs=[(sv, 1)])
        uslot_free[slot2] = sv.n
        wready[g] = sv.n

    def emit_A(h, g):
        par = h % 2
        rd = prep_done[h]
        qc2, kt = divmod(g, NT)
        wslot = g % 4
        for j in range(2):
            wl = [(sv, wready[g])]
            if kt == 0:
                wl += [(sv, rd["vp"]), (sv, psA_free[j])]
            b.emit("tensor", lambda e, kt=kt, j=j, par=par, wslot=wslot: e.matmul(
                psA[j][0:65, :], vprime[par][:, kt * 65: kt * 65 + 65],
                wring[wslot][:, j * 512:(j + 1) * 512],
                start=(kt == 0), stop=(kt == NT - 1)),
                waits=wl, incs=[(sp, 1)] if j == 1 else [])
        wring_free[wslot] = sp.n

    def emit_norm(h, qc2):
        par = h % 2
        rd = prep_done[h]
        cA = sp.n
        for j in range(2):
            qc = qc2 * 2 + j
            b.emit("scalar", lambda e, j=j: e.activation(
                rowt[:], psA[j][64:65, :], AF.Identity, bias=seps1[:, 0:1]),
                waits=[(sp, cA)], incs=[(sa, 1)])
            cr = sa.n
            b.emit("vector", lambda e: e.tensor_mul(rowt2[:], rowt[:], rowt[:]),
                   waits=[(sa, cr)], incs=[(sv, 1)])
            crr = sv.n
            b.emit("tensor", lambda e: e.matmul(
                psS[0][0:64, 0:512], ones_rowf[0:1, 0:64], rowt2[:],
                start=True, stop=True),
                waits=[(sv, crr), (sa, psS_free[0])], incs=[(sp, 1)])
            cb = sp.n
            b.emit("scalar", lambda e, j=j: act_swap(
                nc, e, rec[j][:], psS[0][0:64, 0:512], AF.Rsqrt),
                waits=[(sp, cb)], incs=[(sa, 1)])
            psS0_borrow[0] = sa.n
            cre = sa.n
            b.emit("vector", lambda e, j=j: e.tensor_mul(
                normt[j][:], psA[j][0:64, :], rec[j][:]),
                waits=[(sa, cre)])
            b.emit("vector", lambda e, j=j, qc=qc, par=par, h=h: e.tensor_mul(
                OT[(h % 2) * 64:(h % 2) * 64 + 64,
                   (h // 2) * T + qc * 512:(h // 2) * T + qc * 512 + 512],
                normt[j][:], gTb[par][0:64, qc * 512: qc * 512 + 512]),
                waits=[(sa, rd["g"])], incs=[(sv, 1)])
            psA_free[j] = sv.n

    def emit_main(h):
        wready.clear()
        for g in range(2 * NT):
            emit_S_EW(h, g)
            if g >= LAG:
                emit_A(h, g - LAG)
                if (g - LAG) % NT == NT - 1:
                    emit_norm(h, (g - LAG) // NT)
        for g in range(2 * NT - LAG, 2 * NT):
            emit_A(h, g)
            if g % NT == NT - 1:
                emit_norm(h, g // NT)
        main_sv[h] = sv.n

    emit_prep(0)
    if H > 1:
        emit_prep(1)
    for h in range(H):
        emit_main(h)
        if h + 2 < H:
            emit_prep(h + 2)
    c_heads_sv = sv.n
    c_heads_sa = sa.n
    tap("OT", OT, [(sv, c_heads_sv), (sg, c_consts)])

    b.emit("gpsimd", lambda e: e.dma_start(woutb[:], hwout),
           waits=[(sp, last_phead_mm[0])], incs=[(sdw, 16)])
    c_wout = sdw.n
    b.emit("sync", lambda e: e.dma_start(X_re[:], hx),
           waits=[(sv, c_heads_sv), (sa, c_heads_sa)], incs=[(sdin, 16)])
    c_Xre = sdin.n

    x2banks = [psS[0][:, 0:512], psS[0][:, 512:1024], psS[1][:, 0:512], psS[1][:, 512:1024]]
    x2_free = [c_heads_sv] * 4
    idx = 0
    for t in range(NT):
        for c2 in range(2):
            bank = idx % 4
            wl = [(sv, x2_free[bank]), (sdw, c_wout)]
            if idx == 0:
                wl += [(sa, c_heads_sa), (sg, c_consts)]
            for kt in range(8):
                b.emit("tensor", lambda e, t=t, c2=c2, kt=kt, bank=bank: e.matmul(
                    x2banks[bank], OT[:, kt * T + t * 128: kt * T + (t + 1) * 128],
                    woutb[:, kt * 1024 + c2 * 512: kt * 1024 + c2 * 512 + 512],
                    start=(kt == 0), stop=False),
                    waits=wl if kt == 0 else [])
            b.emit("tensor", lambda e, t=t, c2=c2, bank=bank: e.matmul(
                x2banks[bank], OT[0:1, 8 * T + t * 128: 8 * T + (t + 1) * 128],
                boutr[0:1, c2 * 512: c2 * 512 + 512],
                start=False, stop=True),
                waits=[(sdw, c_bout)] if idx == 0 else [], incs=[(sp, 1)])
            cf = sp.n
            sumcol = sums1 if c2 == 0 else sums1b
            b.emit("vector", lambda e, t=t, c2=c2, bank=bank, sumcol=sumcol: e.scalar_tensor_tensor(
                r_tok[:, t * D + c2 * 512: t * D + c2 * 512 + 512],
                X_re[:, t * D + c2 * 512: t * D + c2 * 512 + 512], 2.0,
                x2banks[bank], op0=ALU.mult, op1=ALU.add,
                accum_out=sumcol[:, t:t + 1]),
                waits=[(sp, cf)] + ([(sdin, c_Xre)] if idx == 0 else []),
                incs=[(sv, 1)])
            x2_free[bank] = sv.n
            idx += 1
    c_x2_mm = sp.n
    vspace()
    b.emit("vector", lambda e: e.tensor_add(sums1[:], sums1[:], sums1b[:]), incs=[(sv, 1)])
    c_r_done = sv.n
    tap("r_tok", r_tok, [(sv, c_r_done)])
    for t in range(NT):
        b.emit("scalar", lambda e, t=t: e.activation(
            scr_sq[:], r_tok[:, t * D:(t + 1) * D], AF.Square,
            accum_out=sums2[:, t:t + 1]),
            waits=[(sv, c_r_done)] if t == 0 else [],
            incs=[(sa, 1)] if t == NT - 1 else [])
    c_r_sq = sa.n
    c_ln2_fin = emit_ln_finish([(sv, c_r_done), (sa, c_r_sq)])
    for t in range(NT):
        b.emit("scalar", lambda e, t=t: e.activation(
            x1_tok[:, t * D:(t + 1) * D], r_tok[:, t * D:(t + 1) * D], AF.Identity,
            scale=rstd[:, t:t + 1], bias=nmr[:, t:t + 1]),
            waits=[(sv, c_ln2_fin), (sdin, c_Xre)] if t == 0 else [],
            incs=[(sa, 1)] if t == NT - 1 else [])
    c_x1_apply = sa.n
    tap("x1_tok", x1_tok, [(sa, c_x1_apply)])

    tr_evac2 = [0, 0]
    idx = 0
    x1T_4d = x1T.rearrange("p (c y) -> p c y", c=ND)
    for t in range(NT):
        for cg in range(2):
            bank = idx % 2
            wl = [(sa, c_x1_apply)] if idx < 2 else [(sv, tr_evac2[bank])]
            for c4 in range(4):
                c = cg * 4 + c4
                b.emit("tensor", lambda e, t=t, c=c, c4=c4, bank=bank: e.transpose(
                    psP[bank][:, c4 * 128:(c4 + 1) * 128],
                    x1_tok[:, t * D + c * 128: t * D + (c + 1) * 128],
                    ident32[:]), waits=wl if c4 == 0 else [],
                    incs=[(sp, 1)] if c4 == 3 else [])
            cf = sp.n
            dst = x1T_4d[:, cg * 4:(cg + 1) * 4, t * 128:(t + 1) * 128]
            b.emit("vector", lambda e, dst=dst, bank=bank: e.tensor_copy(
                dst, psP[bank][:].rearrange("p (c x) -> p c x", c=4)),
                waits=[(sp, cf)], incs=[(sv, 1)])
            tr_evac2[bank] = sv.n
            idx += 1
    c_x1T = sv.n
    tap("x1T", x1T, [(sv, c_x1T)])

    glog_free = [c_x1T, c_x1T]
    for t in range(NT):
        bank = psA[t % 2]
        wl = [(sv, glog_free[t % 2]), (sdw, c_gate)]
        for kt in range(8):
            b.emit("tensor", lambda e, t=t, kt=kt, bank=bank: e.matmul(
                bank[:, 0:8], x1T[:, kt * T + t * 128: kt * T + (t + 1) * 128],
                gatesb[:, kt * 8:(kt + 1) * 8],
                start=(kt == 0), stop=(kt == 7)),
                waits=wl if kt == 0 else [], incs=[(sp, 1)] if kt == 7 else [])
        cf = sp.n
        b.emit("vector", lambda e, t=t, bank=bank: e.tensor_copy(
            logits[:, t * 8:(t + 1) * 8], bank[:, 0:8]), waits=[(sp, cf)],
            incs=[(sv, 1)])
        glog_free[t % 2] = sv.n
    c_logits = sv.n
    tap("logits", logits, [(sv, c_logits)])

    vspace()
    for t in range(NT):
        b.emit("vector", lambda e, t=t: e.reduce_max(
            gm1[:, t:t + 1], logits[:, t * 8:(t + 1) * 8], axis=AX.X),
            waits=[(sv, c_logits), (sg, c_consts)] if t == 0 else [])
    for t in range(NT):
        b.emit("vector", lambda e, t=t: e.tensor_scalar(
            geqA[:, t * 8:(t + 1) * 8], logits[:, t * 8:(t + 1) * 8],
            gm1[:, t:t + 1], None, op0=ALU.is_equal))
    for t in range(NT):
        b.emit("vector", lambda e, t=t: e.scalar_tensor_tensor(
            grow2[:, t * 8:(t + 1) * 8], geqA[:, t * 8:(t + 1) * 8], -1e30,
            logits[:, t * 8:(t + 1) * 8], op0=ALU.mult, op1=ALU.add))
    for t in range(NT):
        b.emit("vector", lambda e, t=t: e.reduce_max(
            gm2[:, t:t + 1], grow2[:, t * 8:(t + 1) * 8], axis=AX.X))
    for t in range(NT):
        b.emit("vector", lambda e, t=t: e.tensor_mul(
            gtmp[:, t * 8:(t + 1) * 8], geqA[:, t * 8:(t + 1) * 8], iotar[:]))
    for t in range(NT):
        b.emit("vector", lambda e, t=t: e.reduce_max(
            gmi[:, t:t + 1], gtmp[:, t * 8:(t + 1) * 8], axis=AX.X))
    for t in range(NT):
        b.emit("vector", lambda e, t=t: e.tensor_scalar(
            geq1[:, t * 8:(t + 1) * 8], iotar[:], gmi[:, t:t + 1], None,
            op0=ALU.is_equal))
    for t in range(NT):
        b.emit("vector", lambda e, t=t: e.tensor_scalar(
            geq2[:, t * 8:(t + 1) * 8], grow2[:, t * 8:(t + 1) * 8],
            gm2[:, t:t + 1], None, op0=ALU.is_equal))
    for t in range(NT):
        b.emit("vector", lambda e, t=t: e.tensor_mul(
            gtmp[:, t * 8:(t + 1) * 8], geq2[:, t * 8:(t + 1) * 8], iotar[:]))
    for t in range(NT):
        b.emit("vector", lambda e, t=t: e.reduce_max(
            gmi2[:, t:t + 1], gtmp[:, t * 8:(t + 1) * 8], axis=AX.X))
    for t in range(NT):
        b.emit("vector", lambda e, t=t: e.tensor_scalar(
            geq2[:, t * 8:(t + 1) * 8], iotar[:], gmi2[:, t:t + 1], None,
            op0=ALU.is_equal))
    vspace()
    b.emit("vector", lambda e: e.tensor_sub(gdm[:], gm1[:], gm2[:]), incs=[(sv, 1)])
    cdm = sv.n
    b.emit("scalar", lambda e: e.activation(gw1[:], gdm[:], AF.Sigmoid),
           waits=[(sv, cdm)], incs=[(sa, 1)])
    cw1 = sa.n
    b.emit("vector", lambda e: e.tensor_scalar(
        gw2[:], gw1[:], -1.0, 1.0, op0=ALU.mult, op1=ALU.add), waits=[(sa, cw1)])
    for t in range(NT):
        b.emit("vector", lambda e, t=t: e.tensor_scalar(
            gtmp[:, t * 8:(t + 1) * 8], geq1[:, t * 8:(t + 1) * 8],
            gw1[:, t:t + 1], None, op0=ALU.mult))
    for t in range(NT):
        b.emit("vector", lambda e, t=t: e.scalar_tensor_tensor(
            comb[:, t * 8:(t + 1) * 8], geq2[:, t * 8:(t + 1) * 8],
            gw2[:, t:t + 1], gtmp[:, t * 8:(t + 1) * 8],
            op0=ALU.mult, op1=ALU.add),
            incs=[(sv, 1)] if t == NT - 1 else [])
    c_comb = sv.n
    tap("comb", comb, [(sv, c_comb)])

    b.emit("vector", lambda e: e.tensor_copy(acc[:], x1T[:]),
           waits=[(sv, c_x1T)], incs=[(sv, 1)])
    c_acc0 = sv.n

    upw_loaded = [0, 0]
    upw_consumed = [c_x2_mm, c_x2_mm]
    dnw_loaded = [0, 0]
    dnw_consumed = [c_x2_mm, c_x2_mm]
    dn_slots = [dnw0, dn2]
    psU_free = [(sv, c_acc0)] * 4
    psDn_free = [c_acc0] * 4
    upslot = [0]
    cb_ready = [0]
    c_prev_dn = [0]

    def up_src(e_, m):
        if e_ < NE:
            return hwup[e_, m]
        return hsgg[m] if m < 12 else hsgu[m - 12]

    def dn_src(e_, m):
        return hwdn[e_, m] if e_ < NE else hsgd[m]

    def emit_expert(ex):
        shared = (ex == NE)
        if not shared:
            crow_done = 0
            for qc in range(4):
                for tt in range(4):
                    t = qc * 4 + tt
                    wl = [(sv, c_comb), (sv, psDn_free[0]), (sg, c_consts)] \
                        if (qc == 0 and tt == 0) else ([(sa, crow_done)] if tt == 0 else [])
                    b.emit("tensor", lambda e, t=t, tt=tt, ex=ex: e.transpose(
                        psA[0][0:1, tt * 128:(tt + 1) * 128],
                        comb[:, t * 8 + ex: t * 8 + ex + 1], ident32[:]),
                        waits=wl, incs=[(sp, 1)] if tt == 3 else [])
                cf = sp.n
                b.emit("scalar", lambda e, qc=qc: e.activation(
                    crow[0:1, qc * 512: qc * 512 + 512], psA[0][0:1, 0:512], AF.Identity),
                    waits=[(sp, cf)], incs=[(sa, 1)])
                crow_done = sa.n
            ca = sa.n
            cbc_ev = 0
            for qc in range(4):
                wl = [(sa, ca)]
                wl.append((sv, psDn_free[1]) if qc == 0 else (sv, cbc_ev))
                b.emit("tensor", lambda e, qc=qc: e.matmul(
                    psA[1][:], ones_rowf[:], crow[0:1, qc * 512: qc * 512 + 512],
                    start=True, stop=True),
                    waits=wl, incs=[(sp, 1)])
                cf = sp.n
                b.emit("vector", lambda e, qc=qc: e.tensor_copy(
                    cbcast[:, qc * 512: qc * 512 + 512], psA[1][:, 0:512]),
                    waits=[(sp, cf)], incs=[(sv, 1)])
                cbc_ev = sv.n
            cb_ready[0] = sv.n
            psDn_free[0] = sv.n
            psDn_free[1] = sv.n

        for half in range(2):
            silu_sa = {}
            first_silu = True
            for mt in range(6):
                for who in range(2):
                    m = (half * 6 + mt) + (0 if who == 0 else 12)
                    slot = upslot[0] % 2
                    upslot[0] += 1
                    b.emit("gpsimd", lambda e, ex=ex, m=m, slot=slot: e.dma_start(
                        upw[slot][:], up_src(ex, m)),
                        waits=[(sp, upw_consumed[slot])], incs=[(sdw, 16)])
                    upw_loaded[slot] = sdw.n
                    banks = (0, 1) if who == 0 else (2, 3)
                    for qc in range(4):
                        bank = banks[qc % 2]
                        wl = [(sdw, upw_loaded[slot]), psU_free[bank], (sv, c_acc0)]
                        for kt in range(8):
                            b.emit("tensor", lambda e, slot=slot, kt=kt, qc=qc, bank=bank: e.matmul(
                                psU[bank], upw[slot][:, kt * 128:(kt + 1) * 128],
                                x1T[:, kt * T + qc * 512: kt * T + qc * 512 + 512],
                                start=(kt == 0), stop=(kt == 7)),
                                waits=wl if kt == 0 else [],
                                incs=[(sp, 1)] if kt == 7 else [])
                        cf = sp.n
                        upw_consumed[slot] = cf
                        if who == 0:
                            swl = [(sp, cf)]
                            if first_silu:
                                swl.append((sp, c_prev_dn[0]))
                                first_silu = False
                            b.emit("scalar", lambda e, mt=mt, qc=qc, bank=bank: e.activation(
                                hbuf[:, mt * T + qc * 512: mt * T + qc * 512 + 512],
                                psU[bank], AF.Silu), waits=swl, incs=[(sa, 1)])
                            silu_sa[(mt, qc)] = sa.n
                            psU_free[bank] = (sa, sa.n)
                        else:
                            if shared:
                                b.emit("vector", lambda e, mt=mt, qc=qc, bank=bank: e.tensor_mul(
                                    hbuf[:, mt * T + qc * 512: mt * T + qc * 512 + 512],
                                    psU[bank],
                                    hbuf[:, mt * T + qc * 512: mt * T + qc * 512 + 512]),
                                    waits=[(sp, cf), (sa, silu_sa[(mt, qc)])],
                                    incs=[(sv, 1)])
                            else:
                                b.emit("vector", lambda e, qc=qc, bank=bank: e.tensor_mul(
                                    utmp[:], psU[bank],
                                    cbcast[:, qc * 512: qc * 512 + 512]),
                                    waits=[(sp, cf), (sv, cb_ready[0])])
                                b.emit("vector", lambda e, mt=mt, qc=qc: e.tensor_mul(
                                    hbuf[:, mt * T + qc * 512: mt * T + qc * 512 + 512],
                                    utmp[:],
                                    hbuf[:, mt * T + qc * 512: mt * T + qc * 512 + 512]),
                                    waits=[(sa, silu_sa[(mt, qc)])],
                                    incs=[(sv, 1)])
                            psU_free[bank] = (sv, sv.n)
            c_h = sv.n

            for mo in range(8):
                slot = mo % 2
                b.emit("gpsimd", lambda e, ex=ex, mo=mo, slot=slot: e.dma_start(
                    dn_slots[slot][:], dn_src(ex, mo)),
                    waits=[(sp, dnw_consumed[slot])], incs=[(sdw, 16)])
                dnw_loaded[slot] = sdw.n
                for qc in range(4):
                    bank = (mo * 4 + qc) % 4
                    wl = [(sdw, dnw_loaded[slot]), (sv, c_h), (sv, psDn_free[bank])]
                    for lt in range(6):
                        kt = half * 6 + lt
                        b.emit("tensor", lambda e, slot=slot, kt=kt, lt=lt, qc=qc, bank=bank: e.matmul(
                            psDn[bank], dn_slots[slot][:, kt * 128:(kt + 1) * 128],
                            hbuf[:, lt * T + qc * 512: lt * T + qc * 512 + 512],
                            start=(lt == 0), stop=(lt == 5)),
                            waits=wl if lt == 0 else [],
                            incs=[(sp, 1)] if lt == 5 else [])
                    cf = sp.n
                    if qc == 3:
                        dnw_consumed[slot] = cf
                    b.emit("vector", lambda e, mo=mo, qc=qc, bank=bank: e.scalar_tensor_tensor(
                        acc[:, mo * T + qc * 512: mo * T + qc * 512 + 512],
                        acc[:, mo * T + qc * 512: mo * T + qc * 512 + 512], 0.0,
                        psDn[bank], op0=ALU.bypass, op1=ALU.add),
                        waits=[(sp, cf)], incs=[(sv, 1)])
                    psDn_free[bank] = sv.n
            c_prev_dn[0] = sp.n

    for ex in range(NE + 1):
        emit_expert(ex)
        if ex == 0:
            tap("hbuf", hbuf, [(sv, sv.n)])
            tap("cbcast", cbcast, [(sv, cb_ready[0])])
    c_ffn_sv = sv.n
    tap("acc", acc, [(sv, c_ffn_sv)])

    tr_evac3 = [0, 0]
    idx = 0
    for t in range(NT):
        for cg in range(2):
            bank = idx % 2
            wl = [(sv, c_ffn_sv)] if idx < 2 else [(sv, tr_evac3[bank])]
            for c4 in range(4):
                c = cg * 4 + c4
                b.emit("tensor", lambda e, t=t, c=c, c4=c4, bank=bank: e.transpose(
                    psP[bank][:, c4 * 128:(c4 + 1) * 128],
                    acc[:, c * T + t * 128: c * T + (t + 1) * 128],
                    ident32[:]), waits=wl if c4 == 0 else [],
                    incs=[(sp, 1)] if c4 == 3 else [])
            cf = sp.n
            b.emit("vector", lambda e, t=t, cg=cg, bank=bank: e.tensor_scalar(
                z_tok[:, t * D + cg * 512: t * D + (cg + 1) * 512], psP[bank][:],
                1.0, 0.0, op0=ALU.mult, op1=ALU.add,
                accum_out=lnz[:, (t * 2 + cg):(t * 2 + cg) + 1]),
                waits=[(sp, cf)], incs=[(sv, 1)])
            tr_evac3[bank] = sv.n
            idx += 1
    c_z = sv.n
    vspace()
    for t in range(NT):
        b.emit("vector", lambda e, t=t: e.reduce_sum(
            sums1[:, t:t + 1], lnz[:, t * 2:(t + 1) * 2], axis=AX.X),
            waits=[(sv, c_z)] if t == 0 else [])
    b.emit("vector", lambda e: e.engine_nop(), incs=[(sv, 1)])
    c_z_sums = sv.n
    for t in range(NT):
        b.emit("scalar", lambda e, t=t: e.activation(
            scr_sq[:], z_tok[:, t * D:(t + 1) * D], AF.Square,
            accum_out=sums2[:, t:t + 1]),
            waits=[(sv, c_z)] if t == 0 else [],
            incs=[(sa, 1)] if t == NT - 1 else [])
    c_z_sq = sa.n
    c_ln3_fin = emit_ln_finish([(sv, c_z_sums), (sa, c_z_sq)])
    for t in range(NT):
        b.emit("scalar", lambda e, t=t: e.activation(
            z_tok[:, t * D:(t + 1) * D], z_tok[:, t * D:(t + 1) * D], AF.Identity,
            scale=rstd[:, t:t + 1], bias=nmr[:, t:t + 1]),
            waits=[(sv, c_ln3_fin)] if t == 0 else [],
            incs=[(sa, 1)] if t == NT - 1 else [])
    c_out_ready = sa.n

    b.emit("sync", lambda e: e.dma_start(hout, z_tok[:]),
           waits=[(sa, c_out_ready)], incs=[(so, 16)])
    tap("z_tok", z_tok, [(sa, c_out_ready)])
    total_so = so.n
    b.emit("gpsimd", lambda e: e.engine_nop(), waits=[(so, total_so)])

    b.finalize(blk)
    est.close()
    return nc


def kernel(L0_attn_X, L0_attn_W_in, L0_attn_b_in, L0_attn_W_out, L0_attn_b_out,
           L0_ffn_gate_w, L0_ffn_w_up, L0_ffn_w_down, L0_ffn_sg_gate,
           L0_ffn_sg_up, L0_ffn_sg_down):
    X = np.asarray(L0_attn_X, dtype=np.float32)
    shared_img = {}
    shared_img["w_in_img"] = _lhsT_slices(np.asarray(L0_attn_W_in, np.float32), 32, 8)
    shared_img["b_in_img"] = np.ascontiguousarray(
        np.asarray(L0_attn_b_in, np.float32).reshape(32, 128).T)
    shared_img["w_out_img"] = _rhs_img(np.ascontiguousarray(
        np.asarray(L0_attn_W_out, np.float32).T))
    shared_img["b_out_row"] = np.ascontiguousarray(
        np.asarray(L0_attn_b_out, np.float32).reshape(1, D))
    shared_img["gate_img"] = _rhs_img(np.ascontiguousarray(
        np.asarray(L0_ffn_gate_w, np.float32).T))
    wup = np.asarray(L0_ffn_w_up, np.float32)
    wdn = np.asarray(L0_ffn_w_down, np.float32)
    shared_img["wup_img"] = np.stack([_lhsT_slices(wup[e], 24, 8) for e in range(NE)])
    shared_img["wdn_img"] = np.stack([_lhsT_slices(wdn[e], 8, 12) for e in range(NE)])
    shared_img["sgg_img"] = _lhsT_slices(np.asarray(L0_ffn_sg_gate, np.float32), 12, 8)
    shared_img["sgu_img"] = _lhsT_slices(np.asarray(L0_ffn_sg_up, np.float32), 12, 8)
    shared_img["sgd_img"] = _lhsT_slices(np.asarray(L0_ffn_sg_down, np.float32), 8, 12)

    nc = build_program()
    in_maps = []
    for c in range(_NUM_CORES):
        d = dict(shared_img)
        d["x_img"] = _img_tok(np.ascontiguousarray(X[c]))
        in_maps.append(d)
    res = run_bass_kernel_spmd(nc, in_maps, core_ids=list(range(_NUM_CORES)))
    out = np.stack([_unimg_tok(res.results[c]["out_img"]) for c in range(_NUM_CORES)])
    if _TAPS:
        kernel._tap_results = [
            {name: res.results[c]["tap_" + name] for (name, _s) in _TAPS}
            for c in range(_NUM_CORES)]
    return out.astype(np.float32)


# version 04
# speedup vs baseline: 1.0029x; 1.0029x over previous
import numpy as np
import ml_dtypes
import concourse.bass as bass
import concourse.mybir as mybir
from concourse.bass_core_rust import BassInstruction
from concourse.bass_utils import run_bass_kernel_spmd

F32 = mybir.dt.float32
F32R = mybir.dt.float32r
BF16 = mybir.dt.bfloat16
AF = mybir.ActivationFunctionType
ALU = mybir.AluOpType
AX = mybir.AxisListType

T = 2048
D = 1024
NT = T // 128
ND = D // 128
H = 16
NE = 8
LN_EPS = 1e-5
SEPS = 1e-4

_TAPS = []
_NUM_CORES = 8


def _img_tok(x2d):
    return np.ascontiguousarray(
        x2d.reshape(NT, 128, D).transpose(1, 0, 2).reshape(128, NT * D))

def _unimg_tok(img):
    return np.ascontiguousarray(
        img.reshape(128, NT, D).transpose(1, 0, 2).reshape(T, D))

def _lhsT_slices(w, n_m, n_k):
    rows, d = w.shape
    assert rows == n_m * 128 and d == n_k * 128
    a = w.reshape(n_m, 128, n_k, 128)
    return np.ascontiguousarray(a.transpose(0, 3, 2, 1).reshape(n_m, 128, n_k * 128))

def _rhs_img(wT):
    d, cols = wT.shape
    nk = d // 128
    a = wT.reshape(nk, 128, cols)
    return np.ascontiguousarray(a.transpose(1, 0, 2).reshape(128, nk * cols))


class Ctr:
    def __init__(self, handle):
        self.h = handle
        self.n = 0

class B:
    def __init__(self, nc):
        self.nc = nc
        self.streams = {e: [] for e in ("tensor", "vector", "scalar", "gpsimd", "sync")}

    def emit(self, eng, fn, waits=(), incs=()):
        for (c, k) in incs:
            c.n += k
        snap = tuple((c.h, v) for (c, v) in waits if v > 0)
        tgt = tuple((c.h, k) for (c, k) in incs)
        def f(e):
            for (h, v) in snap:
                e.wait_ge(h, v)
            inst = fn(e)
            if tgt:
                assert inst is not None
                for (h, k) in tgt:
                    inst = inst.then_inc(h, k)
        self.streams[eng].append(f)

    def finalize(self, block):
        st = self.streams
        @block.sync
        def _(e):
            for f in st["sync"]: f(e)
        @block.gpsimd
        def _(e):
            for f in st["gpsimd"]: f(e)
        @block.vector
        def _(e):
            for f in st["vector"]: f(e)
        @block.scalar
        def _(e):
            for f in st["scalar"]: f(e)
        @block.tensor
        def _(e):
            for f in st["tensor"]: f(e)


def act_swap(nc, s, out, in_, func, bias=0.0, scale=1.0, accum_out=None):
    inst = s.activation(out, in_, AF.Sqrt, bias=bias, scale=scale, accum_out=accum_out)
    i = inst.ins
    clone = mybir.InstActivation(name=i.name, engine=i.engine, ins=list(i.ins),
                                 outs=list(i.outs), func=func)
    clone.sync_info = i.sync_info
    bb = nc.cur_bb.bb
    assert bb.instructions[-1].name == i.name
    bb.instructions[-1] = clone
    return BassInstruction(clone)


def build_program():
    nc = bass.Bass("TRN2", target_bir_lowering=False, detect_race_conditions=False)

    hx = nc.dram_tensor("x_img", [128, NT * D], F32, kind="ExternalInput").ap()
    hwin = nc.dram_tensor("w_in_img", [32, 128, 1024], F32, kind="ExternalInput").ap()
    hbin = nc.dram_tensor("b_in_img", [128, 32], F32, kind="ExternalInput").ap()
    hwout = nc.dram_tensor("w_out_img", [128, 8 * 1024], F32, kind="ExternalInput").ap()
    hbout = nc.dram_tensor("b_out_row", [1, D], F32, kind="ExternalInput").ap()
    hgate = nc.dram_tensor("gate_img", [128, 64], F32, kind="ExternalInput").ap()
    hwup = nc.dram_tensor("wup_img", [NE, 24, 128, 1024], F32, kind="ExternalInput").ap()
    hwdn = nc.dram_tensor("wdn_img", [NE, 8, 128, 1536], F32, kind="ExternalInput").ap()
    hsgg = nc.dram_tensor("sgg_img", [12, 128, 1024], F32, kind="ExternalInput").ap()
    hsgu = nc.dram_tensor("sgu_img", [12, 128, 1024], F32, kind="ExternalInput").ap()
    hsgd = nc.dram_tensor("sgd_img", [8, 128, 1536], F32, kind="ExternalInput").ap()
    hout = nc.dram_tensor("out_img", [128, NT * D], F32, kind="ExternalOutput").ap()
    htap = {}
    for (name, shape) in _TAPS:
        htap[name] = nc.dram_tensor("tap_" + name, list(shape), F32,
                                    kind="ExternalOutput").ap()

    KB = 1024
    ARENA = 200 * KB
    base0, _end = nc.bump_sbuf(ARENA, align_bytes=256)
    A0, B0, C0, D0, S0 = (base0, base0 + 64 * KB, base0 + 128 * KB,
                          base0 + 168 * KB, base0 + 184 * KB)

    def at(name, off, shape, dtype):
        return nc.alloc_sbuf_tensor_at(name, shape, dtype, offset=off).ap()

    X = at("X", A0, [128, NT * D], F32)
    qaug = [at(f"qaug{i}", A0 + i * 4 * KB, [66, T], BF16) for i in range(2)]
    kaug = [at(f"kaug{i}", A0 + 8 * KB + i * 4 * KB, [66, T], BF16) for i in range(2)]
    vT = at("vT", A0 + 16 * KB, [64, T], F32)
    gTb = [at(f"gTb{i}", A0 + 24 * KB + i * 4 * KB, [64, T], BF16) for i in range(2)]
    vprime = [at(f"vp{i}", A0 + 32 * KB + i * 5 * KB, [128, 16 * 65], BF16) for i in range(2)]
    qsq = at("qsq", A0 + 42 * KB, [64, 512], F32)
    rec = [at(f"rec{i}", A0 + 44 * KB + i * 2 * KB, [64, 512], F32) for i in range(2)]
    normt = [at(f"normt{i}", A0 + 48 * KB + i * 2 * KB, [64, 512], F32) for i in range(2)]
    rowt2 = at("rowt2", A0 + 52 * KB, [1, 512], F32)
    qsq2 = at("qsq2", A0 + 54 * KB, [64, 512], F32)
    astg = [at(f"astg{i}", A0 + 56 * KB + i * 2 * KB, [65, 512], F32) for i in range(4)]
    r_tok = at("r_tok", A0, [128, NT * D], F32)
    x1T = at("x1T", A0, [128, ND * T], F32R)
    z_tok = at("z_tok", A0, [128, NT * D], F32)

    xn_tok = at("xn_tok", B0, [128, NT * D], BF16)
    xnT = at("xnT", B0 + 32 * KB, [128, ND * T], BF16)
    u_s = [at(f"u{i}", B0 + i * 2 * KB, [128, 1024], BF16) for i in range(4)]
    uu_s = [at(f"uu{i}", B0 + 8 * KB + i * 2 * KB, [128, 1024], BF16) for i in range(4)]
    wring = [at(f"wr{i}", B0 + 16 * KB + i * 2 * KB, [128, 1024], BF16) for i in range(4)]
    X_re = at("X_re", B0, [128, NT * D], F32)
    x1_tok = at("x1_tok", B0, [128, NT * D], F32)
    acc = at("acc", B0, [128, ND * T], F32)

    OT = at("OT", C0, [128, 9 * T], BF16)
    hbuf = at("hbuf", C0, [128, 6 * T], BF16)
    cbcast = at("cbcast", C0 + 24 * KB, [128, T], F32)
    crow = at("crow", C0 + 32 * KB, [1, T], F32)

    winw = [at(f"winw{i}", D0 + i * 2 * KB, [128, 1024], BF16) for i in range(8)]
    woutb = at("woutb", D0, [128, 8 * 1024], BF16)
    upw = [at(f"upw{i}", D0 + i * 4 * KB, [128, 1024], F32R) for i in range(2)]
    dnw0 = at("dnw0", D0 + 8 * KB, [128, 1536], BF16)
    dn2 = at("dn2", D0 + 11 * KB, [128, 1536], BF16)

    o = S0
    ident32 = at("ident32", o, [128, 128], F32); o += 512
    identbf = at("identbf", o, [128, 128], BF16); o += 256
    ones_col = at("ones_col", o, [128, 1], F32); o += 32
    q2lhs = at("q2lhs", o, [64, 2], F32); o += 32
    k2lhs = at("k2lhs", o, [64, 2], F32); o += 32
    bias_q = at("bias_q", o, [2, 1], F32); o += 32
    bias_k = at("bias_k", o, [2, 1], F32); o += 32
    eps_col = at("eps_col", o, [128, 1], F32); o += 32
    seps1 = at("seps1", o, [1, 1], F32); o += 32
    binsb = at("binsb", o, [128, 32], F32); o += 128
    binsb_n2 = at("binsb_n2", o, [128, 32], F32); o += 128
    ones_rowf = at("ones_rowf", o, [1, 128], F32); o += 512
    boutr = at("boutr", o, [1, D], BF16); o += 2 * KB
    gatesb = at("gatesb", o, [128, 64], F32R); o += 256
    iotar = at("iotar", o, [128, 8], F32); o += 32
    sums1 = at("sums1", o, [128, NT], F32); o += 64
    sums1b = at("sums1b", o, [128, NT], F32); o += 64
    sums2 = at("sums2", o, [128, NT], F32); o += 64
    mu = at("mu", o, [128, NT], F32); o += 64
    var_ = at("var", o, [128, NT], F32); o += 64
    rstd = at("rstd", o, [128, NT], F32); o += 64
    nmr = at("nmr", o, [128, NT], F32); o += 64
    scr_sq = at("scr_sq", o, [128, D], F32); o += 4 * KB
    logits = at("logits", o, [128, NT * 8], F32); o += 512
    comb = at("comb", o, [128, NT * 8], F32); o += 512
    gm1 = at("gm1", o, [128, NT], F32); o += 64
    gm2 = at("gm2", o, [128, NT], F32); o += 64
    gmi = at("gmi", o, [128, NT], F32); o += 64
    gmi2 = at("gmi2", o, [128, NT], F32); o += 64
    gw1 = at("gw1", o, [128, NT], F32); o += 64
    gw2 = at("gw2", o, [128, NT], F32); o += 64
    gdm = at("gdm", o, [128, NT], F32); o += 64
    geqA = at("geqA", o, [128, 128], F32); o += 512
    geq1 = at("geq1", o, [128, 128], F32); o += 512
    geq2 = at("geq2", o, [128, 128], F32); o += 512
    grow2 = at("grow2", o, [128, 128], F32); o += 512
    gtmp = at("gtmp", o, [128, 128], F32); o += 512
    utmp = at("utmp", o, [128, 512], F32); o += 2 * KB
    lnz = at("lnz", o, [128, NT * ND], F32); o += 512
    assert o - S0 <= 16 * KB, f"small region overflow: {o - S0}"

    psS = [nc.alloc_psum_tensor(f"psS{i}", [128, 1024], F32).ap() for i in range(2)]
    psA = [nc.alloc_psum_tensor(f"psA{i}", [128, 512], F32).ap() for i in range(2)]
    psP = [nc.alloc_psum_tensor(f"psP{i}", [128, 512], F32).ap() for i in range(2)]
    psPbf = [p.bitcast(BF16) for p in psP]
    psU = [psS[0][:, 0:512], psS[0][:, 512:1024], psS[1][:, 0:512], psS[1][:, 512:1024]]
    psDn = [psA[0], psA[1], psP[0], psP[1]]

    import contextlib
    est = contextlib.ExitStack()
    sems = {}
    for name in ("sdin", "sdw", "sp", "sa", "sv", "sg", "so"):
        sems[name] = Ctr(est.enter_context(nc.semaphore(name)))
    sdin, sdw, sp, sa, sv, sg, so = (sems[k] for k in
                                     ("sdin", "sdw", "sp", "sa", "sv", "sg", "so"))
    blk = est.enter_context(nc.Block())
    b = B(nc)
    tap_names = {name: shape for (name, shape) in _TAPS}

    def tap(name, src, waits):
        if name not in tap_names:
            return
        shape = tap_names[name]
        b.emit("gpsimd", lambda e, name=name, src=src, shape=shape: e.dma_start(
            htap[name], src[0:shape[0], 0:shape[1]]),
            waits=waits, incs=[(so, 16)])

    b.emit("sync", lambda e: e.dma_start(X[:], hx), incs=[(sdin, 16)])
    c_X = sdin.n
    b.emit("sync", lambda e: e.dma_start(binsb[:], hbin), incs=[(sdin, 16)])
    c_bin = sdin.n
    tap("X", X, [(sdin, c_X)])
    b.emit("gpsimd", lambda e: e.dma_start(gatesb[:], hgate), incs=[(sdw, 16)])
    c_gate = sdw.n
    b.emit("gpsimd", lambda e: e.dma_start(boutr[:], hbout), incs=[(sdw, 16)])
    c_bout = sdw.n

    def _gsetup(e):
        e.memset(ident32[:], 0.0)
        nc.gpsimd.affine_select(out=ident32[:], in_=ident32[:], compare_op=ALU.not_equal,
                                fill=1.0, base=0, pattern=[[-1, 128]], channel_multiplier=1)
        e.memset(identbf[:], 0.0)
        nc.gpsimd.affine_select(out=identbf[:], in_=identbf[:], compare_op=ALU.not_equal,
                                fill=1.0, base=0, pattern=[[-1, 128]], channel_multiplier=1)
        e.memset(ones_col[:], 1.0)
        e.memset(ones_rowf[:], 1.0)
        e.memset(q2lhs[:], 0.0)
        e.memset(q2lhs[:, 0:1], 0.25)
        e.memset(k2lhs[:], 0.0)
        e.memset(k2lhs[:, 1:2], 1.0)
        e.iota(bias_q[:], pattern=[[0, 1]], base=0, channel_multiplier=1,
               allow_small_or_imprecise_dtypes=True)
        e.iota(bias_k[:], pattern=[[0, 1]], base=1, channel_multiplier=-1,
               allow_small_or_imprecise_dtypes=True)
        e.memset(eps_col[:], LN_EPS)
        e.memset(seps1[:], SEPS)
        e.memset(OT[0:1, 8 * T:9 * T], 1.0)
        return nc.gpsimd.iota(iotar[:], pattern=[[-1, 8]], base=8, channel_multiplier=0,
                              allow_small_or_imprecise_dtypes=True)
    b.emit("gpsimd", _gsetup, incs=[(sg, 1)])
    c_consts = sg.n

    win_loaded = [0] * 32
    for m in range(8):
        b.emit("gpsimd", lambda e, m=m: e.dma_start(winw[m % 8][:], hwin[m]),
               incs=[(sdw, 16)])
        win_loaded[m] = sdw.n

    b.emit("vector", lambda e: e.tensor_scalar_mul(binsb_n2[:], binsb[:], -2.0),
           waits=[(sdin, c_bin)], incs=[(sv, 1)])

    def vspace():
        b.emit("vector", lambda e: e.tensor_scalar_mul(scr_sq[:, 0:512],
                                                       scr_sq[:, 0:512], 1.0))

    def emit_ln_finish(waits):
        b.emit("vector", lambda e: e.tensor_scalar_mul(mu[:], sums1[:], 1.0 / D),
               waits=waits)
        b.emit("vector", lambda e: e.tensor_scalar_mul(var_[:], sums2[:], 1.0 / D))
        vspace()
        b.emit("vector", lambda e: e.tensor_mul(nmr[:], mu[:], mu[:]))
        vspace()
        b.emit("vector", lambda e: e.tensor_sub(var_[:], var_[:], nmr[:]), incs=[(sv, 1)])
        cv = sv.n
        b.emit("scalar", lambda e: act_swap(nc, e, rstd[:], var_[:], AF.Rsqrt,
                                            bias=eps_col[:, 0:1]),
               waits=[(sv, cv), (sg, c_consts)], incs=[(sa, 1)])
        ca = sa.n
        b.emit("vector", lambda e: e.scalar_tensor_tensor(
            nmr[:], mu[:], -1.0, rstd[:], op0=ALU.mult, op1=ALU.mult),
            waits=[(sa, ca)], incs=[(sv, 1)])
        return sv.n

    for t in range(NT):
        b.emit("vector", lambda e, t=t: e.reduce_sum(
            sums1[:, t:t + 1], X[:, t * D:(t + 1) * D], axis=AX.X),
            waits=[(sdin, c_X)] if t == 0 else [])
    b.emit("vector", lambda e: e.engine_nop(), incs=[(sv, 1)])
    c_ln1_sums = sv.n
    for t in range(NT):
        b.emit("scalar", lambda e, t=t: e.activation(
            scr_sq[:], X[:, t * D:(t + 1) * D], AF.Square,
            accum_out=sums2[:, t:t + 1]),
            waits=[(sdin, c_X)] if t == 0 else [],
            incs=[(sa, 1)] if t == NT - 1 else [])
    c_ln1_sq = sa.n
    c_ln1_fin = emit_ln_finish([(sv, c_ln1_sums), (sa, c_ln1_sq)])
    tap("mu1", mu, [(sv, c_ln1_fin)])
    tap("rstd1", rstd, [(sv, c_ln1_fin)])
    tap("s1", sums1, [(sv, c_ln1_sums)])
    tap("s2", sums2, [(sa, c_ln1_sq)])
    for t in range(NT):
        b.emit("scalar", lambda e, t=t: e.activation(
            xn_tok[:, t * D:(t + 1) * D], X[:, t * D:(t + 1) * D], AF.Identity,
            scale=rstd[:, t:t + 1], bias=nmr[:, t:t + 1]),
            waits=[(sv, c_ln1_fin)] if t == 0 else [],
            incs=[(sa, 1)] if t == NT - 1 else [])
    c_ln1_apply = sa.n
    tap("xn_tok", xn_tok, [(sa, c_ln1_apply)])

    def _vp_ones(e):
        last = None
        for i in range(2):
            for kt in range(NT):
                last = e.activation(vprime[i][:, kt * 65 + 64: kt * 65 + 65],
                                    ones_col[:], AF.Identity)
        return last
    b.emit("scalar", _vp_ones, waits=[(sg, c_consts)], incs=[(sa, 1)])
    c_vpones = sa.n

    tr_evac = [0, 0]
    for t in range(NT):
        bank = t % 2
        wl = [(sa, c_ln1_apply), (sg, c_consts)] if t < 2 else [(sv, tr_evac[bank])]
        for c in range(ND):
            b.emit("tensor", lambda e, t=t, c=c, bank=bank: e.transpose(
                psPbf[bank][:, c * 128:(c + 1) * 128],
                xn_tok[:, t * D + c * 128: t * D + (c + 1) * 128],
                identbf[:]), waits=wl if c == 0 else [],
                incs=[(sp, 1)] if c == ND - 1 else [])
        cf = sp.n
        dst = xnT.rearrange("p (c y) -> p c y", c=ND)[:, :, t * 128:(t + 1) * 128]
        b.emit("vector", lambda e, dst=dst, bank=bank: e.tensor_copy(
            dst, psPbf[bank][:].rearrange("p (c x) -> p c x", c=ND)),
            waits=[(sp, cf)], incs=[(sv, 1)])
        tr_evac[bank] = sv.n
    c_xnT = sv.n
    tap("xnT", xnT, [(sv, c_xnT)])

    pP_state = [(sv, tr_evac[0]), (sv, tr_evac[1])]
    prep_done = [None] * H
    main_sv = [0] * H
    psA_free = [0, 0]
    psS_free = [0, 0]
    psS0_borrow = [0]
    wring_free = [0, 0, 0, 0]
    uslot_free = [0, 0, 0, 0]
    last_phead_mm = [0]
    last_q2mm = [0]
    wready = {}

    def emit_prep(h):
        par = h % 2
        ready = {}
        c_prev_main = main_sv[h - 2] if h >= 2 else 0
        for mi in range(2):
            m = 2 * h + mi
            slot = m % 8
            for qc in range(4):
                bank = mi
                wl = [(sdw, win_loaded[m]), (sv, c_xnT), pP_state[bank]]
                for kt in range(8):
                    b.emit("tensor", lambda e, qc=qc, kt=kt, slot=slot, bank=bank: e.matmul(
                        psP[bank][:],
                        winw[slot][:, kt * 128:(kt + 1) * 128],
                        xnT[:, kt * T + qc * 512: kt * T + qc * 512 + 512],
                        start=(kt == 0), stop=(kt == 7)),
                        waits=wl if kt == 0 else [],
                        incs=[(sp, 1)] if kt == 7 else [])
                cfill = sp.n
                last_phead_mm[0] = cfill
                if mi == 0:
                    b.emit("scalar", lambda e, par=par, qc=qc, m=m: e.activation(
                        qaug[par][0:64, qc * 512: qc * 512 + 512], psP[0][0:64, :],
                        AF.Identity, scale=-2.0, bias=binsb_n2[0:64, m:m + 1]),
                        waits=[(sp, cfill), (sv, c_prev_main)])
                    b.emit("scalar", lambda e, par=par, qc=qc, m=m: e.activation(
                        kaug[par][0:64, qc * 512: qc * 512 + 512], psP[0][64:128, :],
                        AF.Identity, bias=binsb[64:128, m:m + 1]), incs=[(sa, 1)])
                else:
                    b.emit("scalar", lambda e, qc=qc, m=m: e.activation(
                        vT[0:64, qc * 512: qc * 512 + 512], psP[1][0:64, :],
                        AF.Identity, bias=binsb[0:64, m:m + 1]),
                        waits=[(sp, cfill), (sv, c_prev_main)])
                    b.emit("scalar", lambda e, par=par, qc=qc, m=m: e.activation(
                        gTb[par][0:64, qc * 512: qc * 512 + 512], psP[1][64:128, :],
                        AF.Identity, bias=binsb[64:128, m:m + 1]), incs=[(sa, 1)])
                pP_state[bank] = (sa, sa.n)
            if m + 8 < 32:
                cm = sp.n
                b.emit("gpsimd", lambda e, m=m, slot=slot: e.dma_start(
                    winw[slot][:], hwin[m + 8]), waits=[(sp, cm)], incs=[(sdw, 16)])
                win_loaded[m + 8] = sdw.n
        ready["qk_evac"] = sa.n

        qsq_mm = [0, 0]
        for which in range(2):
            src = qaug[par] if which == 0 else kaug[par]
            lhs = q2lhs if which == 0 else k2lhs
            bias = bias_q if which == 0 else bias_k
            for qc in range(4):
                qb = (which * 4 + qc) % 2
                qbuf = qsq if qb == 0 else qsq2
                b.emit("vector", lambda e, src=src, qc=qc, qbuf=qbuf: e.tensor_mul(
                    qbuf[:], src[0:64, qc * 512: qc * 512 + 512],
                    src[0:64, qc * 512: qc * 512 + 512]),
                    waits=[(sa, ready["qk_evac"]), (sp, qsq_mm[qb])],
                    incs=[(sv, 1)])
                cs = sv.n
                b.emit("tensor", lambda e, lhs=lhs, qbuf=qbuf: e.matmul(
                    psP[0][0:2, :], lhs[:], qbuf[:], start=True, stop=True),
                    waits=[(sv, cs), pP_state[0], (sg, c_consts)], incs=[(sp, 1)])
                qsq_mm[qb] = sp.n
                last_q2mm[0] = sp.n
                b.emit("scalar", lambda e, src=src, qc=qc, bias=bias: e.activation(
                    src[64:66, qc * 512: qc * 512 + 512], psP[0][0:2, :], AF.Identity,
                    bias=bias[:, 0:1]), waits=[(sp, last_q2mm[0])], incs=[(sa, 1)])
                pP_state[0] = (sa, sa.n)
        ready["aug"] = sa.n

        for grp in range(2):
            wl0 = [pP_state[1], (sa, c_vpones)]
            for k8 in range(8):
                kt = grp * 8 + k8
                b.emit("tensor", lambda e, kt=kt, k8=k8: e.transpose(
                    psP[1][:, k8 * 64:(k8 + 1) * 64],
                    vT[0:64, kt * 128:(kt + 1) * 128], ident32[0:64, 0:64]),
                    waits=wl0 if k8 == 0 else [],
                    incs=[(sp, 1)] if k8 == 7 else [])
            cf = sp.n
            dst = vprime[par].rearrange("p (k y) -> p k y", k=16)[
                :, grp * 8:(grp + 1) * 8, 0:64]
            b.emit("vector", lambda e, dst=dst: e.tensor_copy(
                dst, psP[1][:].rearrange("p (k y) -> p k y", k=8)),
                waits=[(sp, cf)], incs=[(sv, 1)])
            pP_state[1] = (sv, sv.n)
        ready["vp"] = sv.n
        ready["g"] = ready["qk_evac"]
        prep_done[h] = ready
        if h == 0:
            tap("qaug0", qaug[0], [(sa, ready["aug"])])
            tap("kaug0", kaug[0], [(sa, ready["aug"])])
            tap("vT", vT, [(sa, ready["qk_evac"])])
            tap("gTb0", gTb[0], [(sa, ready["qk_evac"])])
            tap("vp0", vprime[0], [(sv, ready["vp"])])

    LAG = 3

    def emit_S_EW(h, g):
        par = h % 2
        rd = prep_done[h]
        qc2, kt = divmod(g, NT)
        i = g % 2
        slot2 = g % 4
        for half in range(2):
            wl = []
            if half == 0:
                wl = [(sa, rd["aug"]), (sa, psS_free[i])]
                if i == 0:
                    wl.append((sa, psS0_borrow[0]))
            b.emit("tensor", lambda e, par=par, kt=kt, qc2=qc2, half=half, i=i: e.matmul(
                psS[i][:, half * 512:(half + 1) * 512],
                kaug[par][:, kt * 128:(kt + 1) * 128],
                qaug[par][:, qc2 * 1024 + half * 512: qc2 * 1024 + (half + 1) * 512],
                start=True, stop=True),
                waits=wl, incs=[(sp, 1)] if half == 1 else [])
        cS = sp.n
        b.emit("scalar", lambda e, i=i, slot2=slot2: act_swap(
            nc, e, u_s[slot2][:], psS[i][:], AF.Rsqrt),
            waits=[(sp, cS), (sv, uslot_free[slot2])], incs=[(sa, 1)])
        psS_free[i] = sa.n
        cu = sa.n
        if g % 2 == 0:
            b.emit("scalar", lambda e, slot2=slot2: e.activation(
                uu_s[slot2][:], u_s[slot2][:], AF.Square), incs=[(sa, 1)])
            cuu = (sa, sa.n)
        else:
            b.emit("vector", lambda e, slot2=slot2: e.tensor_mul(
                uu_s[slot2][:], u_s[slot2][:], u_s[slot2][:]),
                waits=[(sa, cu)], incs=[(sv, 1)])
            cuu = (sv, sv.n)
        b.emit("vector", lambda e, slot2=slot2: e.scalar_tensor_tensor(
            uu_s[slot2][:], uu_s[slot2][:], -SEPS, u_s[slot2][:],
            op0=ALU.mult, op1=ALU.add), waits=[cuu, (sa, cu)])
        b.emit("vector", lambda e, slot2=slot2, wslot=g % 4: e.tensor_mul(
            wring[wslot][:], uu_s[slot2][:], uu_s[slot2][:]),
            waits=[(sp, wring_free[g % 4])], incs=[(sv, 1)])
        uslot_free[slot2] = sv.n
        wready[g] = sv.n

    def emit_A(h, g):
        par = h % 2
        rd = prep_done[h]
        qc2, kt = divmod(g, NT)
        wslot = g % 4
        for j in range(2):
            wl = [(sv, wready[g])]
            if kt == 0:
                wl += [(sv, rd["vp"]), (sv, psA_free[j])]
            b.emit("tensor", lambda e, kt=kt, j=j, par=par, wslot=wslot: e.matmul(
                psA[j][0:65, :], vprime[par][:, kt * 65: kt * 65 + 65],
                wring[wslot][:, j * 512:(j + 1) * 512],
                start=(kt == 0), stop=(kt == NT - 1)),
                waits=wl, incs=[(sp, 1)] if j == 1 else [])
        wring_free[wslot] = sp.n

    def emit_norm(h, qc2):
        par = h % 2
        rd = prep_done[h]
        cA = sp.n
        for j in range(2):
            qc = qc2 * 2 + j
            stg = astg[qc2 * 2 + j]
            b.emit("vector", lambda e, j=j, stg=stg: e.tensor_copy(
                stg[:], psA[j][0:65, :]), waits=[(sp, cA)], incs=[(sv, 1)])
            psA_free[j] = sv.n
            cst = sv.n
            b.emit("scalar", lambda e, stg=stg: e.activation(
                rowt2[:], stg[64:65, :], AF.Square, bias=seps1[:, 0:1]),
                waits=[(sv, cst)], incs=[(sa, 1)])
            crr = sa.n
            b.emit("tensor", lambda e: e.matmul(
                psS[0][0:64, 0:512], ones_rowf[0:1, 0:64], rowt2[:],
                start=True, stop=True),
                waits=[(sa, crr), (sa, psS_free[0])], incs=[(sp, 1)])
            cb = sp.n
            b.emit("scalar", lambda e, j=j: act_swap(
                nc, e, rec[j][:], psS[0][0:64, 0:512], AF.Rsqrt),
                waits=[(sp, cb)], incs=[(sa, 1)])
            psS0_borrow[0] = sa.n
            cre = sa.n
            b.emit("vector", lambda e, j=j, stg=stg: e.tensor_mul(
                normt[j][:], stg[0:64, :], rec[j][:]),
                waits=[(sa, cre)])
            b.emit("vector", lambda e, j=j, qc=qc, par=par, h=h: e.tensor_mul(
                OT[(h % 2) * 64:(h % 2) * 64 + 64,
                   (h // 2) * T + qc * 512:(h // 2) * T + qc * 512 + 512],
                normt[j][:], gTb[par][0:64, qc * 512: qc * 512 + 512]),
                waits=[(sa, rd["g"])], incs=[(sv, 1)])

    def emit_main(h):
        wready.clear()
        for g in range(2 * NT):
            emit_S_EW(h, g)
            if g >= LAG:
                emit_A(h, g - LAG)
                if (g - LAG) % NT == NT - 1:
                    emit_norm(h, (g - LAG) // NT)
        for g in range(2 * NT - LAG, 2 * NT):
            emit_A(h, g)
            if g % NT == NT - 1:
                emit_norm(h, g // NT)
        main_sv[h] = sv.n

    emit_prep(0)
    if H > 1:
        emit_prep(1)
    for h in range(H):
        emit_main(h)
        if h + 2 < H:
            emit_prep(h + 2)
    c_heads_sv = sv.n
    c_heads_sa = sa.n
    tap("OT", OT, [(sv, c_heads_sv), (sg, c_consts)])

    b.emit("gpsimd", lambda e: e.dma_start(woutb[:], hwout),
           waits=[(sp, last_phead_mm[0])], incs=[(sdw, 16)])
    c_wout = sdw.n
    b.emit("sync", lambda e: e.dma_start(X_re[:], hx),
           waits=[(sv, c_heads_sv), (sa, c_heads_sa)], incs=[(sdin, 16)])
    c_Xre = sdin.n

    x2banks = [psS[0][:, 0:512], psS[0][:, 512:1024], psS[1][:, 0:512], psS[1][:, 512:1024]]
    x2_free = [c_heads_sv] * 4
    idx = 0
    for t in range(NT):
        for c2 in range(2):
            bank = idx % 4
            wl = [(sv, x2_free[bank]), (sdw, c_wout)]
            if idx == 0:
                wl += [(sa, c_heads_sa), (sg, c_consts)]
            for kt in range(8):
                b.emit("tensor", lambda e, t=t, c2=c2, kt=kt, bank=bank: e.matmul(
                    x2banks[bank], OT[:, kt * T + t * 128: kt * T + (t + 1) * 128],
                    woutb[:, kt * 1024 + c2 * 512: kt * 1024 + c2 * 512 + 512],
                    start=(kt == 0), stop=False),
                    waits=wl if kt == 0 else [])
            b.emit("tensor", lambda e, t=t, c2=c2, bank=bank: e.matmul(
                x2banks[bank], OT[0:1, 8 * T + t * 128: 8 * T + (t + 1) * 128],
                boutr[0:1, c2 * 512: c2 * 512 + 512],
                start=False, stop=True),
                waits=[(sdw, c_bout)] if idx == 0 else [], incs=[(sp, 1)])
            cf = sp.n
            sumcol = sums1 if c2 == 0 else sums1b
            b.emit("vector", lambda e, t=t, c2=c2, bank=bank, sumcol=sumcol: e.scalar_tensor_tensor(
                r_tok[:, t * D + c2 * 512: t * D + c2 * 512 + 512],
                X_re[:, t * D + c2 * 512: t * D + c2 * 512 + 512], 2.0,
                x2banks[bank], op0=ALU.mult, op1=ALU.add,
                accum_out=sumcol[:, t:t + 1]),
                waits=[(sp, cf)] + ([(sdin, c_Xre)] if idx == 0 else []),
                incs=[(sv, 1)])
            x2_free[bank] = sv.n
            idx += 1
    c_x2_mm = sp.n
    vspace()
    b.emit("vector", lambda e: e.tensor_add(sums1[:], sums1[:], sums1b[:]), incs=[(sv, 1)])
    c_r_done = sv.n
    tap("r_tok", r_tok, [(sv, c_r_done)])
    for t in range(NT):
        b.emit("scalar", lambda e, t=t: e.activation(
            scr_sq[:], r_tok[:, t * D:(t + 1) * D], AF.Square,
            accum_out=sums2[:, t:t + 1]),
            waits=[(sv, c_r_done)] if t == 0 else [],
            incs=[(sa, 1)] if t == NT - 1 else [])
    c_r_sq = sa.n
    c_ln2_fin = emit_ln_finish([(sv, c_r_done), (sa, c_r_sq)])
    for t in range(NT):
        b.emit("scalar", lambda e, t=t: e.activation(
            x1_tok[:, t * D:(t + 1) * D], r_tok[:, t * D:(t + 1) * D], AF.Identity,
            scale=rstd[:, t:t + 1], bias=nmr[:, t:t + 1]),
            waits=[(sv, c_ln2_fin), (sdin, c_Xre)] if t == 0 else [],
            incs=[(sa, 1)] if t == NT - 1 else [])
    c_x1_apply = sa.n
    tap("x1_tok", x1_tok, [(sa, c_x1_apply)])

    tr_evac2 = [0, 0]
    idx = 0
    x1T_4d = x1T.rearrange("p (c y) -> p c y", c=ND)
    for t in range(NT):
        for cg in range(2):
            bank = idx % 2
            wl = [(sa, c_x1_apply)] if idx < 2 else [(sv, tr_evac2[bank])]
            for c4 in range(4):
                c = cg * 4 + c4
                b.emit("tensor", lambda e, t=t, c=c, c4=c4, bank=bank: e.transpose(
                    psP[bank][:, c4 * 128:(c4 + 1) * 128],
                    x1_tok[:, t * D + c * 128: t * D + (c + 1) * 128],
                    ident32[:]), waits=wl if c4 == 0 else [],
                    incs=[(sp, 1)] if c4 == 3 else [])
            cf = sp.n
            dst = x1T_4d[:, cg * 4:(cg + 1) * 4, t * 128:(t + 1) * 128]
            b.emit("vector", lambda e, dst=dst, bank=bank: e.tensor_copy(
                dst, psP[bank][:].rearrange("p (c x) -> p c x", c=4)),
                waits=[(sp, cf)], incs=[(sv, 1)])
            tr_evac2[bank] = sv.n
            idx += 1
    c_x1T = sv.n
    tap("x1T", x1T, [(sv, c_x1T)])

    glog_free = [c_x1T, c_x1T]
    for t in range(NT):
        bank = psA[t % 2]
        wl = [(sv, glog_free[t % 2]), (sdw, c_gate)]
        for kt in range(8):
            b.emit("tensor", lambda e, t=t, kt=kt, bank=bank: e.matmul(
                bank[:, 0:8], x1T[:, kt * T + t * 128: kt * T + (t + 1) * 128],
                gatesb[:, kt * 8:(kt + 1) * 8],
                start=(kt == 0), stop=(kt == 7)),
                waits=wl if kt == 0 else [], incs=[(sp, 1)] if kt == 7 else [])
        cf = sp.n
        b.emit("vector", lambda e, t=t, bank=bank: e.tensor_copy(
            logits[:, t * 8:(t + 1) * 8], bank[:, 0:8]), waits=[(sp, cf)],
            incs=[(sv, 1)])
        glog_free[t % 2] = sv.n
    c_logits = sv.n
    tap("logits", logits, [(sv, c_logits)])

    vspace()
    for t in range(NT):
        b.emit("vector", lambda e, t=t: e.reduce_max(
            gm1[:, t:t + 1], logits[:, t * 8:(t + 1) * 8], axis=AX.X),
            waits=[(sv, c_logits), (sg, c_consts)] if t == 0 else [])
    for t in range(NT):
        b.emit("vector", lambda e, t=t: e.tensor_scalar(
            geqA[:, t * 8:(t + 1) * 8], logits[:, t * 8:(t + 1) * 8],
            gm1[:, t:t + 1], None, op0=ALU.is_equal))
    for t in range(NT):
        b.emit("vector", lambda e, t=t: e.scalar_tensor_tensor(
            grow2[:, t * 8:(t + 1) * 8], geqA[:, t * 8:(t + 1) * 8], -1e30,
            logits[:, t * 8:(t + 1) * 8], op0=ALU.mult, op1=ALU.add))
    for t in range(NT):
        b.emit("vector", lambda e, t=t: e.reduce_max(
            gm2[:, t:t + 1], grow2[:, t * 8:(t + 1) * 8], axis=AX.X))
    for t in range(NT):
        b.emit("vector", lambda e, t=t: e.tensor_mul(
            gtmp[:, t * 8:(t + 1) * 8], geqA[:, t * 8:(t + 1) * 8], iotar[:]))
    for t in range(NT):
        b.emit("vector", lambda e, t=t: e.reduce_max(
            gmi[:, t:t + 1], gtmp[:, t * 8:(t + 1) * 8], axis=AX.X))
    for t in range(NT):
        b.emit("vector", lambda e, t=t: e.tensor_scalar(
            geq1[:, t * 8:(t + 1) * 8], iotar[:], gmi[:, t:t + 1], None,
            op0=ALU.is_equal))
    for t in range(NT):
        b.emit("vector", lambda e, t=t: e.tensor_scalar(
            geq2[:, t * 8:(t + 1) * 8], grow2[:, t * 8:(t + 1) * 8],
            gm2[:, t:t + 1], None, op0=ALU.is_equal))
    for t in range(NT):
        b.emit("vector", lambda e, t=t: e.tensor_mul(
            gtmp[:, t * 8:(t + 1) * 8], geq2[:, t * 8:(t + 1) * 8], iotar[:]))
    for t in range(NT):
        b.emit("vector", lambda e, t=t: e.reduce_max(
            gmi2[:, t:t + 1], gtmp[:, t * 8:(t + 1) * 8], axis=AX.X))
    for t in range(NT):
        b.emit("vector", lambda e, t=t: e.tensor_scalar(
            geq2[:, t * 8:(t + 1) * 8], iotar[:], gmi2[:, t:t + 1], None,
            op0=ALU.is_equal))
    vspace()
    b.emit("vector", lambda e: e.tensor_sub(gdm[:], gm1[:], gm2[:]), incs=[(sv, 1)])
    cdm = sv.n
    b.emit("scalar", lambda e: e.activation(gw1[:], gdm[:], AF.Sigmoid),
           waits=[(sv, cdm)], incs=[(sa, 1)])
    cw1 = sa.n
    b.emit("vector", lambda e: e.tensor_scalar(
        gw2[:], gw1[:], -1.0, 1.0, op0=ALU.mult, op1=ALU.add), waits=[(sa, cw1)])
    for t in range(NT):
        b.emit("vector", lambda e, t=t: e.tensor_scalar(
            gtmp[:, t * 8:(t + 1) * 8], geq1[:, t * 8:(t + 1) * 8],
            gw1[:, t:t + 1], None, op0=ALU.mult))
    for t in range(NT):
        b.emit("vector", lambda e, t=t: e.scalar_tensor_tensor(
            comb[:, t * 8:(t + 1) * 8], geq2[:, t * 8:(t + 1) * 8],
            gw2[:, t:t + 1], gtmp[:, t * 8:(t + 1) * 8],
            op0=ALU.mult, op1=ALU.add),
            incs=[(sv, 1)] if t == NT - 1 else [])
    c_comb = sv.n
    tap("comb", comb, [(sv, c_comb)])

    b.emit("vector", lambda e: e.tensor_copy(acc[:], x1T[:]),
           waits=[(sv, c_x1T)], incs=[(sv, 1)])
    c_acc0 = sv.n

    upw_loaded = [0, 0]
    upw_consumed = [c_x2_mm, c_x2_mm]
    dnw_loaded = [0, 0]
    dnw_consumed = [c_x2_mm, c_x2_mm]
    dn_slots = [dnw0, dn2]
    psU_free = [(sv, c_acc0)] * 4
    psDn_free = [c_acc0] * 4
    upslot = [0]
    cb_ready = [0]
    c_prev_dn = [0]

    def up_src(e_, m):
        if e_ < NE:
            return hwup[e_, m]
        return hsgg[m] if m < 12 else hsgu[m - 12]

    def dn_src(e_, m):
        return hwdn[e_, m] if e_ < NE else hsgd[m]

    def emit_expert(ex):
        shared = (ex == NE)
        if not shared:
            crow_done = 0
            for qc in range(4):
                for tt in range(4):
                    t = qc * 4 + tt
                    wl = [(sv, c_comb), (sv, psDn_free[0]), (sg, c_consts)] \
                        if (qc == 0 and tt == 0) else ([(sa, crow_done)] if tt == 0 else [])
                    b.emit("tensor", lambda e, t=t, tt=tt, ex=ex: e.transpose(
                        psA[0][0:1, tt * 128:(tt + 1) * 128],
                        comb[:, t * 8 + ex: t * 8 + ex + 1], ident32[:]),
                        waits=wl, incs=[(sp, 1)] if tt == 3 else [])
                cf = sp.n
                b.emit("scalar", lambda e, qc=qc: e.activation(
                    crow[0:1, qc * 512: qc * 512 + 512], psA[0][0:1, 0:512], AF.Identity),
                    waits=[(sp, cf)], incs=[(sa, 1)])
                crow_done = sa.n
            ca = sa.n
            cbc_ev = 0
            for qc in range(4):
                wl = [(sa, ca)]
                wl.append((sv, psDn_free[1]) if qc == 0 else (sv, cbc_ev))
                b.emit("tensor", lambda e, qc=qc: e.matmul(
                    psA[1][:], ones_rowf[:], crow[0:1, qc * 512: qc * 512 + 512],
                    start=True, stop=True),
                    waits=wl, incs=[(sp, 1)])
                cf = sp.n
                b.emit("vector", lambda e, qc=qc: e.tensor_copy(
                    cbcast[:, qc * 512: qc * 512 + 512], psA[1][:, 0:512]),
                    waits=[(sp, cf)], incs=[(sv, 1)])
                cbc_ev = sv.n
            cb_ready[0] = sv.n
            psDn_free[0] = sv.n
            psDn_free[1] = sv.n

        for half in range(2):
            silu_sa = {}
            first_silu = True
            for mt in range(6):
                for who in range(2):
                    m = (half * 6 + mt) + (0 if who == 0 else 12)
                    slot = upslot[0] % 2
                    upslot[0] += 1
                    b.emit("gpsimd", lambda e, ex=ex, m=m, slot=slot: e.dma_start(
                        upw[slot][:], up_src(ex, m)),
                        waits=[(sp, upw_consumed[slot])], incs=[(sdw, 16)])
                    upw_loaded[slot] = sdw.n
                    banks = (0, 1) if who == 0 else (2, 3)
                    for qc in range(4):
                        bank = banks[qc % 2]
                        wl = [(sdw, upw_loaded[slot]), psU_free[bank], (sv, c_acc0)]
                        for kt in range(8):
                            b.emit("tensor", lambda e, slot=slot, kt=kt, qc=qc, bank=bank: e.matmul(
                                psU[bank], upw[slot][:, kt * 128:(kt + 1) * 128],
                                x1T[:, kt * T + qc * 512: kt * T + qc * 512 + 512],
                                start=(kt == 0), stop=(kt == 7)),
                                waits=wl if kt == 0 else [],
                                incs=[(sp, 1)] if kt == 7 else [])
                        cf = sp.n
                        upw_consumed[slot] = cf
                        if who == 0:
                            swl = [(sp, cf)]
                            if first_silu:
                                swl.append((sp, c_prev_dn[0]))
                                first_silu = False
                            b.emit("scalar", lambda e, mt=mt, qc=qc, bank=bank: e.activation(
                                hbuf[:, mt * T + qc * 512: mt * T + qc * 512 + 512],
                                psU[bank], AF.Silu), waits=swl, incs=[(sa, 1)])
                            silu_sa[(mt, qc)] = sa.n
                            psU_free[bank] = (sa, sa.n)
                        else:
                            if shared:
                                b.emit("vector", lambda e, mt=mt, qc=qc, bank=bank: e.tensor_mul(
                                    hbuf[:, mt * T + qc * 512: mt * T + qc * 512 + 512],
                                    psU[bank],
                                    hbuf[:, mt * T + qc * 512: mt * T + qc * 512 + 512]),
                                    waits=[(sp, cf), (sa, silu_sa[(mt, qc)])],
                                    incs=[(sv, 1)])
                            else:
                                b.emit("vector", lambda e, qc=qc, bank=bank: e.tensor_mul(
                                    utmp[:], psU[bank],
                                    cbcast[:, qc * 512: qc * 512 + 512]),
                                    waits=[(sp, cf), (sv, cb_ready[0])])
                                b.emit("vector", lambda e, mt=mt, qc=qc: e.tensor_mul(
                                    hbuf[:, mt * T + qc * 512: mt * T + qc * 512 + 512],
                                    utmp[:],
                                    hbuf[:, mt * T + qc * 512: mt * T + qc * 512 + 512]),
                                    waits=[(sa, silu_sa[(mt, qc)])],
                                    incs=[(sv, 1)])
                            psU_free[bank] = (sv, sv.n)
            c_h = sv.n

            for mo in range(8):
                slot = mo % 2
                b.emit("gpsimd", lambda e, ex=ex, mo=mo, slot=slot: e.dma_start(
                    dn_slots[slot][:], dn_src(ex, mo)),
                    waits=[(sp, dnw_consumed[slot])], incs=[(sdw, 16)])
                dnw_loaded[slot] = sdw.n
                for qc in range(4):
                    bank = (mo * 4 + qc) % 4
                    wl = [(sdw, dnw_loaded[slot]), (sv, c_h), (sv, psDn_free[bank])]
                    for lt in range(6):
                        kt = half * 6 + lt
                        b.emit("tensor", lambda e, slot=slot, kt=kt, lt=lt, qc=qc, bank=bank: e.matmul(
                            psDn[bank], dn_slots[slot][:, kt * 128:(kt + 1) * 128],
                            hbuf[:, lt * T + qc * 512: lt * T + qc * 512 + 512],
                            start=(lt == 0), stop=(lt == 5)),
                            waits=wl if lt == 0 else [],
                            incs=[(sp, 1)] if lt == 5 else [])
                    cf = sp.n
                    if qc == 3:
                        dnw_consumed[slot] = cf
                    b.emit("vector", lambda e, mo=mo, qc=qc, bank=bank: e.scalar_tensor_tensor(
                        acc[:, mo * T + qc * 512: mo * T + qc * 512 + 512],
                        acc[:, mo * T + qc * 512: mo * T + qc * 512 + 512], 0.0,
                        psDn[bank], op0=ALU.bypass, op1=ALU.add),
                        waits=[(sp, cf)], incs=[(sv, 1)])
                    psDn_free[bank] = sv.n
            c_prev_dn[0] = sp.n

    for ex in range(NE + 1):
        emit_expert(ex)
        if ex == 0:
            tap("hbuf", hbuf, [(sv, sv.n)])
            tap("cbcast", cbcast, [(sv, cb_ready[0])])
    c_ffn_sv = sv.n
    tap("acc", acc, [(sv, c_ffn_sv)])

    tr_evac3 = [0, 0]
    idx = 0
    for t in range(NT):
        for cg in range(2):
            bank = idx % 2
            wl = [(sv, c_ffn_sv)] if idx < 2 else [(sv, tr_evac3[bank])]
            for c4 in range(4):
                c = cg * 4 + c4
                b.emit("tensor", lambda e, t=t, c=c, c4=c4, bank=bank: e.transpose(
                    psP[bank][:, c4 * 128:(c4 + 1) * 128],
                    acc[:, c * T + t * 128: c * T + (t + 1) * 128],
                    ident32[:]), waits=wl if c4 == 0 else [],
                    incs=[(sp, 1)] if c4 == 3 else [])
            cf = sp.n
            b.emit("vector", lambda e, t=t, cg=cg, bank=bank: e.tensor_scalar(
                z_tok[:, t * D + cg * 512: t * D + (cg + 1) * 512], psP[bank][:],
                1.0, 0.0, op0=ALU.mult, op1=ALU.add,
                accum_out=lnz[:, (t * 2 + cg):(t * 2 + cg) + 1]),
                waits=[(sp, cf)], incs=[(sv, 1)])
            tr_evac3[bank] = sv.n
            idx += 1
    c_z = sv.n
    vspace()
    for t in range(NT):
        b.emit("vector", lambda e, t=t: e.reduce_sum(
            sums1[:, t:t + 1], lnz[:, t * 2:(t + 1) * 2], axis=AX.X),
            waits=[(sv, c_z)] if t == 0 else [])
    b.emit("vector", lambda e: e.engine_nop(), incs=[(sv, 1)])
    c_z_sums = sv.n
    for t in range(NT):
        b.emit("scalar", lambda e, t=t: e.activation(
            scr_sq[:], z_tok[:, t * D:(t + 1) * D], AF.Square,
            accum_out=sums2[:, t:t + 1]),
            waits=[(sv, c_z)] if t == 0 else [],
            incs=[(sa, 1)] if t == NT - 1 else [])
    c_z_sq = sa.n
    c_ln3_fin = emit_ln_finish([(sv, c_z_sums), (sa, c_z_sq)])
    for t in range(NT):
        b.emit("scalar", lambda e, t=t: e.activation(
            z_tok[:, t * D:(t + 1) * D], z_tok[:, t * D:(t + 1) * D], AF.Identity,
            scale=rstd[:, t:t + 1], bias=nmr[:, t:t + 1]),
            waits=[(sv, c_ln3_fin)] if t == 0 else [],
            incs=[(sa, 1)] if t == NT - 1 else [])
    c_out_ready = sa.n

    b.emit("sync", lambda e: e.dma_start(hout, z_tok[:]),
           waits=[(sa, c_out_ready)], incs=[(so, 16)])
    tap("z_tok", z_tok, [(sa, c_out_ready)])
    total_so = so.n
    b.emit("gpsimd", lambda e: e.engine_nop(), waits=[(so, total_so)])

    b.finalize(blk)
    est.close()
    return nc


def kernel(L0_attn_X, L0_attn_W_in, L0_attn_b_in, L0_attn_W_out, L0_attn_b_out,
           L0_ffn_gate_w, L0_ffn_w_up, L0_ffn_w_down, L0_ffn_sg_gate,
           L0_ffn_sg_up, L0_ffn_sg_down):
    X = np.asarray(L0_attn_X, dtype=np.float32)
    shared_img = {}
    shared_img["w_in_img"] = _lhsT_slices(np.asarray(L0_attn_W_in, np.float32), 32, 8)
    shared_img["b_in_img"] = np.ascontiguousarray(
        np.asarray(L0_attn_b_in, np.float32).reshape(32, 128).T)
    shared_img["w_out_img"] = _rhs_img(np.ascontiguousarray(
        np.asarray(L0_attn_W_out, np.float32).T))
    shared_img["b_out_row"] = np.ascontiguousarray(
        np.asarray(L0_attn_b_out, np.float32).reshape(1, D))
    shared_img["gate_img"] = _rhs_img(np.ascontiguousarray(
        np.asarray(L0_ffn_gate_w, np.float32).T))
    wup = np.asarray(L0_ffn_w_up, np.float32)
    wdn = np.asarray(L0_ffn_w_down, np.float32)
    shared_img["wup_img"] = np.stack([_lhsT_slices(wup[e], 24, 8) for e in range(NE)])
    shared_img["wdn_img"] = np.stack([_lhsT_slices(wdn[e], 8, 12) for e in range(NE)])
    shared_img["sgg_img"] = _lhsT_slices(np.asarray(L0_ffn_sg_gate, np.float32), 12, 8)
    shared_img["sgu_img"] = _lhsT_slices(np.asarray(L0_ffn_sg_up, np.float32), 12, 8)
    shared_img["sgd_img"] = _lhsT_slices(np.asarray(L0_ffn_sg_down, np.float32), 8, 12)

    nc = build_program()
    in_maps = []
    for c in range(_NUM_CORES):
        d = dict(shared_img)
        d["x_img"] = _img_tok(np.ascontiguousarray(X[c]))
        in_maps.append(d)
    res = run_bass_kernel_spmd(nc, in_maps, core_ids=list(range(_NUM_CORES)))
    out = np.stack([_unimg_tok(res.results[c]["out_img"]) for c in range(_NUM_CORES)])
    if _TAPS:
        kernel._tap_results = [
            {name: res.results[c]["tap_" + name] for (name, _s) in _TAPS}
            for c in range(_NUM_CORES)]
    return out.astype(np.float32)


# version 05
# speedup vs baseline: 1.0154x; 1.0125x over previous
import numpy as np
import ml_dtypes
import concourse.bass as bass
import concourse.mybir as mybir
from concourse.bass_core_rust import BassInstruction
from concourse.bass_utils import run_bass_kernel_spmd

F32 = mybir.dt.float32
F32R = mybir.dt.float32r
BF16 = mybir.dt.bfloat16
AF = mybir.ActivationFunctionType
ALU = mybir.AluOpType
AX = mybir.AxisListType

T = 2048
D = 1024
NT = T // 128
ND = D // 128
H = 16
NE = 8
LN_EPS = 1e-5
SEPS = 1e-4

_TAPS = []
_NUM_CORES = 8


def _img_tok(x2d):
    return np.ascontiguousarray(
        x2d.reshape(NT, 128, D).transpose(1, 0, 2).reshape(128, NT * D))

def _unimg_tok(img):
    return np.ascontiguousarray(
        img.reshape(128, NT, D).transpose(1, 0, 2).reshape(T, D))

def _lhsT_slices(w, n_m, n_k):
    rows, d = w.shape
    assert rows == n_m * 128 and d == n_k * 128
    a = w.reshape(n_m, 128, n_k, 128)
    return np.ascontiguousarray(a.transpose(0, 3, 2, 1).reshape(n_m, 128, n_k * 128))

def _rhs_img(wT):
    d, cols = wT.shape
    nk = d // 128
    a = wT.reshape(nk, 128, cols)
    return np.ascontiguousarray(a.transpose(1, 0, 2).reshape(128, nk * cols))


class Ctr:
    def __init__(self, handle):
        self.h = handle
        self.n = 0

class B:
    def __init__(self, nc):
        self.nc = nc
        self.streams = {e: [] for e in ("tensor", "vector", "scalar", "gpsimd", "sync")}

    def emit(self, eng, fn, waits=(), incs=()):
        for (c, k) in incs:
            c.n += k
        snap = tuple((c.h, v) for (c, v) in waits if v > 0)
        tgt = tuple((c.h, k) for (c, k) in incs)
        def f(e):
            for (h, v) in snap:
                e.wait_ge(h, v)
            inst = fn(e)
            if tgt:
                assert inst is not None
                for (h, k) in tgt:
                    inst = inst.then_inc(h, k)
        self.streams[eng].append(f)

    def finalize(self, block):
        st = self.streams
        @block.sync
        def _(e):
            for f in st["sync"]: f(e)
        @block.gpsimd
        def _(e):
            for f in st["gpsimd"]: f(e)
        @block.vector
        def _(e):
            for f in st["vector"]: f(e)
        @block.scalar
        def _(e):
            for f in st["scalar"]: f(e)
        @block.tensor
        def _(e):
            for f in st["tensor"]: f(e)


def act_swap(nc, s, out, in_, func, bias=0.0, scale=1.0, accum_out=None):
    inst = s.activation(out, in_, AF.Sqrt, bias=bias, scale=scale, accum_out=accum_out)
    i = inst.ins
    clone = mybir.InstActivation(name=i.name, engine=i.engine, ins=list(i.ins),
                                 outs=list(i.outs), func=func)
    clone.sync_info = i.sync_info
    bb = nc.cur_bb.bb
    assert bb.instructions[-1].name == i.name
    bb.instructions[-1] = clone
    return BassInstruction(clone)


def build_program():
    nc = bass.Bass("TRN2", target_bir_lowering=False, detect_race_conditions=False)

    hx = nc.dram_tensor("x_img", [128, NT * D], F32, kind="ExternalInput").ap()
    hwin = nc.dram_tensor("w_in_img", [32, 128, 1024], F32, kind="ExternalInput").ap()
    hbin = nc.dram_tensor("b_in_img", [128, 32], F32, kind="ExternalInput").ap()
    hwout = nc.dram_tensor("w_out_img", [128, 8 * 1024], F32, kind="ExternalInput").ap()
    hbout = nc.dram_tensor("b_out_row", [1, D], F32, kind="ExternalInput").ap()
    hgate = nc.dram_tensor("gate_img", [128, 64], F32, kind="ExternalInput").ap()
    hwup = nc.dram_tensor("wup_img", [NE, 24, 128, 1024], F32, kind="ExternalInput").ap()
    hwdn = nc.dram_tensor("wdn_img", [NE, 8, 128, 1536], F32, kind="ExternalInput").ap()
    hsgg = nc.dram_tensor("sgg_img", [12, 128, 1024], F32, kind="ExternalInput").ap()
    hsgu = nc.dram_tensor("sgu_img", [12, 128, 1024], F32, kind="ExternalInput").ap()
    hsgd = nc.dram_tensor("sgd_img", [8, 128, 1536], F32, kind="ExternalInput").ap()
    hout = nc.dram_tensor("out_img", [128, NT * D], F32, kind="ExternalOutput").ap()
    htap = {}
    for (name, shape) in _TAPS:
        htap[name] = nc.dram_tensor("tap_" + name, list(shape), F32,
                                    kind="ExternalOutput").ap()

    KB = 1024
    ARENA = 200 * KB
    base0, _end = nc.bump_sbuf(ARENA, align_bytes=256)
    A0, B0, C0, D0, S0 = (base0, base0 + 64 * KB, base0 + 128 * KB,
                          base0 + 168 * KB, base0 + 184 * KB)

    def at(name, off, shape, dtype):
        return nc.alloc_sbuf_tensor_at(name, shape, dtype, offset=off).ap()

    X = at("X", A0, [128, NT * D], F32)
    qaug = [at(f"qaug{i}", A0 + i * 4 * KB, [66, T], BF16) for i in range(2)]
    kaug = [at(f"kaug{i}", A0 + 8 * KB + i * 4 * KB, [66, T], BF16) for i in range(2)]
    vT = at("vT", A0 + 16 * KB, [64, T], F32)
    gTb = [at(f"gTb{i}", A0 + 24 * KB + i * 4 * KB, [64, T], BF16) for i in range(2)]
    vprime = [at(f"vp{i}", A0 + 32 * KB + i * 5 * KB, [128, 16 * 65], BF16) for i in range(2)]
    qsq = at("qsq", A0 + 42 * KB, [64, 512], F32)
    rec = [at(f"rec{i}", A0 + 44 * KB + i * 2 * KB, [64, 512], F32) for i in range(2)]
    normt = [at(f"normt{i}", A0 + 48 * KB + i * 2 * KB, [64, 512], F32) for i in range(2)]
    rowt2 = at("rowt2", A0 + 52 * KB, [1, 512], F32)
    qsq2 = at("qsq2", A0 + 54 * KB, [64, 512], F32)
    astg = [at(f"astg{i}", A0 + 56 * KB + i * 2 * KB, [65, 512], F32) for i in range(4)]
    r_tok = at("r_tok", A0, [128, NT * D], F32)
    x1T = at("x1T", A0, [128, ND * T], F32R)
    z_tok = at("z_tok", A0, [128, NT * D], F32)

    xn_tok = at("xn_tok", B0, [128, NT * D], BF16)
    xnT = at("xnT", B0 + 32 * KB, [128, ND * T], BF16)
    u_s = [at(f"u{i}", B0 + i * 2 * KB, [128, 1024], BF16) for i in range(4)]
    uu_s = [at(f"uu{i}", B0 + 8 * KB + i * 2 * KB, [128, 1024], BF16) for i in range(4)]
    wring = [at(f"wr{i}", B0 + 16 * KB + i * 2 * KB, [128, 1024], BF16) for i in range(4)]
    X_re = at("X_re", B0, [128, NT * D], F32)
    x1_tok = at("x1_tok", B0, [128, NT * D], F32)
    acc = at("acc", B0, [128, ND * T], F32)

    OT = at("OT", C0, [128, 9 * T], BF16)
    hbuf = at("hbuf", C0, [128, 6 * T], BF16)
    cbcast = at("cbcast", C0 + 24 * KB, [128, T], F32)
    crow = at("crow", C0 + 32 * KB, [1, T], F32)

    winw = [at(f"winw{i}", D0 + i * 2 * KB, [128, 1024], BF16) for i in range(8)]
    woutb = at("woutb", D0, [128, 8 * 1024], BF16)
    upw = [at(f"upw{i}", D0 + i * 4 * KB, [128, 1024], F32R) for i in range(2)]
    dnw0 = at("dnw0", D0 + 8 * KB, [128, 1536], BF16)
    dn2 = at("dn2", D0 + 11 * KB, [128, 1536], BF16)

    o = S0
    ident32 = at("ident32", o, [128, 128], F32); o += 512
    identbf = at("identbf", o, [128, 128], BF16); o += 256
    ones_col = at("ones_col", o, [128, 1], F32); o += 32
    q2lhs = at("q2lhs", o, [64, 2], F32); o += 32
    k2lhs = at("k2lhs", o, [64, 2], F32); o += 32
    bias_q = at("bias_q", o, [2, 1], F32); o += 32
    bias_k = at("bias_k", o, [2, 1], F32); o += 32
    eps_col = at("eps_col", o, [128, 1], F32); o += 32
    seps1 = at("seps1", o, [1, 1], F32); o += 32
    binsb = at("binsb", o, [128, 32], F32); o += 128
    binsb_n2 = at("binsb_n2", o, [128, 32], F32); o += 128
    ones_rowf = at("ones_rowf", o, [1, 128], F32); o += 512
    boutr = at("boutr", o, [1, D], BF16); o += 2 * KB
    gatesb = at("gatesb", o, [128, 64], F32R); o += 256
    iotar = at("iotar", o, [128, 8], F32); o += 32
    sums1 = at("sums1", o, [128, NT], F32); o += 64
    sums1b = at("sums1b", o, [128, NT], F32); o += 64
    sums2 = at("sums2", o, [128, NT], F32); o += 64
    mu = at("mu", o, [128, NT], F32); o += 64
    var_ = at("var", o, [128, NT], F32); o += 64
    rstd = at("rstd", o, [128, NT], F32); o += 64
    nmr = at("nmr", o, [128, NT], F32); o += 64
    scr_sq = at("scr_sq", o, [128, D], F32); o += 4 * KB
    logits = at("logits", o, [128, NT * 8], F32); o += 512
    comb = at("comb", o, [128, NT * 8], F32); o += 512
    gm1 = at("gm1", o, [128, NT], F32); o += 64
    gm2 = at("gm2", o, [128, NT], F32); o += 64
    gmi = at("gmi", o, [128, NT], F32); o += 64
    gmi2 = at("gmi2", o, [128, NT], F32); o += 64
    gw1 = at("gw1", o, [128, NT], F32); o += 64
    gw2 = at("gw2", o, [128, NT], F32); o += 64
    gdm = at("gdm", o, [128, NT], F32); o += 64
    geqA = at("geqA", o, [128, 128], F32); o += 512
    geq1 = at("geq1", o, [128, 128], F32); o += 512
    geq2 = at("geq2", o, [128, 128], F32); o += 512
    grow2 = at("grow2", o, [128, 128], F32); o += 512
    gtmp = at("gtmp", o, [128, 128], F32); o += 512
    utmp = at("utmp", o, [128, 512], F32); o += 2 * KB
    lnz = at("lnz", o, [128, NT * ND], F32); o += 512
    assert o - S0 <= 16 * KB, f"small region overflow: {o - S0}"

    psS = [nc.alloc_psum_tensor(f"psS{i}", [128, 1024], F32).ap() for i in range(2)]
    psA = [nc.alloc_psum_tensor(f"psA{i}", [128, 512], F32).ap() for i in range(2)]
    psP = [nc.alloc_psum_tensor(f"psP{i}", [128, 512], F32).ap() for i in range(2)]
    psPbf = [p.bitcast(BF16) for p in psP]
    psU = [psS[0][:, 0:512], psS[0][:, 512:1024], psS[1][:, 0:512], psS[1][:, 512:1024]]
    psDn = [psA[0], psA[1], psP[0], psP[1]]

    import contextlib
    est = contextlib.ExitStack()
    sems = {}
    for name in ("sdin", "sdw", "sp", "sa", "sv", "sg", "so"):
        sems[name] = Ctr(est.enter_context(nc.semaphore(name)))
    sdin, sdw, sp, sa, sv, sg, so = (sems[k] for k in
                                     ("sdin", "sdw", "sp", "sa", "sv", "sg", "so"))
    blk = est.enter_context(nc.Block())
    b = B(nc)
    tap_names = {name: shape for (name, shape) in _TAPS}

    def tap(name, src, waits):
        if name not in tap_names:
            return
        shape = tap_names[name]
        b.emit("gpsimd", lambda e, name=name, src=src, shape=shape: e.dma_start(
            htap[name], src[0:shape[0], 0:shape[1]]),
            waits=waits, incs=[(so, 16)])

    b.emit("sync", lambda e: e.dma_start(X[:], hx), incs=[(sdin, 16)])
    c_X = sdin.n
    b.emit("sync", lambda e: e.dma_start(binsb[:], hbin), incs=[(sdin, 16)])
    c_bin = sdin.n
    tap("X", X, [(sdin, c_X)])
    b.emit("gpsimd", lambda e: e.dma_start(gatesb[:], hgate), incs=[(sdw, 16)])
    c_gate = sdw.n
    b.emit("gpsimd", lambda e: e.dma_start(boutr[:], hbout), incs=[(sdw, 16)])
    c_bout = sdw.n

    def _gsetup(e):
        e.memset(ident32[:], 0.0)
        nc.gpsimd.affine_select(out=ident32[:], in_=ident32[:], compare_op=ALU.not_equal,
                                fill=1.0, base=0, pattern=[[-1, 128]], channel_multiplier=1)
        e.memset(identbf[:], 0.0)
        nc.gpsimd.affine_select(out=identbf[:], in_=identbf[:], compare_op=ALU.not_equal,
                                fill=1.0, base=0, pattern=[[-1, 128]], channel_multiplier=1)
        e.memset(ones_col[:], 1.0)
        e.memset(ones_rowf[:], 1.0)
        e.memset(q2lhs[:], 0.0)
        e.memset(q2lhs[:, 0:1], 0.25)
        e.memset(k2lhs[:], 0.0)
        e.memset(k2lhs[:, 1:2], 1.0)
        e.iota(bias_q[:], pattern=[[0, 1]], base=0, channel_multiplier=1,
               allow_small_or_imprecise_dtypes=True)
        e.iota(bias_k[:], pattern=[[0, 1]], base=1, channel_multiplier=-1,
               allow_small_or_imprecise_dtypes=True)
        e.memset(eps_col[:], LN_EPS)
        e.memset(seps1[:], SEPS)
        e.memset(OT[0:1, 8 * T:9 * T], 1.0)
        return nc.gpsimd.iota(iotar[:], pattern=[[-1, 8]], base=8, channel_multiplier=0,
                              allow_small_or_imprecise_dtypes=True)
    b.emit("gpsimd", _gsetup, incs=[(sg, 1)])
    c_consts = sg.n

    win_loaded = [0] * 32
    for m in range(8):
        b.emit("gpsimd", lambda e, m=m: e.dma_start(winw[m % 8][:], hwin[m]),
               incs=[(sdw, 16)])
        win_loaded[m] = sdw.n

    b.emit("vector", lambda e: e.tensor_scalar_mul(binsb_n2[:], binsb[:], -2.0),
           waits=[(sdin, c_bin)], incs=[(sv, 1)])

    def vspace():
        b.emit("vector", lambda e: e.tensor_scalar_mul(scr_sq[:, 0:512],
                                                       scr_sq[:, 0:512], 1.0))

    def emit_ln_finish(waits):
        b.emit("vector", lambda e: e.tensor_scalar_mul(mu[:], sums1[:], 1.0 / D),
               waits=waits)
        b.emit("vector", lambda e: e.tensor_scalar_mul(var_[:], sums2[:], 1.0 / D))
        vspace()
        b.emit("vector", lambda e: e.tensor_mul(nmr[:], mu[:], mu[:]))
        vspace()
        b.emit("vector", lambda e: e.tensor_sub(var_[:], var_[:], nmr[:]), incs=[(sv, 1)])
        cv = sv.n
        b.emit("scalar", lambda e: act_swap(nc, e, rstd[:], var_[:], AF.Rsqrt,
                                            bias=eps_col[:, 0:1]),
               waits=[(sv, cv), (sg, c_consts)], incs=[(sa, 1)])
        ca = sa.n
        b.emit("vector", lambda e: e.scalar_tensor_tensor(
            nmr[:], mu[:], -1.0, rstd[:], op0=ALU.mult, op1=ALU.mult),
            waits=[(sa, ca)], incs=[(sv, 1)])
        return sv.n

    for t in range(NT):
        b.emit("vector", lambda e, t=t: e.reduce_sum(
            sums1[:, t:t + 1], X[:, t * D:(t + 1) * D], axis=AX.X),
            waits=[(sdin, c_X)] if t == 0 else [])
    b.emit("vector", lambda e: e.engine_nop(), incs=[(sv, 1)])
    c_ln1_sums = sv.n
    for t in range(NT):
        b.emit("scalar", lambda e, t=t: e.activation(
            scr_sq[:], X[:, t * D:(t + 1) * D], AF.Square,
            accum_out=sums2[:, t:t + 1]),
            waits=[(sdin, c_X)] if t == 0 else [],
            incs=[(sa, 1)] if t == NT - 1 else [])
    c_ln1_sq = sa.n
    c_ln1_fin = emit_ln_finish([(sv, c_ln1_sums), (sa, c_ln1_sq)])
    tap("mu1", mu, [(sv, c_ln1_fin)])
    tap("rstd1", rstd, [(sv, c_ln1_fin)])
    tap("s1", sums1, [(sv, c_ln1_sums)])
    tap("s2", sums2, [(sa, c_ln1_sq)])
    for t in range(NT):
        b.emit("scalar", lambda e, t=t: e.activation(
            xn_tok[:, t * D:(t + 1) * D], X[:, t * D:(t + 1) * D], AF.Identity,
            scale=rstd[:, t:t + 1], bias=nmr[:, t:t + 1]),
            waits=[(sv, c_ln1_fin)] if t == 0 else [],
            incs=[(sa, 1)] if t == NT - 1 else [])
    c_ln1_apply = sa.n
    tap("xn_tok", xn_tok, [(sa, c_ln1_apply)])

    def _vp_ones(e):
        last = None
        for i in range(2):
            for kt in range(NT):
                last = e.activation(vprime[i][:, kt * 65 + 64: kt * 65 + 65],
                                    ones_col[:], AF.Identity)
        return last
    b.emit("scalar", _vp_ones, waits=[(sg, c_consts)], incs=[(sa, 1)])
    c_vpones = sa.n

    tr_evac = [0, 0]
    for t in range(NT):
        bank = t % 2
        wl = [(sa, c_ln1_apply), (sg, c_consts)] if t < 2 else [(sv, tr_evac[bank])]
        for c in range(ND):
            b.emit("tensor", lambda e, t=t, c=c, bank=bank: e.transpose(
                psPbf[bank][:, c * 128:(c + 1) * 128],
                xn_tok[:, t * D + c * 128: t * D + (c + 1) * 128],
                identbf[:]), waits=wl if c == 0 else [],
                incs=[(sp, 1)] if c == ND - 1 else [])
        cf = sp.n
        dst = xnT.rearrange("p (c y) -> p c y", c=ND)[:, :, t * 128:(t + 1) * 128]
        b.emit("vector", lambda e, dst=dst, bank=bank: e.tensor_copy(
            dst, psPbf[bank][:].rearrange("p (c x) -> p c x", c=ND)),
            waits=[(sp, cf)], incs=[(sv, 1)])
        tr_evac[bank] = sv.n
    c_xnT = sv.n
    tap("xnT", xnT, [(sv, c_xnT)])

    pP_state = [(sv, tr_evac[0]), (sv, tr_evac[1])]
    prep_done = [None] * H
    main_sv = [0] * H
    psA_free = [0, 0]
    psS_free = [0, 0]
    psS0_borrow = [0]
    wring_free = [0, 0, 0, 0]
    uslot_free = [0, 0, 0, 0]
    last_phead_mm = [0]
    last_q2mm = [0]
    wready = {}

    def emit_prep(h):
        thunks = []
        par = h % 2
        ready = {}
        qsq_mm = [0, 0]
        mi0_evac = [0]

        def pproj(mi, qc):
            c_prev_main = main_sv[h - 2] if h >= 2 else 0
            m = 2 * h + mi
            slot = m % 8
            bank = mi
            wl = [(sdw, win_loaded[m]), (sv, c_xnT), pP_state[bank]]
            for kt in range(8):
                b.emit("tensor", lambda e, qc=qc, kt=kt, slot=slot, bank=bank: e.matmul(
                    psP[bank][:],
                    winw[slot][:, kt * 128:(kt + 1) * 128],
                    xnT[:, kt * T + qc * 512: kt * T + qc * 512 + 512],
                    start=(kt == 0), stop=(kt == 7)),
                    waits=wl if kt == 0 else [],
                    incs=[(sp, 1)] if kt == 7 else [])
            cfill = sp.n
            last_phead_mm[0] = cfill
            if mi == 0:
                b.emit("scalar", lambda e, par=par, qc=qc, m=m: e.activation(
                    qaug[par][0:64, qc * 512: qc * 512 + 512], psP[0][0:64, :],
                    AF.Identity, scale=-2.0, bias=binsb_n2[0:64, m:m + 1]),
                    waits=[(sp, cfill), (sv, c_prev_main)])
                b.emit("scalar", lambda e, par=par, qc=qc, m=m: e.activation(
                    kaug[par][0:64, qc * 512: qc * 512 + 512], psP[0][64:128, :],
                    AF.Identity, bias=binsb[64:128, m:m + 1]), incs=[(sa, 1)])
                mi0_evac[0] = sa.n
            else:
                b.emit("scalar", lambda e, qc=qc, m=m: e.activation(
                    vT[0:64, qc * 512: qc * 512 + 512], psP[1][0:64, :],
                    AF.Identity, bias=binsb[0:64, m:m + 1]),
                    waits=[(sp, cfill), (sv, c_prev_main)])
                b.emit("scalar", lambda e, par=par, qc=qc, m=m: e.activation(
                    gTb[par][0:64, qc * 512: qc * 512 + 512], psP[1][64:128, :],
                    AF.Identity, bias=binsb[64:128, m:m + 1]), incs=[(sa, 1)])
            pP_state[bank] = (sa, sa.n)
            if qc == 3 and m + 8 < 32:
                cm = sp.n
                b.emit("gpsimd", lambda e, m=m, slot=slot: e.dma_start(
                    winw[slot][:], hwin[m + 8]), waits=[(sp, cm)], incs=[(sdw, 16)])
                win_loaded[m + 8] = sdw.n

        def q2k2(which, qc):
            src = qaug[par] if which == 0 else kaug[par]
            lhs = q2lhs if which == 0 else k2lhs
            bias = bias_q if which == 0 else bias_k
            qb = (which * 4 + qc) % 2
            qbuf = qsq if qb == 0 else qsq2
            b.emit("vector", lambda e, src=src, qc=qc, qbuf=qbuf: e.tensor_mul(
                qbuf[:], src[0:64, qc * 512: qc * 512 + 512],
                src[0:64, qc * 512: qc * 512 + 512]),
                waits=[(sa, mi0_evac[0]), (sp, qsq_mm[qb])],
                incs=[(sv, 1)])
            cs = sv.n
            b.emit("tensor", lambda e, lhs=lhs, qbuf=qbuf: e.matmul(
                psP[0][0:2, :], lhs[:], qbuf[:], start=True, stop=True),
                waits=[(sv, cs), pP_state[0], (sg, c_consts)], incs=[(sp, 1)])
            qsq_mm[qb] = sp.n
            last_q2mm[0] = sp.n
            b.emit("scalar", lambda e, src=src, qc=qc, bias=bias: e.activation(
                src[64:66, qc * 512: qc * 512 + 512], psP[0][0:2, :], AF.Identity,
                bias=bias[:, 0:1]), waits=[(sp, last_q2mm[0])], incs=[(sa, 1)])
            pP_state[0] = (sa, sa.n)

        def vtrans(grp):
            wl0 = [pP_state[1], (sa, c_vpones)]
            for k8 in range(8):
                kt = grp * 8 + k8
                b.emit("tensor", lambda e, kt=kt, k8=k8: e.transpose(
                    psP[1][:, k8 * 64:(k8 + 1) * 64],
                    vT[0:64, kt * 128:(kt + 1) * 128], ident32[0:64, 0:64]),
                    waits=wl0 if k8 == 0 else [],
                    incs=[(sp, 1)] if k8 == 7 else [])
            cf = sp.n
            dst = vprime[par].rearrange("p (k y) -> p k y", k=16)[
                :, grp * 8:(grp + 1) * 8, 0:64]
            b.emit("vector", lambda e, dst=dst: e.tensor_copy(
                dst, psP[1][:].rearrange("p (k y) -> p k y", k=8)),
                waits=[(sp, cf)], incs=[(sv, 1)])
            pP_state[1] = (sv, sv.n)

        def finalize():
            ready["aug"] = sa.n
            ready["vp"] = sv.n
            ready["g"] = sa.n
            prep_done[h] = ready
            if h == 0:
                tap("qaug0", qaug[0], [(sa, ready["aug"])])
                tap("kaug0", kaug[0], [(sa, ready["aug"])])
                tap("vp0", vprime[0], [(sv, ready["vp"])])

        for qc in range(4):
            thunks.append(lambda qc=qc: pproj(0, qc))
        for which in range(2):
            for qc in range(4):
                thunks.append(lambda which=which, qc=qc: q2k2(which, qc))
        for qc in range(4):
            thunks.append(lambda qc=qc: pproj(1, qc))
        for grp in range(2):
            thunks.append(lambda grp=grp: vtrans(grp))
        thunks.append(finalize)
        return thunks

    LAG = 3

    def emit_S_EW(h, g):
        par = h % 2
        rd = prep_done[h]
        qc2, kt = divmod(g, NT)
        i = g % 2
        slot2 = g % 4
        for half in range(2):
            wl = []
            if half == 0:
                wl = [(sa, rd["aug"]), (sa, psS_free[i])]
                if i == 0:
                    wl.append((sa, psS0_borrow[0]))
            b.emit("tensor", lambda e, par=par, kt=kt, qc2=qc2, half=half, i=i: e.matmul(
                psS[i][:, half * 512:(half + 1) * 512],
                kaug[par][:, kt * 128:(kt + 1) * 128],
                qaug[par][:, qc2 * 1024 + half * 512: qc2 * 1024 + (half + 1) * 512],
                start=True, stop=True),
                waits=wl, incs=[(sp, 1)] if half == 1 else [])
        cS = sp.n
        b.emit("scalar", lambda e, i=i, slot2=slot2: act_swap(
            nc, e, u_s[slot2][:], psS[i][:], AF.Rsqrt),
            waits=[(sp, cS), (sv, uslot_free[slot2])], incs=[(sa, 1)])
        psS_free[i] = sa.n
        cu = sa.n
        if g % 2 == 0:
            b.emit("scalar", lambda e, slot2=slot2: e.activation(
                uu_s[slot2][:], u_s[slot2][:], AF.Square), incs=[(sa, 1)])
            cuu = (sa, sa.n)
        else:
            b.emit("vector", lambda e, slot2=slot2: e.tensor_mul(
                uu_s[slot2][:], u_s[slot2][:], u_s[slot2][:]),
                waits=[(sa, cu)], incs=[(sv, 1)])
            cuu = (sv, sv.n)
        b.emit("vector", lambda e, slot2=slot2: e.tensor_scalar(
            u_s[slot2][:], u_s[slot2][:], -2.0 * SEPS, 1.0,
            op0=ALU.mult, op1=ALU.add), waits=[cuu])
        b.emit("vector", lambda e, slot2=slot2, wslot=g % 4: e.tensor_mul(
            wring[wslot][:], uu_s[slot2][:], u_s[slot2][:]),
            waits=[(sp, wring_free[g % 4])], incs=[(sv, 1)])
        uslot_free[slot2] = sv.n
        wready[g] = sv.n

    def emit_A(h, g):
        par = h % 2
        rd = prep_done[h]
        qc2, kt = divmod(g, NT)
        wslot = g % 4
        for j in range(2):
            wl = [(sv, wready[g])]
            if kt == 0:
                wl += [(sv, rd["vp"]), (sv, psA_free[j])]
            b.emit("tensor", lambda e, kt=kt, j=j, par=par, wslot=wslot: e.matmul(
                psA[j][0:65, :], vprime[par][:, kt * 65: kt * 65 + 65],
                wring[wslot][:, j * 512:(j + 1) * 512],
                start=(kt == 0), stop=(kt == NT - 1)),
                waits=wl, incs=[(sp, 1)] if j == 1 else [])
        wring_free[wslot] = sp.n

    def emit_norm(h, qc2):
        par = h % 2
        rd = prep_done[h]
        cA = sp.n
        for j in range(2):
            qc = qc2 * 2 + j
            stg = astg[qc2 * 2 + j]
            b.emit("vector", lambda e, j=j, stg=stg: e.tensor_copy(
                stg[:], psA[j][0:65, :]), waits=[(sp, cA)], incs=[(sv, 1)])
            psA_free[j] = sv.n
            cst = sv.n
            b.emit("scalar", lambda e, stg=stg: e.activation(
                rowt2[:], stg[64:65, :], AF.Square, bias=seps1[:, 0:1]),
                waits=[(sv, cst)], incs=[(sa, 1)])
            crr = sa.n
            b.emit("tensor", lambda e: e.matmul(
                psS[0][0:64, 0:512], ones_rowf[0:1, 0:64], rowt2[:],
                start=True, stop=True),
                waits=[(sa, crr), (sa, psS_free[0])], incs=[(sp, 1)])
            cb = sp.n
            b.emit("scalar", lambda e, j=j: act_swap(
                nc, e, rec[j][:], psS[0][0:64, 0:512], AF.Rsqrt),
                waits=[(sp, cb)], incs=[(sa, 1)])
            psS0_borrow[0] = sa.n
            cre = sa.n
            b.emit("vector", lambda e, j=j, stg=stg: e.tensor_mul(
                normt[j][:], stg[0:64, :], rec[j][:]),
                waits=[(sa, cre)])
            b.emit("vector", lambda e, j=j, qc=qc, par=par, h=h: e.tensor_mul(
                OT[(h % 2) * 64:(h % 2) * 64 + 64,
                   (h // 2) * T + qc * 512:(h // 2) * T + qc * 512 + 512],
                normt[j][:], gTb[par][0:64, qc * 512: qc * 512 + 512]),
                waits=[(sa, rd["g"])], incs=[(sv, 1)])

    def emit_main(h, feed=()):
        feed = list(feed)
        wready.clear()
        for g in range(2 * NT):
            emit_S_EW(h, g)
            if feed and g >= 2:
                feed.pop(0)()
            if g >= LAG:
                emit_A(h, g - LAG)
                if (g - LAG) % NT == NT - 1:
                    emit_norm(h, (g - LAG) // NT)
        for g in range(2 * NT - LAG, 2 * NT):
            emit_A(h, g)
            if g % NT == NT - 1:
                emit_norm(h, g // NT)
        while feed:
            feed.pop(0)()
        main_sv[h] = sv.n

    for t in emit_prep(0):
        t()
    for h in range(H):
        feed = emit_prep(h + 1) if h + 1 < H else []
        emit_main(h, feed)
    c_heads_sv = sv.n
    c_heads_sa = sa.n
    tap("OT", OT, [(sv, c_heads_sv), (sg, c_consts)])

    b.emit("gpsimd", lambda e: e.dma_start(woutb[:], hwout),
           waits=[(sp, last_phead_mm[0])], incs=[(sdw, 16)])
    c_wout = sdw.n
    b.emit("sync", lambda e: e.dma_start(X_re[:], hx),
           waits=[(sv, c_heads_sv), (sa, c_heads_sa)], incs=[(sdin, 16)])
    c_Xre = sdin.n

    x2banks = [psS[0][:, 0:512], psS[0][:, 512:1024], psS[1][:, 0:512], psS[1][:, 512:1024]]
    x2_free = [c_heads_sv] * 4
    idx = 0
    for t in range(NT):
        for c2 in range(2):
            bank = idx % 4
            wl = [(sv, x2_free[bank]), (sdw, c_wout)]
            if idx == 0:
                wl += [(sa, c_heads_sa), (sg, c_consts)]
            for kt in range(8):
                b.emit("tensor", lambda e, t=t, c2=c2, kt=kt, bank=bank: e.matmul(
                    x2banks[bank], OT[:, kt * T + t * 128: kt * T + (t + 1) * 128],
                    woutb[:, kt * 1024 + c2 * 512: kt * 1024 + c2 * 512 + 512],
                    start=(kt == 0), stop=False),
                    waits=wl if kt == 0 else [])
            b.emit("tensor", lambda e, t=t, c2=c2, bank=bank: e.matmul(
                x2banks[bank], OT[0:1, 8 * T + t * 128: 8 * T + (t + 1) * 128],
                boutr[0:1, c2 * 512: c2 * 512 + 512],
                start=False, stop=True),
                waits=[(sdw, c_bout)] if idx == 0 else [], incs=[(sp, 1)])
            cf = sp.n
            sumcol = sums1 if c2 == 0 else sums1b
            b.emit("vector", lambda e, t=t, c2=c2, bank=bank, sumcol=sumcol: e.scalar_tensor_tensor(
                r_tok[:, t * D + c2 * 512: t * D + c2 * 512 + 512],
                X_re[:, t * D + c2 * 512: t * D + c2 * 512 + 512], 2.0,
                x2banks[bank], op0=ALU.mult, op1=ALU.add,
                accum_out=sumcol[:, t:t + 1]),
                waits=[(sp, cf)] + ([(sdin, c_Xre)] if idx == 0 else []),
                incs=[(sv, 1)])
            x2_free[bank] = sv.n
            idx += 1
    c_x2_mm = sp.n
    vspace()
    b.emit("vector", lambda e: e.tensor_add(sums1[:], sums1[:], sums1b[:]), incs=[(sv, 1)])
    c_r_done = sv.n
    tap("r_tok", r_tok, [(sv, c_r_done)])
    for t in range(NT):
        b.emit("scalar", lambda e, t=t: e.activation(
            scr_sq[:], r_tok[:, t * D:(t + 1) * D], AF.Square,
            accum_out=sums2[:, t:t + 1]),
            waits=[(sv, c_r_done)] if t == 0 else [],
            incs=[(sa, 1)] if t == NT - 1 else [])
    c_r_sq = sa.n
    c_ln2_fin = emit_ln_finish([(sv, c_r_done), (sa, c_r_sq)])
    for t in range(NT):
        b.emit("scalar", lambda e, t=t: e.activation(
            x1_tok[:, t * D:(t + 1) * D], r_tok[:, t * D:(t + 1) * D], AF.Identity,
            scale=rstd[:, t:t + 1], bias=nmr[:, t:t + 1]),
            waits=[(sv, c_ln2_fin), (sdin, c_Xre)] if t == 0 else [],
            incs=[(sa, 1)] if t == NT - 1 else [])
    c_x1_apply = sa.n
    tap("x1_tok", x1_tok, [(sa, c_x1_apply)])

    tr_evac2 = [0, 0]
    idx = 0
    x1T_4d = x1T.rearrange("p (c y) -> p c y", c=ND)
    for t in range(NT):
        for cg in range(2):
            bank = idx % 2
            wl = [(sa, c_x1_apply)] if idx < 2 else [(sv, tr_evac2[bank])]
            for c4 in range(4):
                c = cg * 4 + c4
                b.emit("tensor", lambda e, t=t, c=c, c4=c4, bank=bank: e.transpose(
                    psP[bank][:, c4 * 128:(c4 + 1) * 128],
                    x1_tok[:, t * D + c * 128: t * D + (c + 1) * 128],
                    ident32[:]), waits=wl if c4 == 0 else [],
                    incs=[(sp, 1)] if c4 == 3 else [])
            cf = sp.n
            dst = x1T_4d[:, cg * 4:(cg + 1) * 4, t * 128:(t + 1) * 128]
            b.emit("vector", lambda e, dst=dst, bank=bank: e.tensor_copy(
                dst, psP[bank][:].rearrange("p (c x) -> p c x", c=4)),
                waits=[(sp, cf)], incs=[(sv, 1)])
            tr_evac2[bank] = sv.n
            idx += 1
    c_x1T = sv.n
    tap("x1T", x1T, [(sv, c_x1T)])

    glog_free = [c_x1T, c_x1T]
    for t in range(NT):
        bank = psA[t % 2]
        wl = [(sv, glog_free[t % 2]), (sdw, c_gate)]
        for kt in range(8):
            b.emit("tensor", lambda e, t=t, kt=kt, bank=bank: e.matmul(
                bank[:, 0:8], x1T[:, kt * T + t * 128: kt * T + (t + 1) * 128],
                gatesb[:, kt * 8:(kt + 1) * 8],
                start=(kt == 0), stop=(kt == 7)),
                waits=wl if kt == 0 else [], incs=[(sp, 1)] if kt == 7 else [])
        cf = sp.n
        b.emit("vector", lambda e, t=t, bank=bank: e.tensor_copy(
            logits[:, t * 8:(t + 1) * 8], bank[:, 0:8]), waits=[(sp, cf)],
            incs=[(sv, 1)])
        glog_free[t % 2] = sv.n
    c_logits = sv.n
    tap("logits", logits, [(sv, c_logits)])

    vspace()
    for t in range(NT):
        b.emit("vector", lambda e, t=t: e.reduce_max(
            gm1[:, t:t + 1], logits[:, t * 8:(t + 1) * 8], axis=AX.X),
            waits=[(sv, c_logits), (sg, c_consts)] if t == 0 else [])
    for t in range(NT):
        b.emit("vector", lambda e, t=t: e.tensor_scalar(
            geqA[:, t * 8:(t + 1) * 8], logits[:, t * 8:(t + 1) * 8],
            gm1[:, t:t + 1], None, op0=ALU.is_equal))
    for t in range(NT):
        b.emit("vector", lambda e, t=t: e.scalar_tensor_tensor(
            grow2[:, t * 8:(t + 1) * 8], geqA[:, t * 8:(t + 1) * 8], -1e30,
            logits[:, t * 8:(t + 1) * 8], op0=ALU.mult, op1=ALU.add))
    for t in range(NT):
        b.emit("vector", lambda e, t=t: e.reduce_max(
            gm2[:, t:t + 1], grow2[:, t * 8:(t + 1) * 8], axis=AX.X))
    for t in range(NT):
        b.emit("vector", lambda e, t=t: e.tensor_mul(
            gtmp[:, t * 8:(t + 1) * 8], geqA[:, t * 8:(t + 1) * 8], iotar[:]))
    for t in range(NT):
        b.emit("vector", lambda e, t=t: e.reduce_max(
            gmi[:, t:t + 1], gtmp[:, t * 8:(t + 1) * 8], axis=AX.X))
    for t in range(NT):
        b.emit("vector", lambda e, t=t: e.tensor_scalar(
            geq1[:, t * 8:(t + 1) * 8], iotar[:], gmi[:, t:t + 1], None,
            op0=ALU.is_equal))
    for t in range(NT):
        b.emit("vector", lambda e, t=t: e.tensor_scalar(
            geq2[:, t * 8:(t + 1) * 8], grow2[:, t * 8:(t + 1) * 8],
            gm2[:, t:t + 1], None, op0=ALU.is_equal))
    for t in range(NT):
        b.emit("vector", lambda e, t=t: e.tensor_mul(
            gtmp[:, t * 8:(t + 1) * 8], geq2[:, t * 8:(t + 1) * 8], iotar[:]))
    for t in range(NT):
        b.emit("vector", lambda e, t=t: e.reduce_max(
            gmi2[:, t:t + 1], gtmp[:, t * 8:(t + 1) * 8], axis=AX.X))
    for t in range(NT):
        b.emit("vector", lambda e, t=t: e.tensor_scalar(
            geq2[:, t * 8:(t + 1) * 8], iotar[:], gmi2[:, t:t + 1], None,
            op0=ALU.is_equal))
    vspace()
    b.emit("vector", lambda e: e.tensor_sub(gdm[:], gm1[:], gm2[:]), incs=[(sv, 1)])
    cdm = sv.n
    b.emit("scalar", lambda e: e.activation(gw1[:], gdm[:], AF.Sigmoid),
           waits=[(sv, cdm)], incs=[(sa, 1)])
    cw1 = sa.n
    b.emit("vector", lambda e: e.tensor_scalar(
        gw2[:], gw1[:], -1.0, 1.0, op0=ALU.mult, op1=ALU.add), waits=[(sa, cw1)])
    for t in range(NT):
        b.emit("vector", lambda e, t=t: e.tensor_scalar(
            gtmp[:, t * 8:(t + 1) * 8], geq1[:, t * 8:(t + 1) * 8],
            gw1[:, t:t + 1], None, op0=ALU.mult))
    for t in range(NT):
        b.emit("vector", lambda e, t=t: e.scalar_tensor_tensor(
            comb[:, t * 8:(t + 1) * 8], geq2[:, t * 8:(t + 1) * 8],
            gw2[:, t:t + 1], gtmp[:, t * 8:(t + 1) * 8],
            op0=ALU.mult, op1=ALU.add),
            incs=[(sv, 1)] if t == NT - 1 else [])
    c_comb = sv.n
    tap("comb", comb, [(sv, c_comb)])

    b.emit("vector", lambda e: e.tensor_copy(acc[:], x1T[:]),
           waits=[(sv, c_x1T)], incs=[(sv, 1)])
    c_acc0 = sv.n

    upw_loaded = [0, 0]
    upw_consumed = [c_x2_mm, c_x2_mm]
    dnw_loaded = [0, 0]
    dnw_consumed = [c_x2_mm, c_x2_mm]
    dn_slots = [dnw0, dn2]
    psU_free = [(sv, c_acc0)] * 4
    psDn_free = [c_acc0] * 4
    upslot = [0]
    cb_ready = [0]
    c_prev_dn = [0]

    def up_src(e_, m):
        if e_ < NE:
            return hwup[e_, m]
        return hsgg[m] if m < 12 else hsgu[m - 12]

    def dn_src(e_, m):
        return hwdn[e_, m] if e_ < NE else hsgd[m]

    def emit_expert(ex):
        shared = (ex == NE)
        if not shared:
            crow_done = 0
            for qc in range(4):
                for tt in range(4):
                    t = qc * 4 + tt
                    wl = [(sv, c_comb), (sv, psDn_free[0]), (sg, c_consts)] \
                        if (qc == 0 and tt == 0) else ([(sa, crow_done)] if tt == 0 else [])
                    b.emit("tensor", lambda e, t=t, tt=tt, ex=ex: e.transpose(
                        psA[0][0:1, tt * 128:(tt + 1) * 128],
                        comb[:, t * 8 + ex: t * 8 + ex + 1], ident32[:]),
                        waits=wl, incs=[(sp, 1)] if tt == 3 else [])
                cf = sp.n
                b.emit("scalar", lambda e, qc=qc: e.activation(
                    crow[0:1, qc * 512: qc * 512 + 512], psA[0][0:1, 0:512], AF.Identity),
                    waits=[(sp, cf)], incs=[(sa, 1)])
                crow_done = sa.n
            ca = sa.n
            cbc_ev = 0
            for qc in range(4):
                wl = [(sa, ca)]
                wl.append((sv, psDn_free[1]) if qc == 0 else (sv, cbc_ev))
                b.emit("tensor", lambda e, qc=qc: e.matmul(
                    psA[1][:], ones_rowf[:], crow[0:1, qc * 512: qc * 512 + 512],
                    start=True, stop=True),
                    waits=wl, incs=[(sp, 1)])
                cf = sp.n
                b.emit("vector", lambda e, qc=qc: e.tensor_copy(
                    cbcast[:, qc * 512: qc * 512 + 512], psA[1][:, 0:512]),
                    waits=[(sp, cf)], incs=[(sv, 1)])
                cbc_ev = sv.n
            cb_ready[0] = sv.n
            psDn_free[0] = sv.n
            psDn_free[1] = sv.n

        for half in range(2):
            silu_sa = {}
            first_silu = True
            for mt in range(6):
                for who in range(2):
                    m = (half * 6 + mt) + (0 if who == 0 else 12)
                    slot = upslot[0] % 2
                    upslot[0] += 1
                    b.emit("gpsimd", lambda e, ex=ex, m=m, slot=slot: e.dma_start(
                        upw[slot][:], up_src(ex, m)),
                        waits=[(sp, upw_consumed[slot])], incs=[(sdw, 16)])
                    upw_loaded[slot] = sdw.n
                    banks = (0, 1) if who == 0 else (2, 3)
                    for qc in range(4):
                        bank = banks[qc % 2]
                        wl = [(sdw, upw_loaded[slot]), psU_free[bank], (sv, c_acc0)]
                        for kt in range(8):
                            b.emit("tensor", lambda e, slot=slot, kt=kt, qc=qc, bank=bank: e.matmul(
                                psU[bank], upw[slot][:, kt * 128:(kt + 1) * 128],
                                x1T[:, kt * T + qc * 512: kt * T + qc * 512 + 512],
                                start=(kt == 0), stop=(kt == 7)),
                                waits=wl if kt == 0 else [],
                                incs=[(sp, 1)] if kt == 7 else [])
                        cf = sp.n
                        upw_consumed[slot] = cf
                        if who == 0:
                            swl = [(sp, cf)]
                            if first_silu:
                                swl.append((sp, c_prev_dn[0]))
                                first_silu = False
                            b.emit("scalar", lambda e, mt=mt, qc=qc, bank=bank: e.activation(
                                hbuf[:, mt * T + qc * 512: mt * T + qc * 512 + 512],
                                psU[bank], AF.Silu), waits=swl, incs=[(sa, 1)])
                            silu_sa[(mt, qc)] = sa.n
                            psU_free[bank] = (sa, sa.n)
                        else:
                            if shared:
                                b.emit("vector", lambda e, mt=mt, qc=qc, bank=bank: e.tensor_mul(
                                    hbuf[:, mt * T + qc * 512: mt * T + qc * 512 + 512],
                                    psU[bank],
                                    hbuf[:, mt * T + qc * 512: mt * T + qc * 512 + 512]),
                                    waits=[(sp, cf), (sa, silu_sa[(mt, qc)])],
                                    incs=[(sv, 1)])
                            else:
                                b.emit("vector", lambda e, qc=qc, bank=bank: e.tensor_mul(
                                    utmp[:], psU[bank],
                                    cbcast[:, qc * 512: qc * 512 + 512]),
                                    waits=[(sp, cf), (sv, cb_ready[0])])
                                b.emit("vector", lambda e, mt=mt, qc=qc: e.tensor_mul(
                                    hbuf[:, mt * T + qc * 512: mt * T + qc * 512 + 512],
                                    utmp[:],
                                    hbuf[:, mt * T + qc * 512: mt * T + qc * 512 + 512]),
                                    waits=[(sa, silu_sa[(mt, qc)])],
                                    incs=[(sv, 1)])
                            psU_free[bank] = (sv, sv.n)
            c_h = sv.n

            for mo in range(8):
                slot = mo % 2
                b.emit("gpsimd", lambda e, ex=ex, mo=mo, slot=slot: e.dma_start(
                    dn_slots[slot][:], dn_src(ex, mo)),
                    waits=[(sp, dnw_consumed[slot])], incs=[(sdw, 16)])
                dnw_loaded[slot] = sdw.n
                for qc in range(4):
                    bank = (mo * 4 + qc) % 4
                    wl = [(sdw, dnw_loaded[slot]), (sv, c_h), (sv, psDn_free[bank])]
                    for lt in range(6):
                        kt = half * 6 + lt
                        b.emit("tensor", lambda e, slot=slot, kt=kt, lt=lt, qc=qc, bank=bank: e.matmul(
                            psDn[bank], dn_slots[slot][:, kt * 128:(kt + 1) * 128],
                            hbuf[:, lt * T + qc * 512: lt * T + qc * 512 + 512],
                            start=(lt == 0), stop=(lt == 5)),
                            waits=wl if lt == 0 else [],
                            incs=[(sp, 1)] if lt == 5 else [])
                    cf = sp.n
                    if qc == 3:
                        dnw_consumed[slot] = cf
                    b.emit("vector", lambda e, mo=mo, qc=qc, bank=bank: e.scalar_tensor_tensor(
                        acc[:, mo * T + qc * 512: mo * T + qc * 512 + 512],
                        acc[:, mo * T + qc * 512: mo * T + qc * 512 + 512], 0.0,
                        psDn[bank], op0=ALU.bypass, op1=ALU.add),
                        waits=[(sp, cf)], incs=[(sv, 1)])
                    psDn_free[bank] = sv.n
            c_prev_dn[0] = sp.n

    for ex in range(NE + 1):
        emit_expert(ex)
        if ex == 0:
            tap("hbuf", hbuf, [(sv, sv.n)])
            tap("cbcast", cbcast, [(sv, cb_ready[0])])
    c_ffn_sv = sv.n
    tap("acc", acc, [(sv, c_ffn_sv)])

    tr_evac3 = [0, 0]
    idx = 0
    for t in range(NT):
        for cg in range(2):
            bank = idx % 2
            wl = [(sv, c_ffn_sv)] if idx < 2 else [(sv, tr_evac3[bank])]
            for c4 in range(4):
                c = cg * 4 + c4
                b.emit("tensor", lambda e, t=t, c=c, c4=c4, bank=bank: e.transpose(
                    psP[bank][:, c4 * 128:(c4 + 1) * 128],
                    acc[:, c * T + t * 128: c * T + (t + 1) * 128],
                    ident32[:]), waits=wl if c4 == 0 else [],
                    incs=[(sp, 1)] if c4 == 3 else [])
            cf = sp.n
            b.emit("vector", lambda e, t=t, cg=cg, bank=bank: e.tensor_scalar(
                z_tok[:, t * D + cg * 512: t * D + (cg + 1) * 512], psP[bank][:],
                1.0, 0.0, op0=ALU.mult, op1=ALU.add,
                accum_out=lnz[:, (t * 2 + cg):(t * 2 + cg) + 1]),
                waits=[(sp, cf)], incs=[(sv, 1)])
            tr_evac3[bank] = sv.n
            idx += 1
    c_z = sv.n
    vspace()
    for t in range(NT):
        b.emit("vector", lambda e, t=t: e.reduce_sum(
            sums1[:, t:t + 1], lnz[:, t * 2:(t + 1) * 2], axis=AX.X),
            waits=[(sv, c_z)] if t == 0 else [])
    b.emit("vector", lambda e: e.engine_nop(), incs=[(sv, 1)])
    c_z_sums = sv.n
    for t in range(NT):
        b.emit("scalar", lambda e, t=t: e.activation(
            scr_sq[:], z_tok[:, t * D:(t + 1) * D], AF.Square,
            accum_out=sums2[:, t:t + 1]),
            waits=[(sv, c_z)] if t == 0 else [],
            incs=[(sa, 1)] if t == NT - 1 else [])
    c_z_sq = sa.n
    c_ln3_fin = emit_ln_finish([(sv, c_z_sums), (sa, c_z_sq)])
    for t in range(NT):
        b.emit("scalar", lambda e, t=t: e.activation(
            z_tok[:, t * D:(t + 1) * D], z_tok[:, t * D:(t + 1) * D], AF.Identity,
            scale=rstd[:, t:t + 1], bias=nmr[:, t:t + 1]),
            waits=[(sv, c_ln3_fin)] if t == 0 else [],
            incs=[(sa, 1)] if t == NT - 1 else [])
    c_out_ready = sa.n

    b.emit("sync", lambda e: e.dma_start(hout, z_tok[:]),
           waits=[(sa, c_out_ready)], incs=[(so, 16)])
    tap("z_tok", z_tok, [(sa, c_out_ready)])
    total_so = so.n
    b.emit("gpsimd", lambda e: e.engine_nop(), waits=[(so, total_so)])

    b.finalize(blk)
    est.close()
    return nc


def kernel(L0_attn_X, L0_attn_W_in, L0_attn_b_in, L0_attn_W_out, L0_attn_b_out,
           L0_ffn_gate_w, L0_ffn_w_up, L0_ffn_w_down, L0_ffn_sg_gate,
           L0_ffn_sg_up, L0_ffn_sg_down):
    X = np.asarray(L0_attn_X, dtype=np.float32)
    shared_img = {}
    shared_img["w_in_img"] = _lhsT_slices(np.asarray(L0_attn_W_in, np.float32), 32, 8)
    shared_img["b_in_img"] = np.ascontiguousarray(
        np.asarray(L0_attn_b_in, np.float32).reshape(32, 128).T)
    shared_img["w_out_img"] = _rhs_img(np.ascontiguousarray(
        np.asarray(L0_attn_W_out, np.float32).T))
    shared_img["b_out_row"] = np.ascontiguousarray(
        np.asarray(L0_attn_b_out, np.float32).reshape(1, D))
    shared_img["gate_img"] = _rhs_img(np.ascontiguousarray(
        np.asarray(L0_ffn_gate_w, np.float32).T))
    wup = np.asarray(L0_ffn_w_up, np.float32)
    wdn = np.asarray(L0_ffn_w_down, np.float32)
    shared_img["wup_img"] = np.stack([_lhsT_slices(wup[e], 24, 8) for e in range(NE)])
    shared_img["wdn_img"] = np.stack([_lhsT_slices(wdn[e], 8, 12) for e in range(NE)])
    shared_img["sgg_img"] = _lhsT_slices(np.asarray(L0_ffn_sg_gate, np.float32), 12, 8)
    shared_img["sgu_img"] = _lhsT_slices(np.asarray(L0_ffn_sg_up, np.float32), 12, 8)
    shared_img["sgd_img"] = _lhsT_slices(np.asarray(L0_ffn_sg_down, np.float32), 8, 12)

    nc = build_program()
    in_maps = []
    for c in range(_NUM_CORES):
        d = dict(shared_img)
        d["x_img"] = _img_tok(np.ascontiguousarray(X[c]))
        in_maps.append(d)
    res = run_bass_kernel_spmd(nc, in_maps, core_ids=list(range(_NUM_CORES)))
    out = np.stack([_unimg_tok(res.results[c]["out_img"]) for c in range(_NUM_CORES)])
    if _TAPS:
        kernel._tap_results = [
            {name: res.results[c]["tap_" + name] for (name, _s) in _TAPS}
            for c in range(_NUM_CORES)]
    return out.astype(np.float32)


# version 06
# speedup vs baseline: 1.0607x; 1.0446x over previous
import numpy as np
import ml_dtypes
import concourse.bass as bass
import concourse.mybir as mybir
from concourse.bass_core_rust import BassInstruction
from concourse.bass_utils import run_bass_kernel_spmd

F32 = mybir.dt.float32
F32R = mybir.dt.float32r
BF16 = mybir.dt.bfloat16
AF = mybir.ActivationFunctionType
ALU = mybir.AluOpType
AX = mybir.AxisListType

T = 2048
D = 1024
NT = T // 128
ND = D // 128
H = 16
NE = 8
LN_EPS = 1e-5
SEPS = 1e-4

_TAPS = []
_FEED_IL = ["ppq2vt"]
_NUM_CORES = 8


def _img_tok(x2d):
    return np.ascontiguousarray(
        x2d.reshape(NT, 128, D).transpose(1, 0, 2).reshape(128, NT * D))

def _unimg_tok(img):
    return np.ascontiguousarray(
        img.reshape(128, NT, D).transpose(1, 0, 2).reshape(T, D))

def _lhsT_slices(w, n_m, n_k):
    rows, d = w.shape
    assert rows == n_m * 128 and d == n_k * 128
    a = w.reshape(n_m, 128, n_k, 128)
    return np.ascontiguousarray(a.transpose(0, 3, 2, 1).reshape(n_m, 128, n_k * 128))

def _rhs_img(wT):
    d, cols = wT.shape
    nk = d // 128
    a = wT.reshape(nk, 128, cols)
    return np.ascontiguousarray(a.transpose(1, 0, 2).reshape(128, nk * cols))


class Ctr:
    def __init__(self, handle):
        self.h = handle
        self.n = 0

class B:
    def __init__(self, nc):
        self.nc = nc
        self.streams = {e: [] for e in ("tensor", "vector", "scalar", "gpsimd", "sync")}

    def emit(self, eng, fn, waits=(), incs=()):
        for (c, k) in incs:
            c.n += k
        snap = tuple((c.h, v) for (c, v) in waits if v > 0)
        tgt = tuple((c.h, k) for (c, k) in incs)
        def f(e):
            for (h, v) in snap:
                e.wait_ge(h, v)
            inst = fn(e)
            if tgt:
                assert inst is not None
                for (h, k) in tgt:
                    inst = inst.then_inc(h, k)
        self.streams[eng].append(f)

    def finalize(self, block):
        st = self.streams
        @block.sync
        def _(e):
            for f in st["sync"]: f(e)
        @block.gpsimd
        def _(e):
            for f in st["gpsimd"]: f(e)
        @block.vector
        def _(e):
            for f in st["vector"]: f(e)
        @block.scalar
        def _(e):
            for f in st["scalar"]: f(e)
        @block.tensor
        def _(e):
            for f in st["tensor"]: f(e)


def act_swap(nc, s, out, in_, func, bias=0.0, scale=1.0, accum_out=None):
    inst = s.activation(out, in_, AF.Sqrt, bias=bias, scale=scale, accum_out=accum_out)
    i = inst.ins
    clone = mybir.InstActivation(name=i.name, engine=i.engine, ins=list(i.ins),
                                 outs=list(i.outs), func=func)
    clone.sync_info = i.sync_info
    bb = nc.cur_bb.bb
    assert bb.instructions[-1].name == i.name
    bb.instructions[-1] = clone
    return BassInstruction(clone)


def build_program():
    nc = bass.Bass("TRN2", target_bir_lowering=False, detect_race_conditions=False)

    hx = nc.dram_tensor("x_img", [128, NT * D], F32, kind="ExternalInput").ap()
    hwin = nc.dram_tensor("w_in_img", [32, 128, 1024], F32, kind="ExternalInput").ap()
    hbin = nc.dram_tensor("b_in_img", [128, 32], F32, kind="ExternalInput").ap()
    hwout = nc.dram_tensor("w_out_img", [128, 8 * 1024], F32, kind="ExternalInput").ap()
    hbout = nc.dram_tensor("b_out_row", [1, D], F32, kind="ExternalInput").ap()
    hgate = nc.dram_tensor("gate_img", [128, 64], F32, kind="ExternalInput").ap()
    hwup = nc.dram_tensor("wup_img", [NE, 24, 128, 1024], F32, kind="ExternalInput").ap()
    hwdn = nc.dram_tensor("wdn_img", [NE, 8, 128, 1536], F32, kind="ExternalInput").ap()
    hsgg = nc.dram_tensor("sgg_img", [12, 128, 1024], F32, kind="ExternalInput").ap()
    hsgu = nc.dram_tensor("sgu_img", [12, 128, 1024], F32, kind="ExternalInput").ap()
    hsgd = nc.dram_tensor("sgd_img", [8, 128, 1536], F32, kind="ExternalInput").ap()
    hout = nc.dram_tensor("out_img", [128, NT * D], F32, kind="ExternalOutput").ap()
    htap = {}
    for (name, shape) in _TAPS:
        htap[name] = nc.dram_tensor("tap_" + name, list(shape), F32,
                                    kind="ExternalOutput").ap()

    KB = 1024
    ARENA = 200 * KB
    base0, _end = nc.bump_sbuf(ARENA, align_bytes=256)
    A0, B0, C0, D0, S0 = (base0, base0 + 64 * KB, base0 + 128 * KB,
                          base0 + 168 * KB, base0 + 184 * KB)

    def at(name, off, shape, dtype):
        return nc.alloc_sbuf_tensor_at(name, shape, dtype, offset=off).ap()

    X = at("X", A0, [128, NT * D], F32)
    qaug = [at(f"qaug{i}", A0 + i * 4 * KB, [66, T], BF16) for i in range(2)]
    kaug = [at(f"kaug{i}", A0 + 8 * KB + i * 4 * KB, [66, T], BF16) for i in range(2)]
    vT = at("vT", A0 + 16 * KB, [64, T], F32)
    gTb = [at(f"gTb{i}", A0 + 24 * KB + i * 4 * KB, [64, T], BF16) for i in range(2)]
    vprime = [at(f"vp{i}", A0 + 32 * KB + i * 5 * KB, [128, 16 * 65], BF16) for i in range(2)]
    qsq = at("qsq", A0 + 42 * KB, [64, 512], F32)
    rec = [at(f"rec{i}", A0 + 44 * KB + i * 2 * KB, [64, 512], F32) for i in range(2)]
    normt = [at(f"normt{i}", A0 + 48 * KB + i * 2 * KB, [64, 512], F32) for i in range(2)]
    rowt2 = at("rowt2", A0 + 52 * KB, [1, 512], F32)
    qsq2 = at("qsq2", A0 + 54 * KB, [64, 512], F32)
    astg = [at(f"astg{i}", A0 + 56 * KB + i * 2 * KB, [65, 512], F32) for i in range(4)]
    r_tok = at("r_tok", A0, [128, NT * D], F32)
    x1T = at("x1T", A0, [128, ND * T], F32R)
    z_tok = at("z_tok", A0, [128, NT * D], F32)

    xn_tok = at("xn_tok", B0, [128, NT * D], BF16)
    xnT = at("xnT", B0 + 32 * KB, [128, ND * T], BF16)
    u_s = [at(f"u{i}", B0 + i * 2 * KB, [128, 1024], BF16) for i in range(4)]
    uu_s = [at(f"uu{i}", B0 + 8 * KB + i * 2 * KB, [128, 1024], BF16) for i in range(4)]
    wring = [at(f"wr{i}", B0 + 16 * KB + i * 2 * KB, [128, 1024], BF16) for i in range(4)]
    X_re = at("X_re", B0, [128, NT * D], F32)
    x1_tok = at("x1_tok", B0, [128, NT * D], F32)
    acc = at("acc", B0, [128, ND * T], F32)

    OT = at("OT", C0, [128, 9 * T], BF16)
    hbuf = at("hbuf", C0, [128, 6 * T], BF16)
    cbcast = at("cbcast", C0 + 24 * KB, [128, T], F32)
    crow = at("crow", C0 + 32 * KB, [1, T], F32)

    winw = [at(f"winw{i}", D0 + i * 2 * KB, [128, 1024], BF16) for i in range(8)]
    woutb = at("woutb", D0, [128, 8 * 1024], BF16)
    upw = [at(f"upw{i}", D0 + i * 4 * KB, [128, 1024], F32R) for i in range(2)]
    dnw0 = at("dnw0", D0 + 8 * KB, [128, 1536], BF16)
    dn2 = at("dn2", D0 + 11 * KB, [128, 1536], BF16)

    o = S0
    ident32 = at("ident32", o, [128, 128], F32); o += 512
    identbf = at("identbf", o, [128, 128], BF16); o += 256
    ones_col = at("ones_col", o, [128, 1], F32); o += 32
    q2lhs = at("q2lhs", o, [64, 2], F32); o += 32
    k2lhs = at("k2lhs", o, [64, 2], F32); o += 32
    bias_q = at("bias_q", o, [2, 1], F32); o += 32
    bias_k = at("bias_k", o, [2, 1], F32); o += 32
    eps_col = at("eps_col", o, [128, 1], F32); o += 32
    seps1 = at("seps1", o, [1, 1], F32); o += 32
    binsb = at("binsb", o, [128, 32], F32); o += 128
    binsb_n2 = at("binsb_n2", o, [128, 32], F32); o += 128
    ones_rowf = at("ones_rowf", o, [1, 128], F32); o += 512
    boutr = at("boutr", o, [1, D], BF16); o += 2 * KB
    gatesb = at("gatesb", o, [128, 64], F32R); o += 256
    iotar = at("iotar", o, [128, 8], F32); o += 32
    sums1 = at("sums1", o, [128, NT], F32); o += 64
    sums1b = at("sums1b", o, [128, NT], F32); o += 64
    sums2 = at("sums2", o, [128, NT], F32); o += 64
    mu = at("mu", o, [128, NT], F32); o += 64
    var_ = at("var", o, [128, NT], F32); o += 64
    rstd = at("rstd", o, [128, NT], F32); o += 64
    nmr = at("nmr", o, [128, NT], F32); o += 64
    scr_sq = at("scr_sq", o, [128, D], F32); o += 4 * KB
    logits = at("logits", o, [128, NT * 8], F32); o += 512
    comb = at("comb", o, [128, NT * 8], F32); o += 512
    gm1 = at("gm1", o, [128, NT], F32); o += 64
    gm2 = at("gm2", o, [128, NT], F32); o += 64
    gmi = at("gmi", o, [128, NT], F32); o += 64
    gmi2 = at("gmi2", o, [128, NT], F32); o += 64
    gw1 = at("gw1", o, [128, NT], F32); o += 64
    gw2 = at("gw2", o, [128, NT], F32); o += 64
    gdm = at("gdm", o, [128, NT], F32); o += 64
    geqA = at("geqA", o, [128, 128], F32); o += 512
    geq1 = at("geq1", o, [128, 128], F32); o += 512
    geq2 = at("geq2", o, [128, 128], F32); o += 512
    grow2 = at("grow2", o, [128, 128], F32); o += 512
    gtmp = at("gtmp", o, [128, 128], F32); o += 512
    utmp = at("utmp", o, [128, 512], F32); o += 2 * KB
    lnz = at("lnz", o, [128, NT * ND], F32); o += 512
    assert o - S0 <= 16 * KB, f"small region overflow: {o - S0}"

    psS = [nc.alloc_psum_tensor(f"psS{i}", [128, 1024], F32).ap() for i in range(2)]
    psA = [nc.alloc_psum_tensor(f"psA{i}", [128, 512], F32).ap() for i in range(2)]
    psP = [nc.alloc_psum_tensor(f"psP{i}", [128, 512], F32).ap() for i in range(2)]
    psPbf = [p.bitcast(BF16) for p in psP]
    psU = [psS[0][:, 0:512], psS[0][:, 512:1024], psS[1][:, 0:512], psS[1][:, 512:1024]]
    psDn = [psA[0], psA[1], psP[0], psP[1]]

    import contextlib
    est = contextlib.ExitStack()
    sems = {}
    for name in ("sdin", "sdw", "sp", "sa", "sv", "sg", "so"):
        sems[name] = Ctr(est.enter_context(nc.semaphore(name)))
    sdin, sdw, sp, sa, sv, sg, so = (sems[k] for k in
                                     ("sdin", "sdw", "sp", "sa", "sv", "sg", "so"))
    blk = est.enter_context(nc.Block())
    b = B(nc)
    tap_names = {name: shape for (name, shape) in _TAPS}

    def tap(name, src, waits):
        if name not in tap_names:
            return
        shape = tap_names[name]
        b.emit("gpsimd", lambda e, name=name, src=src, shape=shape: e.dma_start(
            htap[name], src[0:shape[0], 0:shape[1]]),
            waits=waits, incs=[(so, 16)])

    b.emit("sync", lambda e: e.dma_start(X[:], hx), incs=[(sdin, 16)])
    c_X = sdin.n
    b.emit("sync", lambda e: e.dma_start(binsb[:], hbin), incs=[(sdin, 16)])
    c_bin = sdin.n
    tap("X", X, [(sdin, c_X)])
    b.emit("gpsimd", lambda e: e.dma_start(gatesb[:], hgate), incs=[(sdw, 16)])
    c_gate = sdw.n
    b.emit("gpsimd", lambda e: e.dma_start(boutr[:], hbout), incs=[(sdw, 16)])
    c_bout = sdw.n

    def _gsetup(e):
        e.memset(ident32[:], 0.0)
        nc.gpsimd.affine_select(out=ident32[:], in_=ident32[:], compare_op=ALU.not_equal,
                                fill=1.0, base=0, pattern=[[-1, 128]], channel_multiplier=1)
        e.memset(identbf[:], 0.0)
        nc.gpsimd.affine_select(out=identbf[:], in_=identbf[:], compare_op=ALU.not_equal,
                                fill=1.0, base=0, pattern=[[-1, 128]], channel_multiplier=1)
        e.memset(ones_col[:], 1.0)
        e.memset(ones_rowf[:], 1.0)
        e.memset(q2lhs[:], 0.0)
        e.memset(q2lhs[:, 0:1], 0.25)
        e.memset(k2lhs[:], 0.0)
        e.memset(k2lhs[:, 1:2], 1.0)
        e.iota(bias_q[:], pattern=[[0, 1]], base=0, channel_multiplier=1,
               allow_small_or_imprecise_dtypes=True)
        e.iota(bias_k[:], pattern=[[0, 1]], base=1, channel_multiplier=-1,
               allow_small_or_imprecise_dtypes=True)
        e.memset(eps_col[:], LN_EPS)
        e.memset(seps1[:], SEPS)
        e.memset(OT[0:1, 8 * T:9 * T], 1.0)
        return nc.gpsimd.iota(iotar[:], pattern=[[-1, 8]], base=8, channel_multiplier=0,
                              allow_small_or_imprecise_dtypes=True)
    b.emit("gpsimd", _gsetup, incs=[(sg, 1)])
    c_consts = sg.n

    win_loaded = [0] * 32
    for m in range(8):
        b.emit("gpsimd", lambda e, m=m: e.dma_start(winw[m % 8][:], hwin[m]),
               incs=[(sdw, 16)])
        win_loaded[m] = sdw.n

    b.emit("vector", lambda e: e.tensor_scalar_mul(binsb_n2[:], binsb[:], -2.0),
           waits=[(sdin, c_bin)], incs=[(sv, 1)])

    def vspace():
        b.emit("vector", lambda e: e.tensor_scalar_mul(scr_sq[:, 0:512],
                                                       scr_sq[:, 0:512], 1.0))

    def emit_ln_finish(waits):
        b.emit("vector", lambda e: e.tensor_scalar_mul(mu[:], sums1[:], 1.0 / D),
               waits=waits)
        b.emit("vector", lambda e: e.tensor_scalar_mul(var_[:], sums2[:], 1.0 / D))
        vspace()
        b.emit("vector", lambda e: e.tensor_mul(nmr[:], mu[:], mu[:]))
        vspace()
        b.emit("vector", lambda e: e.tensor_sub(var_[:], var_[:], nmr[:]), incs=[(sv, 1)])
        cv = sv.n
        b.emit("scalar", lambda e: act_swap(nc, e, rstd[:], var_[:], AF.Rsqrt,
                                            bias=eps_col[:, 0:1]),
               waits=[(sv, cv), (sg, c_consts)], incs=[(sa, 1)])
        ca = sa.n
        b.emit("vector", lambda e: e.scalar_tensor_tensor(
            nmr[:], mu[:], -1.0, rstd[:], op0=ALU.mult, op1=ALU.mult),
            waits=[(sa, ca)], incs=[(sv, 1)])
        return sv.n

    for t in range(NT):
        b.emit("vector", lambda e, t=t: e.reduce_sum(
            sums1[:, t:t + 1], X[:, t * D:(t + 1) * D], axis=AX.X),
            waits=[(sdin, c_X)] if t == 0 else [])
    b.emit("vector", lambda e: e.engine_nop(), incs=[(sv, 1)])
    c_ln1_sums = sv.n
    for t in range(NT):
        b.emit("scalar", lambda e, t=t: e.activation(
            scr_sq[:], X[:, t * D:(t + 1) * D], AF.Square,
            accum_out=sums2[:, t:t + 1]),
            waits=[(sdin, c_X)] if t == 0 else [],
            incs=[(sa, 1)] if t == NT - 1 else [])
    c_ln1_sq = sa.n
    c_ln1_fin = emit_ln_finish([(sv, c_ln1_sums), (sa, c_ln1_sq)])
    tap("mu1", mu, [(sv, c_ln1_fin)])
    tap("rstd1", rstd, [(sv, c_ln1_fin)])
    tap("s1", sums1, [(sv, c_ln1_sums)])
    tap("s2", sums2, [(sa, c_ln1_sq)])
    for t in range(NT):
        b.emit("scalar", lambda e, t=t: e.activation(
            xn_tok[:, t * D:(t + 1) * D], X[:, t * D:(t + 1) * D], AF.Identity,
            scale=rstd[:, t:t + 1], bias=nmr[:, t:t + 1]),
            waits=[(sv, c_ln1_fin)] if t == 0 else [],
            incs=[(sa, 1)] if t == NT - 1 else [])
    c_ln1_apply = sa.n
    tap("xn_tok", xn_tok, [(sa, c_ln1_apply)])

    def _vp_ones(e):
        last = None
        for i in range(2):
            for kt in range(NT):
                last = e.activation(vprime[i][:, kt * 65 + 64: kt * 65 + 65],
                                    ones_col[:], AF.Identity)
        return last
    b.emit("scalar", _vp_ones, waits=[(sg, c_consts)], incs=[(sa, 1)])
    c_vpones = sa.n

    tr_evac = [0, 0]
    for t in range(NT):
        bank = t % 2
        wl = [(sa, c_ln1_apply), (sg, c_consts)] if t < 2 else [(sv, tr_evac[bank])]
        for c in range(ND):
            b.emit("tensor", lambda e, t=t, c=c, bank=bank: e.transpose(
                psPbf[bank][:, c * 128:(c + 1) * 128],
                xn_tok[:, t * D + c * 128: t * D + (c + 1) * 128],
                identbf[:]), waits=wl if c == 0 else [],
                incs=[(sp, 1)] if c == ND - 1 else [])
        cf = sp.n
        dst = xnT.rearrange("p (c y) -> p c y", c=ND)[:, :, t * 128:(t + 1) * 128]
        b.emit("vector", lambda e, dst=dst, bank=bank: e.tensor_copy(
            dst, psPbf[bank][:].rearrange("p (c x) -> p c x", c=ND)),
            waits=[(sp, cf)], incs=[(sv, 1)])
        tr_evac[bank] = sv.n
    c_xnT = sv.n
    tap("xnT", xnT, [(sv, c_xnT)])

    pP_state = [(sv, tr_evac[0]), (sv, tr_evac[1])]
    prep_done = [None] * H
    main_sv = [0] * H
    psA_free = [0, 0]
    psS_free = [0, 0]
    psS0_borrow = [0]
    wring_free = [0, 0, 0, 0]
    uslot_free = [0, 0, 0, 0]
    last_phead_mm = [0]
    last_q2mm = [0]
    wready = {}

    def emit_prep(h):
        thunks = []
        par = h % 2
        ready = {}
        qsq_mm = [0, 0]
        mi0_evac = [0]

        def pproj(mi, qc):
            c_prev_main = main_sv[h - 2] if h >= 2 else 0
            m = 2 * h + mi
            slot = m % 8
            bank = mi
            wl = [(sdw, win_loaded[m]), (sv, c_xnT), pP_state[bank]]
            for kt in range(8):
                b.emit("tensor", lambda e, qc=qc, kt=kt, slot=slot, bank=bank: e.matmul(
                    psP[bank][:],
                    winw[slot][:, kt * 128:(kt + 1) * 128],
                    xnT[:, kt * T + qc * 512: kt * T + qc * 512 + 512],
                    start=(kt == 0), stop=(kt == 7)),
                    waits=wl if kt == 0 else [],
                    incs=[(sp, 1)] if kt == 7 else [])
            cfill = sp.n
            last_phead_mm[0] = cfill
            if mi == 0:
                b.emit("scalar", lambda e, par=par, qc=qc, m=m: e.activation(
                    qaug[par][0:64, qc * 512: qc * 512 + 512], psP[0][0:64, :],
                    AF.Identity, scale=-2.0, bias=binsb_n2[0:64, m:m + 1]),
                    waits=[(sp, cfill), (sv, c_prev_main)])
                b.emit("scalar", lambda e, par=par, qc=qc, m=m: e.activation(
                    kaug[par][0:64, qc * 512: qc * 512 + 512], psP[0][64:128, :],
                    AF.Identity, bias=binsb[64:128, m:m + 1]), incs=[(sa, 1)])
                evacd[qc] = sa.n
            else:
                b.emit("scalar", lambda e, qc=qc, m=m: e.activation(
                    vT[0:64, qc * 512: qc * 512 + 512], psP[1][0:64, :],
                    AF.Identity, bias=binsb[0:64, m:m + 1]),
                    waits=[(sp, cfill), (sv, c_prev_main)])
                b.emit("scalar", lambda e, par=par, qc=qc, m=m: e.activation(
                    gTb[par][0:64, qc * 512: qc * 512 + 512], psP[1][64:128, :],
                    AF.Identity, bias=binsb[64:128, m:m + 1]), incs=[(sa, 1)])
            pP_state[bank] = (sa, sa.n)
            if qc == 3 and m + 8 < 32:
                cm = sp.n
                b.emit("gpsimd", lambda e, m=m, slot=slot: e.dma_start(
                    winw[slot][:], hwin[m + 8]), waits=[(sp, cm)], incs=[(sdw, 16)])
                win_loaded[m + 8] = sdw.n

        evacd = {}
        qsq_sv = {}
        q2mm_sp = {}

        def qsq_op(k):
            which, qc = divmod(k, 4)
            src_ = qaug[par] if which == 0 else kaug[par]
            qb = k % 2
            qbuf = qsq if qb == 0 else qsq2
            b.emit("vector", lambda e, src_=src_, qc=qc, qbuf=qbuf: e.tensor_mul(
                qbuf[:], src_[0:64, qc * 512: qc * 512 + 512],
                src_[0:64, qc * 512: qc * 512 + 512]),
                waits=[(sa, evacd[qc]), (sp, qsq_mm[qb])],
                incs=[(sv, 1)])
            qsq_sv[k] = sv.n

        def q2mm_op(k):
            which, qc = divmod(k, 4)
            lhs = q2lhs if which == 0 else k2lhs
            qb = k % 2
            qbuf = qsq if qb == 0 else qsq2
            b.emit("tensor", lambda e, lhs=lhs, qbuf=qbuf: e.matmul(
                psP[0][0:2, :], lhs[:], qbuf[:], start=True, stop=True),
                waits=[(sv, qsq_sv[k]), pP_state[0], (sg, c_consts)], incs=[(sp, 1)])
            qsq_mm[qb] = sp.n
            q2mm_sp[k] = sp.n
            last_q2mm[0] = sp.n

        def q2evac_op(k):
            which, qc = divmod(k, 4)
            src_ = qaug[par] if which == 0 else kaug[par]
            bias = bias_q if which == 0 else bias_k
            b.emit("scalar", lambda e, src_=src_, qc=qc, bias=bias: e.activation(
                src_[64:66, qc * 512: qc * 512 + 512], psP[0][0:2, :], AF.Identity,
                bias=bias[:, 0:1]), waits=[(sp, q2mm_sp[k])], incs=[(sa, 1)])
            pP_state[0] = (sa, sa.n)

        def vtrans(grp):
            wl0 = [pP_state[1], (sa, c_vpones)]
            for k8 in range(8):
                kt = grp * 8 + k8
                b.emit("tensor", lambda e, kt=kt, k8=k8: e.transpose(
                    psP[1][:, k8 * 64:(k8 + 1) * 64],
                    vT[0:64, kt * 128:(kt + 1) * 128], ident32[0:64, 0:64]),
                    waits=wl0 if k8 == 0 else [],
                    incs=[(sp, 1)] if k8 == 7 else [])

        def vtrans_evac(grp):
            cf = sp.n if False else vt_fill[grp]
            dst = vprime[par].rearrange("p (k y) -> p k y", k=16)[
                :, grp * 8:(grp + 1) * 8, 0:64]
            b.emit("vector", lambda e, dst=dst: e.tensor_copy(
                dst, psP[1][:].rearrange("p (k y) -> p k y", k=8)),
                waits=[(sp, vt_fill[grp])], incs=[(sv, 1)])
            pP_state[1] = (sv, sv.n)

        vt_fill = {}

        def vtrans_fill(grp):
            vtrans(grp)
            vt_fill[grp] = sp.n

        def finalize():
            ready["aug"] = sa.n
            ready["vp"] = sv.n
            ready["g"] = sa.n
            prep_done[h] = ready
            if h == 0:
                tap("qaug0", qaug[0], [(sa, ready["aug"])])
                tap("kaug0", kaug[0], [(sa, ready["aug"])])
                tap("vp0", vprime[0], [(sv, ready["vp"])])

        def TH(*fns, kind="pp"):
            t = lambda fns=fns: [f() for f in fns]
            t.kind = kind
            thunks.append(t)
        TH(lambda: pproj(0, 0))
        TH(lambda: pproj(0, 1))
        TH(lambda: pproj(0, 2))
        TH(lambda: pproj(0, 3), lambda: qsq_op(0), kind="q2")
        TH(lambda: qsq_op(1), lambda: q2mm_op(0), kind="q2")
        TH(lambda: qsq_op(2), lambda: q2evac_op(0), lambda: q2mm_op(1), kind="q2")
        TH(lambda: qsq_op(3), lambda: q2evac_op(1), lambda: q2mm_op(2), kind="q2")
        TH(lambda: qsq_op(4), lambda: q2evac_op(2), lambda: q2mm_op(3), kind="q2")
        TH(lambda: qsq_op(5), lambda: q2evac_op(3), lambda: q2mm_op(4), kind="q2")
        TH(lambda: qsq_op(6), lambda: q2evac_op(4), lambda: q2mm_op(5), kind="q2")
        TH(lambda: qsq_op(7), lambda: q2evac_op(5), lambda: q2mm_op(6), kind="q2")
        TH(lambda: q2evac_op(6), lambda: q2mm_op(7), kind="q2")
        TH(lambda: q2evac_op(7), kind="q2")
        TH(lambda: pproj(1, 0))
        TH(lambda: pproj(1, 1))
        TH(lambda: pproj(1, 2))
        TH(lambda: pproj(1, 3))
        TH(lambda: vtrans_fill(0), kind="vt")
        TH(lambda: vtrans_evac(0), kind="vt")
        TH(lambda: vtrans_fill(1), kind="vt")
        TH(lambda: vtrans_evac(1), finalize, kind="vt")
        return thunks

    LAG = 3

    def emit_S_EW(h, g):
        par = h % 2
        rd = prep_done[h]
        qc2, kt = divmod(g, NT)
        i = g % 2
        slot2 = g % 4
        for half in range(2):
            wl = []
            if half == 0:
                wl = [(sa, rd["aug"]), (sa, psS_free[i])]
                if i == 0:
                    wl.append((sa, psS0_borrow[0]))
            b.emit("tensor", lambda e, par=par, kt=kt, qc2=qc2, half=half, i=i: e.matmul(
                psS[i][:, half * 512:(half + 1) * 512],
                kaug[par][:, kt * 128:(kt + 1) * 128],
                qaug[par][:, qc2 * 1024 + half * 512: qc2 * 1024 + (half + 1) * 512],
                start=True, stop=True),
                waits=wl, incs=[(sp, 1)] if half == 1 else [])
        cS = sp.n
        b.emit("scalar", lambda e, i=i, slot2=slot2: act_swap(
            nc, e, u_s[slot2][:], psS[i][:], AF.Rsqrt),
            waits=[(sp, cS), (sv, uslot_free[slot2])], incs=[(sa, 1)])
        psS_free[i] = sa.n
        cu = sa.n
        if g % 2 == 0:
            b.emit("scalar", lambda e, slot2=slot2: e.activation(
                uu_s[slot2][:], u_s[slot2][:], AF.Square), incs=[(sa, 1)])
            cuu = (sa, sa.n)
        else:
            b.emit("vector", lambda e, slot2=slot2: e.tensor_mul(
                uu_s[slot2][:], u_s[slot2][:], u_s[slot2][:]),
                waits=[(sa, cu)], incs=[(sv, 1)])
            cuu = (sv, sv.n)
        b.emit("vector", lambda e, slot2=slot2: e.tensor_scalar(
            u_s[slot2][:], u_s[slot2][:], -2.0 * SEPS, 1.0,
            op0=ALU.mult, op1=ALU.add), waits=[cuu])
        b.emit("vector", lambda e, slot2=slot2, wslot=g % 4: e.tensor_mul(
            wring[wslot][:], uu_s[slot2][:], u_s[slot2][:]),
            waits=[(sp, wring_free[g % 4])], incs=[(sv, 1)])
        uslot_free[slot2] = sv.n
        wready[g] = sv.n

    def emit_A(h, g):
        par = h % 2
        rd = prep_done[h]
        qc2, kt = divmod(g, NT)
        wslot = g % 4
        for j in range(2):
            wl = [(sv, wready[g])]
            if kt == 0:
                wl += [(sv, rd["vp"]), (sv, psA_free[j])]
            b.emit("tensor", lambda e, kt=kt, j=j, par=par, wslot=wslot: e.matmul(
                psA[j][0:65, :], vprime[par][:, kt * 65: kt * 65 + 65],
                wring[wslot][:, j * 512:(j + 1) * 512],
                start=(kt == 0), stop=(kt == NT - 1)),
                waits=wl, incs=[(sp, 1)] if j == 1 else [])
        wring_free[wslot] = sp.n

    def emit_norm(h, qc2):
        par = h % 2
        rd = prep_done[h]
        cA = sp.n
        for j in range(2):
            qc = qc2 * 2 + j
            stg = astg[qc2 * 2 + j]
            b.emit("vector", lambda e, j=j, stg=stg: e.tensor_copy(
                stg[:], psA[j][0:65, :]), waits=[(sp, cA)], incs=[(sv, 1)])
            psA_free[j] = sv.n
            cst = sv.n
            b.emit("scalar", lambda e, stg=stg: e.activation(
                rowt2[:], stg[64:65, :], AF.Square, bias=seps1[:, 0:1]),
                waits=[(sv, cst)], incs=[(sa, 1)])
            crr = sa.n
            b.emit("tensor", lambda e: e.matmul(
                psS[0][0:64, 0:512], ones_rowf[0:1, 0:64], rowt2[:],
                start=True, stop=True),
                waits=[(sa, crr), (sa, psS_free[0])], incs=[(sp, 1)])
            cb = sp.n
            b.emit("scalar", lambda e, j=j: act_swap(
                nc, e, rec[j][:], psS[0][0:64, 0:512], AF.Rsqrt),
                waits=[(sp, cb)], incs=[(sa, 1)])
            psS0_borrow[0] = sa.n
            cre = sa.n
            b.emit("vector", lambda e, j=j, stg=stg: e.tensor_mul(
                normt[j][:], stg[0:64, :], rec[j][:]),
                waits=[(sa, cre)])
            b.emit("vector", lambda e, j=j, qc=qc, par=par, h=h: e.tensor_mul(
                OT[(h % 2) * 64:(h % 2) * 64 + 64,
                   (h // 2) * T + qc * 512:(h // 2) * T + qc * 512 + 512],
                normt[j][:], gTb[par][0:64, qc * 512: qc * 512 + 512]),
                waits=[(sa, rd["g"])], incs=[(sv, 1)])

    def emit_main(h, feed=()):
        feed = list(feed)
        wready.clear()
        for g in range(2 * NT):
            emit_S_EW(h, g)
            if feed and g >= 2 and getattr(feed[0], "kind", "pp") in _FEED_IL[0]:
                feed.pop(0)()
            if g >= LAG:
                emit_A(h, g - LAG)
                if (g - LAG) % NT == NT - 1:
                    emit_norm(h, (g - LAG) // NT)
        for g in range(2 * NT - LAG, 2 * NT):
            emit_A(h, g)
            if g % NT == NT - 1:
                emit_norm(h, g // NT)
        while feed:
            feed.pop(0)()
        main_sv[h] = sv.n

    for t in emit_prep(0):
        t()
    for h in range(H):
        feed = emit_prep(h + 1) if h + 1 < H else []
        emit_main(h, feed)
    c_heads_sv = sv.n
    c_heads_sa = sa.n
    tap("OT", OT, [(sv, c_heads_sv), (sg, c_consts)])

    b.emit("gpsimd", lambda e: e.dma_start(woutb[:], hwout),
           waits=[(sp, last_phead_mm[0])], incs=[(sdw, 16)])
    c_wout = sdw.n
    b.emit("sync", lambda e: e.dma_start(X_re[:], hx),
           waits=[(sv, c_heads_sv), (sa, c_heads_sa)], incs=[(sdin, 16)])
    c_Xre = sdin.n

    x2banks = [psS[0][:, 0:512], psS[0][:, 512:1024], psS[1][:, 0:512], psS[1][:, 512:1024]]
    x2_free = [c_heads_sv] * 4
    idx = 0
    for t in range(NT):
        for c2 in range(2):
            bank = idx % 4
            wl = [(sv, x2_free[bank]), (sdw, c_wout)]
            if idx == 0:
                wl += [(sa, c_heads_sa), (sg, c_consts)]
            for kt in range(8):
                b.emit("tensor", lambda e, t=t, c2=c2, kt=kt, bank=bank: e.matmul(
                    x2banks[bank], OT[:, kt * T + t * 128: kt * T + (t + 1) * 128],
                    woutb[:, kt * 1024 + c2 * 512: kt * 1024 + c2 * 512 + 512],
                    start=(kt == 0), stop=False),
                    waits=wl if kt == 0 else [])
            b.emit("tensor", lambda e, t=t, c2=c2, bank=bank: e.matmul(
                x2banks[bank], OT[0:1, 8 * T + t * 128: 8 * T + (t + 1) * 128],
                boutr[0:1, c2 * 512: c2 * 512 + 512],
                start=False, stop=True),
                waits=[(sdw, c_bout)] if idx == 0 else [], incs=[(sp, 1)])
            cf = sp.n
            sumcol = sums1 if c2 == 0 else sums1b
            b.emit("vector", lambda e, t=t, c2=c2, bank=bank, sumcol=sumcol: e.scalar_tensor_tensor(
                r_tok[:, t * D + c2 * 512: t * D + c2 * 512 + 512],
                X_re[:, t * D + c2 * 512: t * D + c2 * 512 + 512], 2.0,
                x2banks[bank], op0=ALU.mult, op1=ALU.add,
                accum_out=sumcol[:, t:t + 1]),
                waits=[(sp, cf)] + ([(sdin, c_Xre)] if idx == 0 else []),
                incs=[(sv, 1)])
            x2_free[bank] = sv.n
            idx += 1
    c_x2_mm = sp.n
    vspace()
    b.emit("vector", lambda e: e.tensor_add(sums1[:], sums1[:], sums1b[:]), incs=[(sv, 1)])
    c_r_done = sv.n
    tap("r_tok", r_tok, [(sv, c_r_done)])
    for t in range(NT):
        b.emit("scalar", lambda e, t=t: e.activation(
            scr_sq[:], r_tok[:, t * D:(t + 1) * D], AF.Square,
            accum_out=sums2[:, t:t + 1]),
            waits=[(sv, c_r_done)] if t == 0 else [],
            incs=[(sa, 1)] if t == NT - 1 else [])
    c_r_sq = sa.n
    c_ln2_fin = emit_ln_finish([(sv, c_r_done), (sa, c_r_sq)])
    for t in range(NT):
        b.emit("scalar", lambda e, t=t: e.activation(
            x1_tok[:, t * D:(t + 1) * D], r_tok[:, t * D:(t + 1) * D], AF.Identity,
            scale=rstd[:, t:t + 1], bias=nmr[:, t:t + 1]),
            waits=[(sv, c_ln2_fin), (sdin, c_Xre)] if t == 0 else [],
            incs=[(sa, 1)] if t == NT - 1 else [])
    c_x1_apply = sa.n
    tap("x1_tok", x1_tok, [(sa, c_x1_apply)])

    tr_evac2 = [0, 0]
    idx = 0
    x1T_4d = x1T.rearrange("p (c y) -> p c y", c=ND)
    for t in range(NT):
        for cg in range(2):
            bank = idx % 2
            wl = [(sa, c_x1_apply)] if idx < 2 else [(sv, tr_evac2[bank])]
            for c4 in range(4):
                c = cg * 4 + c4
                b.emit("tensor", lambda e, t=t, c=c, c4=c4, bank=bank: e.transpose(
                    psP[bank][:, c4 * 128:(c4 + 1) * 128],
                    x1_tok[:, t * D + c * 128: t * D + (c + 1) * 128],
                    ident32[:]), waits=wl if c4 == 0 else [],
                    incs=[(sp, 1)] if c4 == 3 else [])
            cf = sp.n
            dst = x1T_4d[:, cg * 4:(cg + 1) * 4, t * 128:(t + 1) * 128]
            b.emit("vector", lambda e, dst=dst, bank=bank: e.tensor_copy(
                dst, psP[bank][:].rearrange("p (c x) -> p c x", c=4)),
                waits=[(sp, cf)], incs=[(sv, 1)])
            tr_evac2[bank] = sv.n
            idx += 1
    c_x1T = sv.n
    tap("x1T", x1T, [(sv, c_x1T)])

    glog_free = [c_x1T, c_x1T]
    for t in range(NT):
        bank = psA[t % 2]
        wl = [(sv, glog_free[t % 2]), (sdw, c_gate)]
        for kt in range(8):
            b.emit("tensor", lambda e, t=t, kt=kt, bank=bank: e.matmul(
                bank[:, 0:8], x1T[:, kt * T + t * 128: kt * T + (t + 1) * 128],
                gatesb[:, kt * 8:(kt + 1) * 8],
                start=(kt == 0), stop=(kt == 7)),
                waits=wl if kt == 0 else [], incs=[(sp, 1)] if kt == 7 else [])
        cf = sp.n
        b.emit("vector", lambda e, t=t, bank=bank: e.tensor_copy(
            logits[:, t * 8:(t + 1) * 8], bank[:, 0:8]), waits=[(sp, cf)],
            incs=[(sv, 1)])
        glog_free[t % 2] = sv.n
    c_logits = sv.n
    tap("logits", logits, [(sv, c_logits)])

    vspace()
    for t in range(NT):
        b.emit("vector", lambda e, t=t: e.reduce_max(
            gm1[:, t:t + 1], logits[:, t * 8:(t + 1) * 8], axis=AX.X),
            waits=[(sv, c_logits), (sg, c_consts)] if t == 0 else [])
    for t in range(NT):
        b.emit("vector", lambda e, t=t: e.tensor_scalar(
            geqA[:, t * 8:(t + 1) * 8], logits[:, t * 8:(t + 1) * 8],
            gm1[:, t:t + 1], None, op0=ALU.is_equal))
    for t in range(NT):
        b.emit("vector", lambda e, t=t: e.scalar_tensor_tensor(
            grow2[:, t * 8:(t + 1) * 8], geqA[:, t * 8:(t + 1) * 8], -1e30,
            logits[:, t * 8:(t + 1) * 8], op0=ALU.mult, op1=ALU.add))
    for t in range(NT):
        b.emit("vector", lambda e, t=t: e.reduce_max(
            gm2[:, t:t + 1], grow2[:, t * 8:(t + 1) * 8], axis=AX.X))
    for t in range(NT):
        b.emit("vector", lambda e, t=t: e.tensor_mul(
            gtmp[:, t * 8:(t + 1) * 8], geqA[:, t * 8:(t + 1) * 8], iotar[:]))
    for t in range(NT):
        b.emit("vector", lambda e, t=t: e.reduce_max(
            gmi[:, t:t + 1], gtmp[:, t * 8:(t + 1) * 8], axis=AX.X))
    for t in range(NT):
        b.emit("vector", lambda e, t=t: e.tensor_scalar(
            geq1[:, t * 8:(t + 1) * 8], iotar[:], gmi[:, t:t + 1], None,
            op0=ALU.is_equal))
    for t in range(NT):
        b.emit("vector", lambda e, t=t: e.tensor_scalar(
            geq2[:, t * 8:(t + 1) * 8], grow2[:, t * 8:(t + 1) * 8],
            gm2[:, t:t + 1], None, op0=ALU.is_equal))
    for t in range(NT):
        b.emit("vector", lambda e, t=t: e.tensor_mul(
            gtmp[:, t * 8:(t + 1) * 8], geq2[:, t * 8:(t + 1) * 8], iotar[:]))
    for t in range(NT):
        b.emit("vector", lambda e, t=t: e.reduce_max(
            gmi2[:, t:t + 1], gtmp[:, t * 8:(t + 1) * 8], axis=AX.X))
    for t in range(NT):
        b.emit("vector", lambda e, t=t: e.tensor_scalar(
            geq2[:, t * 8:(t + 1) * 8], iotar[:], gmi2[:, t:t + 1], None,
            op0=ALU.is_equal))
    vspace()
    b.emit("vector", lambda e: e.tensor_sub(gdm[:], gm1[:], gm2[:]), incs=[(sv, 1)])
    cdm = sv.n
    b.emit("scalar", lambda e: e.activation(gw1[:], gdm[:], AF.Sigmoid),
           waits=[(sv, cdm)], incs=[(sa, 1)])
    cw1 = sa.n
    b.emit("vector", lambda e: e.tensor_scalar(
        gw2[:], gw1[:], -1.0, 1.0, op0=ALU.mult, op1=ALU.add), waits=[(sa, cw1)])
    for t in range(NT):
        b.emit("vector", lambda e, t=t: e.tensor_scalar(
            gtmp[:, t * 8:(t + 1) * 8], geq1[:, t * 8:(t + 1) * 8],
            gw1[:, t:t + 1], None, op0=ALU.mult))
    for t in range(NT):
        b.emit("vector", lambda e, t=t: e.scalar_tensor_tensor(
            comb[:, t * 8:(t + 1) * 8], geq2[:, t * 8:(t + 1) * 8],
            gw2[:, t:t + 1], gtmp[:, t * 8:(t + 1) * 8],
            op0=ALU.mult, op1=ALU.add),
            incs=[(sv, 1)] if t == NT - 1 else [])
    c_comb = sv.n
    tap("comb", comb, [(sv, c_comb)])

    b.emit("vector", lambda e: e.tensor_copy(acc[:], x1T[:]),
           waits=[(sv, c_x1T)], incs=[(sv, 1)])
    c_acc0 = sv.n

    upw_loaded = [0, 0]
    upw_consumed = [c_x2_mm, c_x2_mm]
    dnw_loaded = [0, 0]
    dnw_consumed = [c_x2_mm, c_x2_mm]
    dn_slots = [dnw0, dn2]
    psU_free = [(sv, c_acc0)] * 4
    psDn_free = [c_acc0] * 4
    upslot = [0]
    cb_ready = [0]
    c_prev_dn = [0]

    def up_src(e_, m):
        if e_ < NE:
            return hwup[e_, m]
        return hsgg[m] if m < 12 else hsgu[m - 12]

    def dn_src(e_, m):
        return hwdn[e_, m] if e_ < NE else hsgd[m]

    def emit_expert(ex):
        shared = (ex == NE)
        if not shared:
            crow_done = 0
            for qc in range(4):
                for tt in range(4):
                    t = qc * 4 + tt
                    wl = [(sv, c_comb), (sv, psDn_free[0]), (sg, c_consts)] \
                        if (qc == 0 and tt == 0) else ([(sa, crow_done)] if tt == 0 else [])
                    b.emit("tensor", lambda e, t=t, tt=tt, ex=ex: e.transpose(
                        psA[0][0:1, tt * 128:(tt + 1) * 128],
                        comb[:, t * 8 + ex: t * 8 + ex + 1], ident32[:]),
                        waits=wl, incs=[(sp, 1)] if tt == 3 else [])
                cf = sp.n
                b.emit("scalar", lambda e, qc=qc: e.activation(
                    crow[0:1, qc * 512: qc * 512 + 512], psA[0][0:1, 0:512], AF.Identity),
                    waits=[(sp, cf)], incs=[(sa, 1)])
                crow_done = sa.n
            ca = sa.n
            cbc_ev = 0
            for qc in range(4):
                wl = [(sa, ca)]
                wl.append((sv, psDn_free[1]) if qc == 0 else (sv, cbc_ev))
                b.emit("tensor", lambda e, qc=qc: e.matmul(
                    psA[1][:], ones_rowf[:], crow[0:1, qc * 512: qc * 512 + 512],
                    start=True, stop=True),
                    waits=wl, incs=[(sp, 1)])
                cf = sp.n
                b.emit("vector", lambda e, qc=qc: e.tensor_copy(
                    cbcast[:, qc * 512: qc * 512 + 512], psA[1][:, 0:512]),
                    waits=[(sp, cf)], incs=[(sv, 1)])
                cbc_ev = sv.n
            cb_ready[0] = sv.n
            psDn_free[0] = sv.n
            psDn_free[1] = sv.n

        for half in range(2):
            silu_sa = {}
            first_silu = True
            for mt in range(6):
                for who in range(2):
                    m = (half * 6 + mt) + (0 if who == 0 else 12)
                    slot = upslot[0] % 2
                    upslot[0] += 1
                    b.emit("gpsimd", lambda e, ex=ex, m=m, slot=slot: e.dma_start(
                        upw[slot][:], up_src(ex, m)),
                        waits=[(sp, upw_consumed[slot])], incs=[(sdw, 16)])
                    upw_loaded[slot] = sdw.n
                    banks = (0, 1) if who == 0 else (2, 3)
                    for qc in range(4):
                        bank = banks[qc % 2]
                        wl = [(sdw, upw_loaded[slot]), psU_free[bank], (sv, c_acc0)]
                        for kt in range(8):
                            b.emit("tensor", lambda e, slot=slot, kt=kt, qc=qc, bank=bank: e.matmul(
                                psU[bank], upw[slot][:, kt * 128:(kt + 1) * 128],
                                x1T[:, kt * T + qc * 512: kt * T + qc * 512 + 512],
                                start=(kt == 0), stop=(kt == 7)),
                                waits=wl if kt == 0 else [],
                                incs=[(sp, 1)] if kt == 7 else [])
                        cf = sp.n
                        upw_consumed[slot] = cf
                        if who == 0:
                            swl = [(sp, cf)]
                            if first_silu:
                                swl.append((sp, c_prev_dn[0]))
                                first_silu = False
                            b.emit("scalar", lambda e, mt=mt, qc=qc, bank=bank: e.activation(
                                hbuf[:, mt * T + qc * 512: mt * T + qc * 512 + 512],
                                psU[bank], AF.Silu), waits=swl, incs=[(sa, 1)])
                            silu_sa[(mt, qc)] = sa.n
                            psU_free[bank] = (sa, sa.n)
                        else:
                            if shared:
                                b.emit("vector", lambda e, mt=mt, qc=qc, bank=bank: e.tensor_mul(
                                    hbuf[:, mt * T + qc * 512: mt * T + qc * 512 + 512],
                                    psU[bank],
                                    hbuf[:, mt * T + qc * 512: mt * T + qc * 512 + 512]),
                                    waits=[(sp, cf), (sa, silu_sa[(mt, qc)])],
                                    incs=[(sv, 1)])
                            else:
                                b.emit("vector", lambda e, qc=qc, bank=bank: e.tensor_mul(
                                    utmp[:], psU[bank],
                                    cbcast[:, qc * 512: qc * 512 + 512]),
                                    waits=[(sp, cf), (sv, cb_ready[0])])
                                b.emit("vector", lambda e, mt=mt, qc=qc: e.tensor_mul(
                                    hbuf[:, mt * T + qc * 512: mt * T + qc * 512 + 512],
                                    utmp[:],
                                    hbuf[:, mt * T + qc * 512: mt * T + qc * 512 + 512]),
                                    waits=[(sa, silu_sa[(mt, qc)])],
                                    incs=[(sv, 1)])
                            psU_free[bank] = (sv, sv.n)
            c_h = sv.n

            for mo in range(8):
                slot = mo % 2
                b.emit("gpsimd", lambda e, ex=ex, mo=mo, slot=slot: e.dma_start(
                    dn_slots[slot][:], dn_src(ex, mo)),
                    waits=[(sp, dnw_consumed[slot])], incs=[(sdw, 16)])
                dnw_loaded[slot] = sdw.n
                for qc in range(4):
                    bank = (mo * 4 + qc) % 4
                    wl = [(sdw, dnw_loaded[slot]), (sv, c_h), (sv, psDn_free[bank])]
                    for lt in range(6):
                        kt = half * 6 + lt
                        b.emit("tensor", lambda e, slot=slot, kt=kt, lt=lt, qc=qc, bank=bank: e.matmul(
                            psDn[bank], dn_slots[slot][:, kt * 128:(kt + 1) * 128],
                            hbuf[:, lt * T + qc * 512: lt * T + qc * 512 + 512],
                            start=(lt == 0), stop=(lt == 5)),
                            waits=wl if lt == 0 else [],
                            incs=[(sp, 1)] if lt == 5 else [])
                    cf = sp.n
                    if qc == 3:
                        dnw_consumed[slot] = cf
                    b.emit("vector", lambda e, mo=mo, qc=qc, bank=bank: e.scalar_tensor_tensor(
                        acc[:, mo * T + qc * 512: mo * T + qc * 512 + 512],
                        acc[:, mo * T + qc * 512: mo * T + qc * 512 + 512], 0.0,
                        psDn[bank], op0=ALU.bypass, op1=ALU.add),
                        waits=[(sp, cf)], incs=[(sv, 1)])
                    psDn_free[bank] = sv.n
            c_prev_dn[0] = sp.n

    for ex in range(NE + 1):
        emit_expert(ex)
        if ex == 0:
            tap("hbuf", hbuf, [(sv, sv.n)])
            tap("cbcast", cbcast, [(sv, cb_ready[0])])
    c_ffn_sv = sv.n
    tap("acc", acc, [(sv, c_ffn_sv)])

    tr_evac3 = [0, 0]
    idx = 0
    for t in range(NT):
        for cg in range(2):
            bank = idx % 2
            wl = [(sv, c_ffn_sv)] if idx < 2 else [(sv, tr_evac3[bank])]
            for c4 in range(4):
                c = cg * 4 + c4
                b.emit("tensor", lambda e, t=t, c=c, c4=c4, bank=bank: e.transpose(
                    psP[bank][:, c4 * 128:(c4 + 1) * 128],
                    acc[:, c * T + t * 128: c * T + (t + 1) * 128],
                    ident32[:]), waits=wl if c4 == 0 else [],
                    incs=[(sp, 1)] if c4 == 3 else [])
            cf = sp.n
            b.emit("vector", lambda e, t=t, cg=cg, bank=bank: e.tensor_scalar(
                z_tok[:, t * D + cg * 512: t * D + (cg + 1) * 512], psP[bank][:],
                1.0, 0.0, op0=ALU.mult, op1=ALU.add,
                accum_out=lnz[:, (t * 2 + cg):(t * 2 + cg) + 1]),
                waits=[(sp, cf)], incs=[(sv, 1)])
            tr_evac3[bank] = sv.n
            idx += 1
    c_z = sv.n
    vspace()
    for t in range(NT):
        b.emit("vector", lambda e, t=t: e.reduce_sum(
            sums1[:, t:t + 1], lnz[:, t * 2:(t + 1) * 2], axis=AX.X),
            waits=[(sv, c_z)] if t == 0 else [])
    b.emit("vector", lambda e: e.engine_nop(), incs=[(sv, 1)])
    c_z_sums = sv.n
    for t in range(NT):
        b.emit("scalar", lambda e, t=t: e.activation(
            scr_sq[:], z_tok[:, t * D:(t + 1) * D], AF.Square,
            accum_out=sums2[:, t:t + 1]),
            waits=[(sv, c_z)] if t == 0 else [],
            incs=[(sa, 1)] if t == NT - 1 else [])
    c_z_sq = sa.n
    c_ln3_fin = emit_ln_finish([(sv, c_z_sums), (sa, c_z_sq)])
    for t in range(NT):
        b.emit("scalar", lambda e, t=t: e.activation(
            z_tok[:, t * D:(t + 1) * D], z_tok[:, t * D:(t + 1) * D], AF.Identity,
            scale=rstd[:, t:t + 1], bias=nmr[:, t:t + 1]),
            waits=[(sv, c_ln3_fin)] if t == 0 else [],
            incs=[(sa, 1)] if t == NT - 1 else [])
    c_out_ready = sa.n

    b.emit("sync", lambda e: e.dma_start(hout, z_tok[:]),
           waits=[(sa, c_out_ready)], incs=[(so, 16)])
    tap("z_tok", z_tok, [(sa, c_out_ready)])
    total_so = so.n
    b.emit("gpsimd", lambda e: e.engine_nop(), waits=[(so, total_so)])

    b.finalize(blk)
    est.close()
    return nc


def kernel(L0_attn_X, L0_attn_W_in, L0_attn_b_in, L0_attn_W_out, L0_attn_b_out,
           L0_ffn_gate_w, L0_ffn_w_up, L0_ffn_w_down, L0_ffn_sg_gate,
           L0_ffn_sg_up, L0_ffn_sg_down):
    X = np.asarray(L0_attn_X, dtype=np.float32)
    shared_img = {}
    shared_img["w_in_img"] = _lhsT_slices(np.asarray(L0_attn_W_in, np.float32), 32, 8)
    shared_img["b_in_img"] = np.ascontiguousarray(
        np.asarray(L0_attn_b_in, np.float32).reshape(32, 128).T)
    shared_img["w_out_img"] = _rhs_img(np.ascontiguousarray(
        np.asarray(L0_attn_W_out, np.float32).T))
    shared_img["b_out_row"] = np.ascontiguousarray(
        np.asarray(L0_attn_b_out, np.float32).reshape(1, D))
    shared_img["gate_img"] = _rhs_img(np.ascontiguousarray(
        np.asarray(L0_ffn_gate_w, np.float32).T))
    wup = np.asarray(L0_ffn_w_up, np.float32)
    wdn = np.asarray(L0_ffn_w_down, np.float32)
    shared_img["wup_img"] = np.stack([_lhsT_slices(wup[e], 24, 8) for e in range(NE)])
    shared_img["wdn_img"] = np.stack([_lhsT_slices(wdn[e], 8, 12) for e in range(NE)])
    shared_img["sgg_img"] = _lhsT_slices(np.asarray(L0_ffn_sg_gate, np.float32), 12, 8)
    shared_img["sgu_img"] = _lhsT_slices(np.asarray(L0_ffn_sg_up, np.float32), 12, 8)
    shared_img["sgd_img"] = _lhsT_slices(np.asarray(L0_ffn_sg_down, np.float32), 8, 12)

    nc = build_program()
    in_maps = []
    for c in range(_NUM_CORES):
        d = dict(shared_img)
        d["x_img"] = _img_tok(np.ascontiguousarray(X[c]))
        in_maps.append(d)
    res = run_bass_kernel_spmd(nc, in_maps, core_ids=list(range(_NUM_CORES)))
    out = np.stack([_unimg_tok(res.results[c]["out_img"]) for c in range(_NUM_CORES)])
    if _TAPS:
        kernel._tap_results = [
            {name: res.results[c]["tap_" + name] for (name, _s) in _TAPS}
            for c in range(_NUM_CORES)]
    return out.astype(np.float32)


# version 07
# speedup vs baseline: 1.0762x; 1.0146x over previous
import numpy as np
import ml_dtypes
import concourse.bass as bass
import concourse.mybir as mybir
from concourse.bass_core_rust import BassInstruction
from concourse.bass_utils import run_bass_kernel_spmd

F32 = mybir.dt.float32
F32R = mybir.dt.float32r
BF16 = mybir.dt.bfloat16
AF = mybir.ActivationFunctionType
ALU = mybir.AluOpType
AX = mybir.AxisListType

T = 2048
D = 1024
NT = T // 128
ND = D // 128
H = 16
NE = 8
LN_EPS = 1e-5
SEPS = 1e-4

_TAPS = []
_FEED_IL = ["ppq2vt"]
_NUM_CORES = 8


def _img_tok(x2d):
    return np.ascontiguousarray(
        x2d.reshape(NT, 128, D).transpose(1, 0, 2).reshape(128, NT * D))

def _unimg_tok(img):
    return np.ascontiguousarray(
        img.reshape(128, NT, D).transpose(1, 0, 2).reshape(T, D))

def _lhsT_slices(w, n_m, n_k):
    rows, d = w.shape
    assert rows == n_m * 128 and d == n_k * 128
    a = w.reshape(n_m, 128, n_k, 128)
    return np.ascontiguousarray(a.transpose(0, 3, 2, 1).reshape(n_m, 128, n_k * 128))

def _rhs_img(wT):
    d, cols = wT.shape
    nk = d // 128
    a = wT.reshape(nk, 128, cols)
    return np.ascontiguousarray(a.transpose(1, 0, 2).reshape(128, nk * cols))


class Ctr:
    def __init__(self, handle):
        self.h = handle
        self.n = 0

class B:
    def __init__(self, nc):
        self.nc = nc
        self.streams = {e: [] for e in ("tensor", "vector", "scalar", "gpsimd", "sync")}

    def emit(self, eng, fn, waits=(), incs=()):
        for (c, k) in incs:
            c.n += k
        snap = tuple((c.h, v) for (c, v) in waits if v > 0)
        tgt = tuple((c.h, k) for (c, k) in incs)
        def f(e):
            for (h, v) in snap:
                e.wait_ge(h, v)
            inst = fn(e)
            if tgt:
                assert inst is not None
                for (h, k) in tgt:
                    inst = inst.then_inc(h, k)
        self.streams[eng].append(f)

    def finalize(self, block):
        st = self.streams
        @block.sync
        def _(e):
            for f in st["sync"]: f(e)
        @block.gpsimd
        def _(e):
            for f in st["gpsimd"]: f(e)
        @block.vector
        def _(e):
            for f in st["vector"]: f(e)
        @block.scalar
        def _(e):
            for f in st["scalar"]: f(e)
        @block.tensor
        def _(e):
            for f in st["tensor"]: f(e)


def act_swap(nc, s, out, in_, func, bias=0.0, scale=1.0, accum_out=None):
    inst = s.activation(out, in_, AF.Sqrt, bias=bias, scale=scale, accum_out=accum_out)
    i = inst.ins
    clone = mybir.InstActivation(name=i.name, engine=i.engine, ins=list(i.ins),
                                 outs=list(i.outs), func=func)
    clone.sync_info = i.sync_info
    bb = nc.cur_bb.bb
    assert bb.instructions[-1].name == i.name
    bb.instructions[-1] = clone
    return BassInstruction(clone)


def build_program():
    nc = bass.Bass("TRN2", target_bir_lowering=False, detect_race_conditions=False)

    hx = nc.dram_tensor("x_img", [128, NT * D], F32, kind="ExternalInput").ap()
    hwin = nc.dram_tensor("w_in_img", [32, 128, 1024], F32, kind="ExternalInput").ap()
    hbin = nc.dram_tensor("b_in_img", [128, 32], F32, kind="ExternalInput").ap()
    hwout = nc.dram_tensor("w_out_img", [128, 8 * 1024], F32, kind="ExternalInput").ap()
    hbout = nc.dram_tensor("b_out_row", [1, D], F32, kind="ExternalInput").ap()
    hgate = nc.dram_tensor("gate_img", [128, 64], F32, kind="ExternalInput").ap()
    hwup = nc.dram_tensor("wup_img", [NE, 24, 128, 1024], F32, kind="ExternalInput").ap()
    hwdn = nc.dram_tensor("wdn_img", [NE, 8, 128, 1536], F32, kind="ExternalInput").ap()
    hsgg = nc.dram_tensor("sgg_img", [12, 128, 1024], F32, kind="ExternalInput").ap()
    hsgu = nc.dram_tensor("sgu_img", [12, 128, 1024], F32, kind="ExternalInput").ap()
    hsgd = nc.dram_tensor("sgd_img", [8, 128, 1536], F32, kind="ExternalInput").ap()
    hout = nc.dram_tensor("out_img", [128, NT * D], F32, kind="ExternalOutput").ap()
    htap = {}
    for (name, shape) in _TAPS:
        htap[name] = nc.dram_tensor("tap_" + name, list(shape), F32,
                                    kind="ExternalOutput").ap()

    KB = 1024
    ARENA = 200 * KB
    base0, _end = nc.bump_sbuf(ARENA, align_bytes=256)
    A0, B0, C0, D0, S0 = (base0, base0 + 64 * KB, base0 + 128 * KB,
                          base0 + 168 * KB, base0 + 184 * KB)

    def at(name, off, shape, dtype):
        return nc.alloc_sbuf_tensor_at(name, shape, dtype, offset=off).ap()

    X = at("X", A0, [128, NT * D], F32)
    qaug = [at(f"qaug{i}", A0 + i * 4 * KB, [66, T], BF16) for i in range(2)]
    kaug = [at(f"kaug{i}", A0 + 8 * KB + i * 4 * KB, [66, T], BF16) for i in range(2)]
    vT = at("vT", A0 + 16 * KB, [64, T], F32)
    gTb = [at(f"gTb{i}", A0 + 24 * KB + i * 4 * KB, [64, T], BF16) for i in range(2)]
    vprime = [at(f"vp{i}", A0 + 32 * KB + i * 5 * KB, [128, 16 * 65], BF16) for i in range(2)]
    qsq = at("qsq", A0 + 42 * KB, [64, 512], F32)
    rec = [at(f"rec{i}", A0 + 44 * KB + i * 2 * KB, [64, 512], F32) for i in range(2)]
    normt = [at(f"normt{i}", A0 + 48 * KB + i * 2 * KB, [64, 512], F32) for i in range(2)]
    rowt2 = at("rowt2", A0 + 52 * KB, [1, 512], F32)
    qsq2 = at("qsq2", A0 + 54 * KB, [64, 512], F32)
    astg = [at(f"astg{i}", A0 + 56 * KB + i * 2 * KB, [65, 512], F32) for i in range(4)]
    r_tok = at("r_tok", A0, [128, NT * D], F32)
    x1T = at("x1T", A0, [128, ND * T], F32R)
    z_tok = at("z_tok", A0, [128, NT * D], F32)

    xn_tok = at("xn_tok", B0, [128, NT * D], BF16)
    xnT = at("xnT", B0 + 32 * KB, [128, ND * T], BF16)
    u_s = [at(f"u{i}", B0 + i * 2 * KB, [128, 1024], BF16) for i in range(4)]
    uu_s = [at(f"uu{i}", B0 + 8 * KB + i * 2 * KB, [128, 1024], BF16) for i in range(4)]
    wring = [at(f"wr{i}", B0 + 16 * KB + i * 2 * KB, [128, 1024], BF16) for i in range(4)]
    X_re = at("X_re", B0, [128, NT * D], F32)
    x1_tok = at("x1_tok", B0, [128, NT * D], F32)
    acc = at("acc", B0, [128, ND * T], F32)

    OT = at("OT", C0, [128, 9 * T], BF16)
    hbuf = at("hbuf", C0, [128, 6 * T], BF16)
    cbcast = at("cbcast", C0 + 24 * KB, [128, T], F32)
    crow = at("crow", C0 + 32 * KB, [1, T], F32)

    winw = [at(f"winw{i}", D0 + i * 2 * KB, [128, 1024], BF16) for i in range(8)]
    woutb = at("woutb", D0, [128, 8 * 1024], BF16)
    upw = [at(f"upw{i}", D0 + i * 4 * KB, [128, 1024], F32R) for i in range(2)]
    dnw0 = at("dnw0", D0 + 8 * KB, [128, 1536], BF16)
    dn2 = at("dn2", D0 + 11 * KB, [128, 1536], BF16)

    o = S0
    ident32 = at("ident32", o, [128, 128], F32); o += 512
    identbf = at("identbf", o, [128, 128], BF16); o += 256
    ones_col = at("ones_col", o, [128, 1], F32); o += 32
    q2lhs = at("q2lhs", o, [64, 2], F32); o += 32
    k2lhs = at("k2lhs", o, [64, 2], F32); o += 32
    bias_q = at("bias_q", o, [2, 1], F32); o += 32
    bias_k = at("bias_k", o, [2, 1], F32); o += 32
    eps_col = at("eps_col", o, [128, 1], F32); o += 32
    seps1 = at("seps1", o, [1, 1], F32); o += 32
    binsb = at("binsb", o, [128, 32], F32); o += 128
    binsb_n2 = at("binsb_n2", o, [128, 32], F32); o += 128
    ones_rowf = at("ones_rowf", o, [1, 128], F32); o += 512
    boutr = at("boutr", o, [1, D], BF16); o += 2 * KB
    gatesb = at("gatesb", o, [128, 64], F32R); o += 256
    iotar = at("iotar", o, [128, 8], F32); o += 32
    sums1 = at("sums1", o, [128, NT], F32); o += 64
    sums1b = at("sums1b", o, [128, NT], F32); o += 64
    sums2 = at("sums2", o, [128, NT], F32); o += 64
    mu = at("mu", o, [128, NT], F32); o += 64
    var_ = at("var", o, [128, NT], F32); o += 64
    rstd = at("rstd", o, [128, NT], F32); o += 64
    nmr = at("nmr", o, [128, NT], F32); o += 64
    scr_sq = at("scr_sq", o, [128, D], F32); o += 4 * KB
    logits = at("logits", o, [128, NT * 8], F32); o += 512
    comb = at("comb", o, [128, NT * 8], F32); o += 512
    gm1 = at("gm1", o, [128, NT], F32); o += 64
    gm2 = at("gm2", o, [128, NT], F32); o += 64
    gmi = at("gmi", o, [128, NT], F32); o += 64
    gmi2 = at("gmi2", o, [128, NT], F32); o += 64
    gw1 = at("gw1", o, [128, NT], F32); o += 64
    gw2 = at("gw2", o, [128, NT], F32); o += 64
    gdm = at("gdm", o, [128, NT], F32); o += 64
    geqA = at("geqA", o, [128, 128], F32); o += 512
    geq1 = at("geq1", o, [128, 128], F32); o += 512
    geq2 = at("geq2", o, [128, 128], F32); o += 512
    grow2 = at("grow2", o, [128, 128], F32); o += 512
    gtmp = at("gtmp", o, [128, 128], F32); o += 512
    utmp = at("utmp", o, [128, 512], F32); o += 2 * KB
    lnz = at("lnz", o, [128, NT * ND], F32); o += 512
    assert o - S0 <= 16 * KB, f"small region overflow: {o - S0}"

    psS = [nc.alloc_psum_tensor(f"psS{i}", [128, 1024], F32).ap() for i in range(2)]
    psA = [nc.alloc_psum_tensor(f"psA{i}", [128, 512], F32).ap() for i in range(2)]
    psP = [nc.alloc_psum_tensor(f"psP{i}", [128, 512], F32).ap() for i in range(2)]
    psPbf = [p.bitcast(BF16) for p in psP]
    psU = [psS[0][:, 0:512], psS[0][:, 512:1024], psS[1][:, 0:512], psS[1][:, 512:1024]]
    psDn = [psA[0], psA[1], psP[0], psP[1]]

    import contextlib
    est = contextlib.ExitStack()
    sems = {}
    for name in ("sdin", "sdw", "sp", "sa", "sv", "sg", "so"):
        sems[name] = Ctr(est.enter_context(nc.semaphore(name)))
    sdin, sdw, sp, sa, sv, sg, so = (sems[k] for k in
                                     ("sdin", "sdw", "sp", "sa", "sv", "sg", "so"))
    blk = est.enter_context(nc.Block())
    b = B(nc)
    tap_names = {name: shape for (name, shape) in _TAPS}

    def tap(name, src, waits):
        if name not in tap_names:
            return
        shape = tap_names[name]
        b.emit("gpsimd", lambda e, name=name, src=src, shape=shape: e.dma_start(
            htap[name], src[0:shape[0], 0:shape[1]]),
            waits=waits, incs=[(so, 16)])

    b.emit("sync", lambda e: e.dma_start(X[:], hx), incs=[(sdin, 16)])
    c_X = sdin.n
    b.emit("sync", lambda e: e.dma_start(binsb[:], hbin), incs=[(sdin, 16)])
    c_bin = sdin.n
    tap("X", X, [(sdin, c_X)])
    b.emit("gpsimd", lambda e: e.dma_start(gatesb[:], hgate), incs=[(sdw, 16)])
    c_gate = sdw.n
    b.emit("gpsimd", lambda e: e.dma_start(boutr[:], hbout), incs=[(sdw, 16)])
    c_bout = sdw.n

    def _gsetup(e):
        e.memset(ident32[:], 0.0)
        nc.gpsimd.affine_select(out=ident32[:], in_=ident32[:], compare_op=ALU.not_equal,
                                fill=1.0, base=0, pattern=[[-1, 128]], channel_multiplier=1)
        e.memset(identbf[:], 0.0)
        nc.gpsimd.affine_select(out=identbf[:], in_=identbf[:], compare_op=ALU.not_equal,
                                fill=1.0, base=0, pattern=[[-1, 128]], channel_multiplier=1)
        e.memset(ones_col[:], 1.0)
        e.memset(ones_rowf[:], 1.0)
        e.memset(q2lhs[:], 0.0)
        e.memset(q2lhs[:, 0:1], 0.25)
        e.memset(k2lhs[:], 0.0)
        e.memset(k2lhs[:, 1:2], 1.0)
        e.iota(bias_q[:], pattern=[[0, 1]], base=0, channel_multiplier=1,
               allow_small_or_imprecise_dtypes=True)
        e.iota(bias_k[:], pattern=[[0, 1]], base=1, channel_multiplier=-1,
               allow_small_or_imprecise_dtypes=True)
        e.memset(eps_col[:], LN_EPS)
        e.memset(seps1[:], SEPS)
        e.memset(OT[0:1, 8 * T:9 * T], 1.0)
        return nc.gpsimd.iota(iotar[:], pattern=[[-1, 8]], base=8, channel_multiplier=0,
                              allow_small_or_imprecise_dtypes=True)
    b.emit("gpsimd", _gsetup, incs=[(sg, 1)])
    c_consts = sg.n

    win_loaded = [0] * 32
    for m in range(8):
        b.emit("gpsimd", lambda e, m=m: e.dma_start(winw[m % 8][:], hwin[m]),
               incs=[(sdw, 16)])
        win_loaded[m] = sdw.n

    b.emit("vector", lambda e: e.tensor_scalar_mul(binsb_n2[:], binsb[:], -2.0),
           waits=[(sdin, c_bin)], incs=[(sv, 1)])

    def vspace():
        b.emit("vector", lambda e: e.tensor_scalar_mul(scr_sq[:, 0:512],
                                                       scr_sq[:, 0:512], 1.0))

    def emit_ln_finish(waits):
        b.emit("vector", lambda e: e.tensor_scalar_mul(mu[:], sums1[:], 1.0 / D),
               waits=waits)
        b.emit("vector", lambda e: e.tensor_scalar_mul(var_[:], sums2[:], 1.0 / D))
        vspace()
        b.emit("vector", lambda e: e.tensor_mul(nmr[:], mu[:], mu[:]))
        vspace()
        b.emit("vector", lambda e: e.tensor_sub(var_[:], var_[:], nmr[:]), incs=[(sv, 1)])
        cv = sv.n
        b.emit("scalar", lambda e: act_swap(nc, e, rstd[:], var_[:], AF.Rsqrt,
                                            bias=eps_col[:, 0:1]),
               waits=[(sv, cv), (sg, c_consts)], incs=[(sa, 1)])
        ca = sa.n
        b.emit("vector", lambda e: e.scalar_tensor_tensor(
            nmr[:], mu[:], -1.0, rstd[:], op0=ALU.mult, op1=ALU.mult),
            waits=[(sa, ca)], incs=[(sv, 1)])
        return sv.n

    for t in range(NT):
        b.emit("vector", lambda e, t=t: e.reduce_sum(
            sums1[:, t:t + 1], X[:, t * D:(t + 1) * D], axis=AX.X),
            waits=[(sdin, c_X)] if t == 0 else [])
    b.emit("vector", lambda e: e.engine_nop(), incs=[(sv, 1)])
    c_ln1_sums = sv.n
    for t in range(NT):
        b.emit("scalar", lambda e, t=t: e.activation(
            scr_sq[:], X[:, t * D:(t + 1) * D], AF.Square,
            accum_out=sums2[:, t:t + 1]),
            waits=[(sdin, c_X)] if t == 0 else [],
            incs=[(sa, 1)] if t == NT - 1 else [])
    c_ln1_sq = sa.n
    c_ln1_fin = emit_ln_finish([(sv, c_ln1_sums), (sa, c_ln1_sq)])
    tap("mu1", mu, [(sv, c_ln1_fin)])
    tap("rstd1", rstd, [(sv, c_ln1_fin)])
    tap("s1", sums1, [(sv, c_ln1_sums)])
    tap("s2", sums2, [(sa, c_ln1_sq)])
    for t in range(NT):
        b.emit("scalar", lambda e, t=t: e.activation(
            xn_tok[:, t * D:(t + 1) * D], X[:, t * D:(t + 1) * D], AF.Identity,
            scale=rstd[:, t:t + 1], bias=nmr[:, t:t + 1]),
            waits=[(sv, c_ln1_fin)] if t == 0 else [],
            incs=[(sa, 1)] if t == NT - 1 else [])
    c_ln1_apply = sa.n
    tap("xn_tok", xn_tok, [(sa, c_ln1_apply)])

    def _vp_ones(e):
        last = None
        for i in range(2):
            for kt in range(NT):
                last = e.activation(vprime[i][:, kt * 65 + 64: kt * 65 + 65],
                                    ones_col[:], AF.Identity)
        return last
    b.emit("scalar", _vp_ones, waits=[(sg, c_consts)], incs=[(sa, 1)])
    c_vpones = sa.n

    tr_evac = [0, 0]
    for t in range(NT):
        bank = t % 2
        wl = [(sa, c_ln1_apply), (sg, c_consts)] if t < 2 else [(sv, tr_evac[bank])]
        for c in range(ND):
            b.emit("tensor", lambda e, t=t, c=c, bank=bank: e.transpose(
                psPbf[bank][:, c * 128:(c + 1) * 128],
                xn_tok[:, t * D + c * 128: t * D + (c + 1) * 128],
                identbf[:]), waits=wl if c == 0 else [],
                incs=[(sp, 1)] if c == ND - 1 else [])
        cf = sp.n
        dst = xnT.rearrange("p (c y) -> p c y", c=ND)[:, :, t * 128:(t + 1) * 128]
        b.emit("vector", lambda e, dst=dst, bank=bank: e.tensor_copy(
            dst, psPbf[bank][:].rearrange("p (c x) -> p c x", c=ND)),
            waits=[(sp, cf)], incs=[(sv, 1)])
        tr_evac[bank] = sv.n
    c_xnT = sv.n
    tap("xnT", xnT, [(sv, c_xnT)])

    pP_state = [(sv, tr_evac[0]), (sv, tr_evac[1])]
    prep_done = [None] * H
    main_sv = [0] * H
    psA_free = [0, 0]
    psS_free = [0, 0]
    psS0_borrow = [0]
    wring_free = [0, 0, 0, 0]
    uslot_free = [0, 0, 0, 0]
    last_phead_mm = [0]
    last_q2mm = [0]
    wready = {}

    def emit_prep(h):
        thunks = []
        par = h % 2
        ready = {}
        qsq_mm = [0, 0]
        mi0_evac = [0]

        def pproj(mi, qc):
            c_prev_main = main_sv[h - 2] if h >= 2 else 0
            m = 2 * h + mi
            slot = m % 8
            bank = mi
            wl = [(sdw, win_loaded[m]), (sv, c_xnT), pP_state[bank]]
            for kt in range(8):
                b.emit("tensor", lambda e, qc=qc, kt=kt, slot=slot, bank=bank: e.matmul(
                    psP[bank][:],
                    winw[slot][:, kt * 128:(kt + 1) * 128],
                    xnT[:, kt * T + qc * 512: kt * T + qc * 512 + 512],
                    start=(kt == 0), stop=(kt == 7)),
                    waits=wl if kt == 0 else [],
                    incs=[(sp, 1)] if kt == 7 else [])
            cfill = sp.n
            last_phead_mm[0] = cfill
            if mi == 0:
                b.emit("scalar", lambda e, par=par, qc=qc, m=m: e.activation(
                    qaug[par][0:64, qc * 512: qc * 512 + 512], psP[0][0:64, :],
                    AF.Identity, scale=-2.0, bias=binsb_n2[0:64, m:m + 1]),
                    waits=[(sp, cfill), (sv, c_prev_main)])
                b.emit("scalar", lambda e, par=par, qc=qc, m=m: e.activation(
                    kaug[par][0:64, qc * 512: qc * 512 + 512], psP[0][64:128, :],
                    AF.Identity, bias=binsb[64:128, m:m + 1]), incs=[(sa, 1)])
                evacd[qc] = sa.n
            else:
                b.emit("scalar", lambda e, qc=qc, m=m: e.activation(
                    vT[0:64, qc * 512: qc * 512 + 512], psP[1][0:64, :],
                    AF.Identity, bias=binsb[0:64, m:m + 1]),
                    waits=[(sp, cfill), (sv, c_prev_main)])
                b.emit("scalar", lambda e, par=par, qc=qc, m=m: e.activation(
                    gTb[par][0:64, qc * 512: qc * 512 + 512], psP[1][64:128, :],
                    AF.Identity, bias=binsb[64:128, m:m + 1]), incs=[(sa, 1)])
            pP_state[bank] = (sa, sa.n)
            if qc == 3 and m + 8 < 32:
                cm = sp.n
                b.emit("gpsimd", lambda e, m=m, slot=slot: e.dma_start(
                    winw[slot][:], hwin[m + 8]), waits=[(sp, cm)], incs=[(sdw, 16)])
                win_loaded[m + 8] = sdw.n

        evacd = {}
        qsq_sv = {}
        q2mm_sp = {}

        def qsq_op(k):
            which, qc = divmod(k, 4)
            src_ = qaug[par] if which == 0 else kaug[par]
            qb = k % 2
            qbuf = qsq if qb == 0 else qsq2
            b.emit("vector", lambda e, src_=src_, qc=qc, qbuf=qbuf: e.tensor_mul(
                qbuf[:], src_[0:64, qc * 512: qc * 512 + 512],
                src_[0:64, qc * 512: qc * 512 + 512]),
                waits=[(sa, evacd[qc]), (sp, qsq_mm[qb])],
                incs=[(sv, 1)])
            qsq_sv[k] = sv.n

        def q2mm_op(k):
            which, qc = divmod(k, 4)
            lhs = q2lhs if which == 0 else k2lhs
            qb = k % 2
            qbuf = qsq if qb == 0 else qsq2
            b.emit("tensor", lambda e, lhs=lhs, qbuf=qbuf: e.matmul(
                psP[0][0:2, :], lhs[:], qbuf[:], start=True, stop=True),
                waits=[(sv, qsq_sv[k]), pP_state[0], (sg, c_consts)], incs=[(sp, 1)])
            qsq_mm[qb] = sp.n
            q2mm_sp[k] = sp.n
            last_q2mm[0] = sp.n

        def q2evac_op(k):
            which, qc = divmod(k, 4)
            src_ = qaug[par] if which == 0 else kaug[par]
            bias = bias_q if which == 0 else bias_k
            b.emit("scalar", lambda e, src_=src_, qc=qc, bias=bias: e.activation(
                src_[64:66, qc * 512: qc * 512 + 512], psP[0][0:2, :], AF.Identity,
                bias=bias[:, 0:1]), waits=[(sp, q2mm_sp[k])], incs=[(sa, 1)])
            pP_state[0] = (sa, sa.n)

        def vtrans(grp):
            wl0 = [pP_state[1], (sa, c_vpones)]
            for k8 in range(8):
                kt = grp * 8 + k8
                b.emit("tensor", lambda e, kt=kt, k8=k8: e.transpose(
                    psP[1][:, k8 * 64:(k8 + 1) * 64],
                    vT[0:64, kt * 128:(kt + 1) * 128], ident32[0:64, 0:64]),
                    waits=wl0 if k8 == 0 else [],
                    incs=[(sp, 1)] if k8 == 7 else [])

        def vtrans_evac(grp):
            cf = sp.n if False else vt_fill[grp]
            dst = vprime[par].rearrange("p (k y) -> p k y", k=16)[
                :, grp * 8:(grp + 1) * 8, 0:64]
            b.emit("vector", lambda e, dst=dst: e.tensor_copy(
                dst, psP[1][:].rearrange("p (k y) -> p k y", k=8)),
                waits=[(sp, vt_fill[grp])], incs=[(sv, 1)])
            pP_state[1] = (sv, sv.n)

        vt_fill = {}

        def vtrans_fill(grp):
            vtrans(grp)
            vt_fill[grp] = sp.n

        def finalize():
            ready["aug"] = sa.n
            ready["vp"] = sv.n
            ready["g"] = sa.n
            prep_done[h] = ready
            if h == 0:
                tap("qaug0", qaug[0], [(sa, ready["aug"])])
                tap("kaug0", kaug[0], [(sa, ready["aug"])])
                tap("vp0", vprime[0], [(sv, ready["vp"])])

        def TH(*fns, kind="pp"):
            t = lambda fns=fns: [f() for f in fns]
            t.kind = kind
            thunks.append(t)
        TH(lambda: pproj(0, 0))
        TH(lambda: pproj(0, 1))
        TH(lambda: pproj(0, 2))
        TH(lambda: pproj(0, 3), lambda: qsq_op(0), kind="q2")
        TH(lambda: qsq_op(1), lambda: q2mm_op(0), kind="q2")
        TH(lambda: qsq_op(2), lambda: q2evac_op(0), lambda: q2mm_op(1), kind="q2")
        TH(lambda: qsq_op(3), lambda: q2evac_op(1), lambda: q2mm_op(2), kind="q2")
        TH(lambda: qsq_op(4), lambda: q2evac_op(2), lambda: q2mm_op(3), kind="q2")
        TH(lambda: qsq_op(5), lambda: q2evac_op(3), lambda: q2mm_op(4), kind="q2")
        TH(lambda: qsq_op(6), lambda: q2evac_op(4), lambda: q2mm_op(5), kind="q2")
        TH(lambda: qsq_op(7), lambda: q2evac_op(5), lambda: q2mm_op(6), kind="q2")
        TH(lambda: q2evac_op(6), lambda: q2mm_op(7), kind="q2")
        TH(lambda: q2evac_op(7), kind="q2")
        TH(lambda: pproj(1, 0))
        TH(lambda: pproj(1, 1))
        TH(lambda: pproj(1, 2))
        TH(lambda: pproj(1, 3))
        TH(lambda: vtrans_fill(0), kind="vt")
        TH(lambda: vtrans_evac(0), kind="vt")
        TH(lambda: vtrans_fill(1), kind="vt")
        TH(lambda: vtrans_evac(1), finalize, kind="vt")
        return thunks

    LAG = 3

    def emit_S_EW(h, g):
        par = h % 2
        rd = prep_done[h]
        qc2, kt = divmod(g, NT)
        i = g % 2
        slot2 = g % 4
        for half in range(2):
            wl = []
            if half == 0:
                wl = [(sa, rd["aug"]), (sa, psS_free[i])]
                if i == 0:
                    wl.append((sa, psS0_borrow[0]))
            b.emit("tensor", lambda e, par=par, kt=kt, qc2=qc2, half=half, i=i: e.matmul(
                psS[i][:, half * 512:(half + 1) * 512],
                kaug[par][:, kt * 128:(kt + 1) * 128],
                qaug[par][:, qc2 * 1024 + half * 512: qc2 * 1024 + (half + 1) * 512],
                start=True, stop=True),
                waits=wl, incs=[(sp, 1)] if half == 1 else [])
        cS = sp.n
        b.emit("scalar", lambda e, i=i, slot2=slot2: act_swap(
            nc, e, u_s[slot2][:], psS[i][:], AF.Rsqrt),
            waits=[(sp, cS), (sv, uslot_free[slot2])], incs=[(sa, 1)])
        psS_free[i] = sa.n
        cu = sa.n
        if g % 2 == 0:
            b.emit("scalar", lambda e, slot2=slot2: e.activation(
                uu_s[slot2][:], u_s[slot2][:], AF.Square), incs=[(sa, 1)])
            cuu = (sa, sa.n)
        else:
            b.emit("vector", lambda e, slot2=slot2: e.tensor_mul(
                uu_s[slot2][:], u_s[slot2][:], u_s[slot2][:]),
                waits=[(sa, cu)], incs=[(sv, 1)])
            cuu = (sv, sv.n)
        b.emit("vector", lambda e, slot2=slot2: e.tensor_scalar(
            u_s[slot2][:], u_s[slot2][:], -2.0 * SEPS, 1.0,
            op0=ALU.mult, op1=ALU.add), waits=[cuu])
        b.emit("vector", lambda e, slot2=slot2, wslot=g % 4: e.tensor_mul(
            wring[wslot][:], uu_s[slot2][:], u_s[slot2][:]),
            waits=[(sp, wring_free[g % 4])], incs=[(sv, 1)])
        uslot_free[slot2] = sv.n
        wready[g] = sv.n

    def emit_A(h, g):
        par = h % 2
        rd = prep_done[h]
        qc2, kt = divmod(g, NT)
        wslot = g % 4
        for j in range(2):
            wl = [(sv, wready[g])]
            if kt == 0:
                wl += [(sv, rd["vp"]), (sv, psA_free[j])]
            b.emit("tensor", lambda e, kt=kt, j=j, par=par, wslot=wslot: e.matmul(
                psA[j][0:65, :], vprime[par][:, kt * 65: kt * 65 + 65],
                wring[wslot][:, j * 512:(j + 1) * 512],
                start=(kt == 0), stop=(kt == NT - 1)),
                waits=wl, incs=[(sp, 1)] if j == 1 else [])
        wring_free[wslot] = sp.n

    def emit_norm(h, qc2):
        par = h % 2
        rd = prep_done[h]
        cA = sp.n
        for j in range(2):
            qc = qc2 * 2 + j
            stg = astg[qc2 * 2 + j]
            b.emit("vector", lambda e, j=j, stg=stg: e.tensor_copy(
                stg[:], psA[j][0:65, :]), waits=[(sp, cA)], incs=[(sv, 1)])
            psA_free[j] = sv.n
            cst = sv.n
            b.emit("scalar", lambda e, stg=stg: e.activation(
                rowt2[:], stg[64:65, :], AF.Square, bias=seps1[:, 0:1]),
                waits=[(sv, cst)], incs=[(sa, 1)])
            crr = sa.n
            b.emit("tensor", lambda e: e.matmul(
                psS[0][0:64, 0:512], ones_rowf[0:1, 0:64], rowt2[:],
                start=True, stop=True),
                waits=[(sa, crr), (sa, psS_free[0])], incs=[(sp, 1)])
            cb = sp.n
            b.emit("scalar", lambda e, j=j: act_swap(
                nc, e, rec[j][:], psS[0][0:64, 0:512], AF.Rsqrt),
                waits=[(sp, cb)], incs=[(sa, 1)])
            psS0_borrow[0] = sa.n
            cre = sa.n
            b.emit("vector", lambda e, j=j, stg=stg: e.tensor_mul(
                normt[j][:], stg[0:64, :], rec[j][:]),
                waits=[(sa, cre)])
            b.emit("vector", lambda e, j=j, qc=qc, par=par, h=h: e.tensor_mul(
                OT[(h % 2) * 64:(h % 2) * 64 + 64,
                   (h // 2) * T + qc * 512:(h // 2) * T + qc * 512 + 512],
                normt[j][:], gTb[par][0:64, qc * 512: qc * 512 + 512]),
                waits=[(sa, rd["g"])], incs=[(sv, 1)])

    def emit_main(h, feed=()):
        feed = list(feed)
        wready.clear()
        for g in range(2 * NT):
            emit_S_EW(h, g)
            if feed and g >= 2 and getattr(feed[0], "kind", "pp") in _FEED_IL[0]:
                feed.pop(0)()
            if g >= LAG:
                emit_A(h, g - LAG)
                if (g - LAG) % NT == NT - 1:
                    emit_norm(h, (g - LAG) // NT)
        for g in range(2 * NT - LAG, 2 * NT):
            emit_A(h, g)
            if g % NT == NT - 1:
                emit_norm(h, g // NT)
        while feed:
            feed.pop(0)()
        main_sv[h] = sv.n

    for t in emit_prep(0):
        t()
    for h in range(H):
        feed = emit_prep(h + 1) if h + 1 < H else []
        emit_main(h, feed)
    c_heads_sv = sv.n
    c_heads_sa = sa.n
    tap("OT", OT, [(sv, c_heads_sv), (sg, c_consts)])

    b.emit("gpsimd", lambda e: e.dma_start(woutb[:], hwout),
           waits=[(sp, last_phead_mm[0])], incs=[(sdw, 16)])
    c_wout = sdw.n
    c_Xre_t = []
    for t in range(NT):
        b.emit("sync", lambda e, t=t: e.dma_start(
            X_re[:, t * D:(t + 1) * D], hx[:, t * D:(t + 1) * D]),
            waits=[(sv, c_heads_sv), (sa, c_heads_sa)] if t == 0 else [],
            incs=[(sdin, 16)])
        c_Xre_t.append(sdin.n)
    c_Xre = sdin.n

    x2banks = [psS[0][:, 0:512], psS[0][:, 512:1024], psS[1][:, 0:512], psS[1][:, 512:1024]]
    x2_free = [c_heads_sv] * 4
    idx = 0
    for t in range(NT):
        for c2 in range(2):
            bank = idx % 4
            wl = [(sv, x2_free[bank]), (sdw, c_wout)]
            if idx == 0:
                wl += [(sa, c_heads_sa), (sg, c_consts)]
            for kt in range(8):
                b.emit("tensor", lambda e, t=t, c2=c2, kt=kt, bank=bank: e.matmul(
                    x2banks[bank], OT[:, kt * T + t * 128: kt * T + (t + 1) * 128],
                    woutb[:, kt * 1024 + c2 * 512: kt * 1024 + c2 * 512 + 512],
                    start=(kt == 0), stop=False),
                    waits=wl if kt == 0 else [])
            b.emit("tensor", lambda e, t=t, c2=c2, bank=bank: e.matmul(
                x2banks[bank], OT[0:1, 8 * T + t * 128: 8 * T + (t + 1) * 128],
                boutr[0:1, c2 * 512: c2 * 512 + 512],
                start=False, stop=True),
                waits=[(sdw, c_bout)] if idx == 0 else [], incs=[(sp, 1)])
            cf = sp.n
            sumcol = sums1 if c2 == 0 else sums1b
            b.emit("vector", lambda e, t=t, c2=c2, bank=bank, sumcol=sumcol: e.scalar_tensor_tensor(
                r_tok[:, t * D + c2 * 512: t * D + c2 * 512 + 512],
                X_re[:, t * D + c2 * 512: t * D + c2 * 512 + 512], 2.0,
                x2banks[bank], op0=ALU.mult, op1=ALU.add,
                accum_out=sumcol[:, t:t + 1]),
                waits=[(sp, cf)] + ([(sdin, c_Xre_t[t])] if c2 == 0 else []),
                incs=[(sv, 1)])
            x2_free[bank] = sv.n
            if c2 == 1:
                b.emit("scalar", lambda e, t=t: e.activation(
                    scr_sq[:], r_tok[:, t * D:(t + 1) * D], AF.Square,
                    accum_out=sums2[:, t:t + 1]),
                    waits=[(sv, sv.n)], incs=[(sa, 1)])
            idx += 1
    c_x2_mm = sp.n
    vspace()
    b.emit("vector", lambda e: e.tensor_add(sums1[:], sums1[:], sums1b[:]), incs=[(sv, 1)])
    c_r_done = sv.n
    tap("r_tok", r_tok, [(sv, c_r_done)])
    c_r_sq = sa.n
    c_ln2_fin = emit_ln_finish([(sv, c_r_done), (sa, c_r_sq)])
    for t in range(NT):
        b.emit("scalar", lambda e, t=t: e.activation(
            x1_tok[:, t * D:(t + 1) * D], r_tok[:, t * D:(t + 1) * D], AF.Identity,
            scale=rstd[:, t:t + 1], bias=nmr[:, t:t + 1]),
            waits=[(sv, c_ln2_fin), (sdin, c_Xre)] if t == 0 else [],
            incs=[(sa, 1)] if t == NT - 1 else [])
    c_x1_apply = sa.n
    tap("x1_tok", x1_tok, [(sa, c_x1_apply)])

    tr_evac2 = [0, 0]
    idx = 0
    x1T_4d = x1T.rearrange("p (c y) -> p c y", c=ND)
    for t in range(NT):
        for cg in range(2):
            bank = idx % 2
            wl = [(sa, c_x1_apply)] if idx < 2 else [(sv, tr_evac2[bank])]
            for c4 in range(4):
                c = cg * 4 + c4
                b.emit("tensor", lambda e, t=t, c=c, c4=c4, bank=bank: e.transpose(
                    psP[bank][:, c4 * 128:(c4 + 1) * 128],
                    x1_tok[:, t * D + c * 128: t * D + (c + 1) * 128],
                    ident32[:]), waits=wl if c4 == 0 else [],
                    incs=[(sp, 1)] if c4 == 3 else [])
            cf = sp.n
            dst = x1T_4d[:, cg * 4:(cg + 1) * 4, t * 128:(t + 1) * 128]
            b.emit("vector", lambda e, dst=dst, bank=bank: e.tensor_copy(
                dst, psP[bank][:].rearrange("p (c x) -> p c x", c=4)),
                waits=[(sp, cf)], incs=[(sv, 1)])
            tr_evac2[bank] = sv.n
            idx += 1
    c_x1T = sv.n
    tap("x1T", x1T, [(sv, c_x1T)])

    glog_free = [c_x1T, c_x1T]
    for t in range(NT):
        bank = psA[t % 2]
        wl = [(sv, glog_free[t % 2]), (sdw, c_gate)]
        for kt in range(8):
            b.emit("tensor", lambda e, t=t, kt=kt, bank=bank: e.matmul(
                bank[:, 0:8], x1T[:, kt * T + t * 128: kt * T + (t + 1) * 128],
                gatesb[:, kt * 8:(kt + 1) * 8],
                start=(kt == 0), stop=(kt == 7)),
                waits=wl if kt == 0 else [], incs=[(sp, 1)] if kt == 7 else [])
        cf = sp.n
        b.emit("vector", lambda e, t=t, bank=bank: e.tensor_copy(
            logits[:, t * 8:(t + 1) * 8], bank[:, 0:8]), waits=[(sp, cf)],
            incs=[(sv, 1)])
        glog_free[t % 2] = sv.n
    c_logits = sv.n
    tap("logits", logits, [(sv, c_logits)])

    vspace()
    for t in range(NT):
        b.emit("vector", lambda e, t=t: e.reduce_max(
            gm1[:, t:t + 1], logits[:, t * 8:(t + 1) * 8], axis=AX.X),
            waits=[(sv, c_logits), (sg, c_consts)] if t == 0 else [])
    for t in range(NT):
        b.emit("vector", lambda e, t=t: e.tensor_scalar(
            geqA[:, t * 8:(t + 1) * 8], logits[:, t * 8:(t + 1) * 8],
            gm1[:, t:t + 1], None, op0=ALU.is_equal))
    for t in range(NT):
        b.emit("vector", lambda e, t=t: e.scalar_tensor_tensor(
            grow2[:, t * 8:(t + 1) * 8], geqA[:, t * 8:(t + 1) * 8], -1e30,
            logits[:, t * 8:(t + 1) * 8], op0=ALU.mult, op1=ALU.add))
    for t in range(NT):
        b.emit("vector", lambda e, t=t: e.reduce_max(
            gm2[:, t:t + 1], grow2[:, t * 8:(t + 1) * 8], axis=AX.X))
    for t in range(NT):
        b.emit("vector", lambda e, t=t: e.tensor_mul(
            gtmp[:, t * 8:(t + 1) * 8], geqA[:, t * 8:(t + 1) * 8], iotar[:]))
    for t in range(NT):
        b.emit("vector", lambda e, t=t: e.reduce_max(
            gmi[:, t:t + 1], gtmp[:, t * 8:(t + 1) * 8], axis=AX.X))
    for t in range(NT):
        b.emit("vector", lambda e, t=t: e.tensor_scalar(
            geq1[:, t * 8:(t + 1) * 8], iotar[:], gmi[:, t:t + 1], None,
            op0=ALU.is_equal))
    for t in range(NT):
        b.emit("vector", lambda e, t=t: e.tensor_scalar(
            geq2[:, t * 8:(t + 1) * 8], grow2[:, t * 8:(t + 1) * 8],
            gm2[:, t:t + 1], None, op0=ALU.is_equal))
    for t in range(NT):
        b.emit("vector", lambda e, t=t: e.tensor_mul(
            gtmp[:, t * 8:(t + 1) * 8], geq2[:, t * 8:(t + 1) * 8], iotar[:]))
    for t in range(NT):
        b.emit("vector", lambda e, t=t: e.reduce_max(
            gmi2[:, t:t + 1], gtmp[:, t * 8:(t + 1) * 8], axis=AX.X))
    for t in range(NT):
        b.emit("vector", lambda e, t=t: e.tensor_scalar(
            geq2[:, t * 8:(t + 1) * 8], iotar[:], gmi2[:, t:t + 1], None,
            op0=ALU.is_equal))
    vspace()
    b.emit("vector", lambda e: e.tensor_sub(gdm[:], gm1[:], gm2[:]), incs=[(sv, 1)])
    cdm = sv.n
    b.emit("scalar", lambda e: e.activation(gw1[:], gdm[:], AF.Sigmoid),
           waits=[(sv, cdm)], incs=[(sa, 1)])
    cw1 = sa.n
    b.emit("vector", lambda e: e.tensor_scalar(
        gw2[:], gw1[:], -1.0, 1.0, op0=ALU.mult, op1=ALU.add), waits=[(sa, cw1)])
    for t in range(NT):
        b.emit("vector", lambda e, t=t: e.tensor_scalar(
            gtmp[:, t * 8:(t + 1) * 8], geq1[:, t * 8:(t + 1) * 8],
            gw1[:, t:t + 1], None, op0=ALU.mult))
    for t in range(NT):
        b.emit("vector", lambda e, t=t: e.scalar_tensor_tensor(
            comb[:, t * 8:(t + 1) * 8], geq2[:, t * 8:(t + 1) * 8],
            gw2[:, t:t + 1], gtmp[:, t * 8:(t + 1) * 8],
            op0=ALU.mult, op1=ALU.add),
            incs=[(sv, 1)] if t == NT - 1 else [])
    c_comb = sv.n
    tap("comb", comb, [(sv, c_comb)])

    b.emit("vector", lambda e: e.tensor_copy(acc[:], x1T[:]),
           waits=[(sv, c_x1T)], incs=[(sv, 1)])
    c_acc0 = sv.n

    upw_loaded = [0, 0]
    upw_consumed = [c_x2_mm, c_x2_mm]
    dnw_loaded = [0, 0]
    dnw_consumed = [c_x2_mm, c_x2_mm]
    dn_slots = [dnw0, dn2]
    psU_free = [(sv, c_acc0)] * 4
    psDn_free = [c_acc0] * 4
    upslot = [0]
    cb_ready = [0]
    c_prev_dn = [0]

    def up_src(e_, m):
        if e_ < NE:
            return hwup[e_, m]
        return hsgg[m] if m < 12 else hsgu[m - 12]

    def dn_src(e_, m):
        return hwdn[e_, m] if e_ < NE else hsgd[m]

    def emit_expert(ex):
        shared = (ex == NE)
        if not shared:
            crow_done = 0
            for qc in range(4):
                for tt in range(4):
                    t = qc * 4 + tt
                    wl = [(sv, c_comb), (sv, psDn_free[0]), (sg, c_consts)] \
                        if (qc == 0 and tt == 0) else ([(sa, crow_done)] if tt == 0 else [])
                    b.emit("tensor", lambda e, t=t, tt=tt, ex=ex: e.transpose(
                        psA[0][0:1, tt * 128:(tt + 1) * 128],
                        comb[:, t * 8 + ex: t * 8 + ex + 1], ident32[:]),
                        waits=wl, incs=[(sp, 1)] if tt == 3 else [])
                cf = sp.n
                b.emit("scalar", lambda e, qc=qc: e.activation(
                    crow[0:1, qc * 512: qc * 512 + 512], psA[0][0:1, 0:512], AF.Identity),
                    waits=[(sp, cf)], incs=[(sa, 1)])
                crow_done = sa.n
            ca = sa.n
            cbc_ev = 0
            for qc in range(4):
                wl = [(sa, ca)]
                wl.append((sv, psDn_free[1]) if qc == 0 else (sv, cbc_ev))
                b.emit("tensor", lambda e, qc=qc: e.matmul(
                    psA[1][:], ones_rowf[:], crow[0:1, qc * 512: qc * 512 + 512],
                    start=True, stop=True),
                    waits=wl, incs=[(sp, 1)])
                cf = sp.n
                b.emit("vector", lambda e, qc=qc: e.tensor_copy(
                    cbcast[:, qc * 512: qc * 512 + 512], psA[1][:, 0:512]),
                    waits=[(sp, cf)], incs=[(sv, 1)])
                cbc_ev = sv.n
            cb_ready[0] = sv.n
            psDn_free[0] = sv.n
            psDn_free[1] = sv.n

        for half in range(2):
            silu_sa = {}
            first_silu = True
            for mt in range(6):
                for who in range(2):
                    m = (half * 6 + mt) + (0 if who == 0 else 12)
                    slot = upslot[0] % 2
                    upslot[0] += 1
                    b.emit("gpsimd", lambda e, ex=ex, m=m, slot=slot: e.dma_start(
                        upw[slot][:], up_src(ex, m)),
                        waits=[(sp, upw_consumed[slot])], incs=[(sdw, 16)])
                    upw_loaded[slot] = sdw.n
                    banks = (0, 1) if who == 0 else (2, 3)
                    for qc in range(4):
                        bank = banks[qc % 2]
                        wl = [(sdw, upw_loaded[slot]), psU_free[bank], (sv, c_acc0)]
                        for kt in range(8):
                            b.emit("tensor", lambda e, slot=slot, kt=kt, qc=qc, bank=bank: e.matmul(
                                psU[bank], upw[slot][:, kt * 128:(kt + 1) * 128],
                                x1T[:, kt * T + qc * 512: kt * T + qc * 512 + 512],
                                start=(kt == 0), stop=(kt == 7)),
                                waits=wl if kt == 0 else [],
                                incs=[(sp, 1)] if kt == 7 else [])
                        cf = sp.n
                        upw_consumed[slot] = cf
                        if who == 0:
                            swl = [(sp, cf)]
                            if first_silu:
                                swl.append((sp, c_prev_dn[0]))
                                first_silu = False
                            b.emit("scalar", lambda e, mt=mt, qc=qc, bank=bank: e.activation(
                                hbuf[:, mt * T + qc * 512: mt * T + qc * 512 + 512],
                                psU[bank], AF.Silu), waits=swl, incs=[(sa, 1)])
                            silu_sa[(mt, qc)] = sa.n
                            psU_free[bank] = (sa, sa.n)
                        else:
                            if shared:
                                b.emit("vector", lambda e, mt=mt, qc=qc, bank=bank: e.tensor_mul(
                                    hbuf[:, mt * T + qc * 512: mt * T + qc * 512 + 512],
                                    psU[bank],
                                    hbuf[:, mt * T + qc * 512: mt * T + qc * 512 + 512]),
                                    waits=[(sp, cf), (sa, silu_sa[(mt, qc)])],
                                    incs=[(sv, 1)])
                            else:
                                b.emit("vector", lambda e, qc=qc, bank=bank: e.tensor_mul(
                                    utmp[:], psU[bank],
                                    cbcast[:, qc * 512: qc * 512 + 512]),
                                    waits=[(sp, cf), (sv, cb_ready[0])])
                                b.emit("vector", lambda e, mt=mt, qc=qc: e.tensor_mul(
                                    hbuf[:, mt * T + qc * 512: mt * T + qc * 512 + 512],
                                    utmp[:],
                                    hbuf[:, mt * T + qc * 512: mt * T + qc * 512 + 512]),
                                    waits=[(sa, silu_sa[(mt, qc)])],
                                    incs=[(sv, 1)])
                            psU_free[bank] = (sv, sv.n)
            c_h = sv.n

            for mo in range(8):
                slot = mo % 2
                b.emit("gpsimd", lambda e, ex=ex, mo=mo, slot=slot: e.dma_start(
                    dn_slots[slot][:], dn_src(ex, mo)),
                    waits=[(sp, dnw_consumed[slot])], incs=[(sdw, 16)])
                dnw_loaded[slot] = sdw.n
                for qc in range(4):
                    bank = (mo * 4 + qc) % 4
                    wl = [(sdw, dnw_loaded[slot]), (sv, c_h), (sv, psDn_free[bank])]
                    for lt in range(6):
                        kt = half * 6 + lt
                        b.emit("tensor", lambda e, slot=slot, kt=kt, lt=lt, qc=qc, bank=bank: e.matmul(
                            psDn[bank], dn_slots[slot][:, kt * 128:(kt + 1) * 128],
                            hbuf[:, lt * T + qc * 512: lt * T + qc * 512 + 512],
                            start=(lt == 0), stop=(lt == 5)),
                            waits=wl if lt == 0 else [],
                            incs=[(sp, 1)] if lt == 5 else [])
                    cf = sp.n
                    if qc == 3:
                        dnw_consumed[slot] = cf
                    b.emit("vector", lambda e, mo=mo, qc=qc, bank=bank: e.scalar_tensor_tensor(
                        acc[:, mo * T + qc * 512: mo * T + qc * 512 + 512],
                        acc[:, mo * T + qc * 512: mo * T + qc * 512 + 512], 0.0,
                        psDn[bank], op0=ALU.bypass, op1=ALU.add),
                        waits=[(sp, cf)], incs=[(sv, 1)])
                    psDn_free[bank] = sv.n
            c_prev_dn[0] = sp.n

    for ex in range(NE + 1):
        emit_expert(ex)
        if ex == 0:
            tap("hbuf", hbuf, [(sv, sv.n)])
            tap("cbcast", cbcast, [(sv, cb_ready[0])])
    c_ffn_sv = sv.n
    tap("acc", acc, [(sv, c_ffn_sv)])

    tr_evac3 = [0, 0]
    idx = 0
    for t in range(NT):
        for cg in range(2):
            bank = idx % 2
            wl = [(sv, c_ffn_sv)] if idx < 2 else [(sv, tr_evac3[bank])]
            for c4 in range(4):
                c = cg * 4 + c4
                b.emit("tensor", lambda e, t=t, c=c, c4=c4, bank=bank: e.transpose(
                    psP[bank][:, c4 * 128:(c4 + 1) * 128],
                    acc[:, c * T + t * 128: c * T + (t + 1) * 128],
                    ident32[:]), waits=wl if c4 == 0 else [],
                    incs=[(sp, 1)] if c4 == 3 else [])
            cf = sp.n
            b.emit("vector", lambda e, t=t, cg=cg, bank=bank: e.tensor_scalar(
                z_tok[:, t * D + cg * 512: t * D + (cg + 1) * 512], psP[bank][:],
                1.0, 0.0, op0=ALU.mult, op1=ALU.add,
                accum_out=lnz[:, (t * 2 + cg):(t * 2 + cg) + 1]),
                waits=[(sp, cf)], incs=[(sv, 1)])
            tr_evac3[bank] = sv.n
            if cg == 1:
                b.emit("scalar", lambda e, t=t: e.activation(
                    scr_sq[:], z_tok[:, t * D:(t + 1) * D], AF.Square,
                    accum_out=sums2[:, t:t + 1]),
                    waits=[(sv, sv.n)], incs=[(sa, 1)])
                if t >= 1:
                    b.emit("vector", lambda e, t=t: e.reduce_sum(
                        sums1[:, t - 1:t], lnz[:, (t - 1) * 2:t * 2], axis=AX.X))
            idx += 1
    c_z = sv.n
    c_z_sq = sa.n
    vspace()
    b.emit("vector", lambda e: e.reduce_sum(
        sums1[:, NT - 1:NT], lnz[:, (NT - 1) * 2:NT * 2], axis=AX.X))
    b.emit("vector", lambda e: e.engine_nop(), incs=[(sv, 1)])
    c_z_sums = sv.n
    c_ln3_fin = emit_ln_finish([(sv, c_z_sums), (sa, c_z_sq)])
    for t in range(NT):
        b.emit("scalar", lambda e, t=t: e.activation(
            z_tok[:, t * D:(t + 1) * D], z_tok[:, t * D:(t + 1) * D], AF.Identity,
            scale=rstd[:, t:t + 1], bias=nmr[:, t:t + 1]),
            waits=[(sv, c_ln3_fin)] if t == 0 else [], incs=[(sa, 1)])
        ca = sa.n
        b.emit("sync", lambda e, t=t: e.dma_start(
            hout[:, t * D:(t + 1) * D], z_tok[:, t * D:(t + 1) * D]),
            waits=[(sa, ca)], incs=[(so, 16)])
    c_out_ready = sa.n
    tap("z_tok", z_tok, [(sa, c_out_ready)])
    total_so = so.n
    b.emit("gpsimd", lambda e: e.engine_nop(), waits=[(so, total_so)])

    b.finalize(blk)
    est.close()
    return nc


def kernel(L0_attn_X, L0_attn_W_in, L0_attn_b_in, L0_attn_W_out, L0_attn_b_out,
           L0_ffn_gate_w, L0_ffn_w_up, L0_ffn_w_down, L0_ffn_sg_gate,
           L0_ffn_sg_up, L0_ffn_sg_down):
    X = np.asarray(L0_attn_X, dtype=np.float32)
    shared_img = {}
    shared_img["w_in_img"] = _lhsT_slices(np.asarray(L0_attn_W_in, np.float32), 32, 8)
    shared_img["b_in_img"] = np.ascontiguousarray(
        np.asarray(L0_attn_b_in, np.float32).reshape(32, 128).T)
    shared_img["w_out_img"] = _rhs_img(np.ascontiguousarray(
        np.asarray(L0_attn_W_out, np.float32).T))
    shared_img["b_out_row"] = np.ascontiguousarray(
        np.asarray(L0_attn_b_out, np.float32).reshape(1, D))
    shared_img["gate_img"] = _rhs_img(np.ascontiguousarray(
        np.asarray(L0_ffn_gate_w, np.float32).T))
    wup = np.asarray(L0_ffn_w_up, np.float32)
    wdn = np.asarray(L0_ffn_w_down, np.float32)
    shared_img["wup_img"] = np.stack([_lhsT_slices(wup[e], 24, 8) for e in range(NE)])
    shared_img["wdn_img"] = np.stack([_lhsT_slices(wdn[e], 8, 12) for e in range(NE)])
    shared_img["sgg_img"] = _lhsT_slices(np.asarray(L0_ffn_sg_gate, np.float32), 12, 8)
    shared_img["sgu_img"] = _lhsT_slices(np.asarray(L0_ffn_sg_up, np.float32), 12, 8)
    shared_img["sgd_img"] = _lhsT_slices(np.asarray(L0_ffn_sg_down, np.float32), 8, 12)

    nc = build_program()
    in_maps = []
    for c in range(_NUM_CORES):
        d = dict(shared_img)
        d["x_img"] = _img_tok(np.ascontiguousarray(X[c]))
        in_maps.append(d)
    res = run_bass_kernel_spmd(nc, in_maps, core_ids=list(range(_NUM_CORES)))
    out = np.stack([_unimg_tok(res.results[c]["out_img"]) for c in range(_NUM_CORES)])
    if _TAPS:
        kernel._tap_results = [
            {name: res.results[c]["tap_" + name] for (name, _s) in _TAPS}
            for c in range(_NUM_CORES)]
    return out.astype(np.float32)
